# version 12
# baseline (speedup 1.0000x reference)
import numpy as np
import ml_dtypes
from contextlib import ExitStack

import concourse.bass as bass
import concourse.mybir as mybir
from concourse.bass_utils import run_bass_kernel_spmd

F32 = mybir.dt.float32
BF16 = mybir.dt.bfloat16
I32 = mybir.dt.int32
AF = mybir.ActivationFunctionType
ALU = mybir.AluOpType
AX = mybir.AxisListType
BF = ml_dtypes.bfloat16

T = 2048
D = 2048
INTER = 4096
NH = 64
HD = 64
NS = 64
CONVD = 4224
HF = 8192
NCH = 16
NCORES = 8
RG = [[0, 1], [2, 3], [4, 5], [6, 7]]

ENGS = ("t", "v", "s", "g", "q")


class FW:

    def __init__(self, nc):
        self.nc = nc
        self.ops = {E: [] for E in ENGS}
        self.writer = {}
        self.readers = {}
        self.hw = {E: {} for E in ENGS}
        self.dcnt = {}
        self.dbatch = {}

    def dsem(self, name, batch=False):
        if name not in self.dcnt:
            self.dcnt[name] = 0
            self.dbatch[name] = batch
        return name

    def dma_begin(self, name):
        self.dsem(name)
        self.pending = getattr(self, "pending", {})
        self.pending[name] = []

    def dma_end(self, name, extra_keys=()):
        tok = ("d", name, self.dcnt[name])
        for wks, rks in self.pending.pop(name):
            for k in wks:
                self.writer[k] = tok
                self.readers[k] = []
            for k in rks:
                self.readers.setdefault(k, []).append(tok)
        for k in extra_keys:
            self.writer[k] = tok
            self.readers[k] = []
        return tok

    def op(self, E, fn, reads=(), writes=(), rw=(), dma=None, group=False):
        waits = []

        def need(tok):
            if tok is None:
                return
            if tok[0] == "e":
                _, E2, seq = tok
                if seq <= self.hw[E].get(("e", E2), -1):
                    return
                self.hw[E][("e", E2)] = seq
                self.ops[E2][seq]["mark"] = True
                waits.append(tok)
            elif tok[0] == "d":
                _, nm, val = tok
                if val <= self.hw[E].get(("d", nm), 0):
                    return
                self.hw[E][("d", nm)] = val
                waits.append(tok)
            else:
                if self.hw[E].get(("dF", tok[1])):
                    return
                self.hw[E][("dF", tok[1])] = True
                waits.append(tok)

        reads = tuple(reads) + tuple(rw)
        writes = tuple(writes) + tuple(rw)
        for k in reads:
            need(self.writer.get(k))
        for k in writes:
            need(self.writer.get(k))
            for t_ in self.readers.get(k, ()):
                need(t_)
        seq = len(self.ops[E])
        self.ops[E].append({"fn": fn, "waits": waits, "mark": False, "dma": dma})
        if dma is not None:
            self.dcnt[dma] += 16
            if group:
                self.pending[dma].append((writes, reads))
                return None
            tok = ("dF", dma) if self.dbatch[dma] else ("d", dma, self.dcnt[dma])
        else:
            tok = ("e", E, seq)
        for k in reads:
            self.readers.setdefault(k, []).append(tok)
        for k in writes:
            self.writer[k] = tok
            self.readers[k] = []
        return tok

    def barrier(self):
        def qfn(eng):
            return eng.nop()
        toks = []
        for E in ENGS:
            if E == "q" or not self.ops[E]:
                continue
            seq = len(self.ops[E]) - 1
            while seq >= 0 and self.ops[E][seq]["dma"] is not None:
                seq -= 1
            if seq >= 0:
                toks.append(("e", E, seq))
        dtoks = []
        for nm, c in self.dcnt.items():
            if c > self.hw["q"].get(("d", nm), 0) or (self.dbatch[nm] and c):
                dtoks.append(("d", nm, c) if not self.dbatch[nm] else ("dF2", nm, c))
        waits = []
        for tok in toks:
            _, E2, seq = tok
            if seq <= self.hw["q"].get(("e", E2), -1):
                continue
            self.hw["q"][("e", E2)] = seq
            self.ops[E2][seq]["mark"] = True
            waits.append(tok)
        for tok in dtoks:
            if tok[0] == "dF2":
                _, nm, c = tok
                waits.append(("d", nm, c))
                self.hw["q"][("d", nm)] = c
            else:
                _, nm, c = tok
                if c <= self.hw["q"].get(("d", nm), 0):
                    continue
                self.hw["q"][("d", nm)] = c
                waits.append(tok)
        seq = len(self.ops["q"])
        self.ops["q"].append({"fn": qfn, "waits": waits, "mark": True, "dma": None})
        qtok = ("e", "q", seq)
        for E in ENGS:
            if E == "q":
                continue
            seqE = len(self.ops[E])
            self.ops[E].append({"fn": lambda eng: eng.nop(),
                               "waits": [qtok], "mark": False, "dma": None})
            self.hw[E][("e", "q")] = seq
        self.writer = {}
        self.readers = {}
        return qtok

    def emit(self, block, ctx):
        esem = {E: ctx.enter_context(self.nc.semaphore("es_" + E)) for E in ENGS}
        dsem = {nm: ctx.enter_context(self.nc.semaphore("ds_" + nm))
                for nm in self.dcnt}
        prefix = {}
        for E in ENGS:
            c = 0
            pl = []
            for o in self.ops[E]:
                if o["mark"]:
                    c += 1
                pl.append(c)
            prefix[E] = pl

        def mk(E):
            def run(eng):
                for o in self.ops[E]:
                    for tok in o["waits"]:
                        if tok[0] == "e":
                            eng.wait_ge(esem[tok[1]], prefix[tok[1]][tok[2]])
                        elif tok[0] == "d":
                            eng.wait_ge(dsem[tok[1]], tok[2])
                        else:
                            eng.wait_ge(dsem[tok[1]], self.dcnt[tok[1]])
                    inst = o["fn"](eng)
                    if o["dma"] is not None:
                        inst.then_inc(dsem[o["dma"]], 16)
                    elif o["mark"]:
                        inst.then_inc(esem[E], 1)
            return run

        block.tensor(mk("t"))
        block.vector(mk("v"))
        block.scalar(mk("s"))
        block.gpsimd(mk("g"))
        block.sync(mk("q"))

    def finalize(self, extra_sems=()):
        waits = []
        for nm, c in self.dcnt.items():
            if c > self.hw["q"].get(("d", nm), 0):
                waits.append(("d", nm, c))
                self.hw["q"][("d", nm)] = c
        self.ops["q"].append({"fn": lambda eng: eng.nop(), "waits": waits,
                              "mark": False, "dma": None})


class Pool:

    def __init__(self, base_f32):
        self.base = base_f32
        self.n = base_f32.shape[1]
        self.off = 0
        self.stack = []

    def push(self):
        self.stack.append(self.off)

    def pop(self):
        self.off = self.stack.pop()

    def f32(self, words, parts=128):
        o = self.off
        self.off = o + ((words + 15) // 16) * 16
        assert self.off <= self.n, f"pool overflow {self.off} > {self.n}"
        return self.base[0:parts, o:o + words]

    def bf16(self, cols, parts=128):
        w = (cols + 1) // 2
        return self.f32(w, parts).bitcast(BF16)


def build_program(phases=99, dumps=()):
    nc = bass.Bass("TRN2", target_bir_lowering=False, debug=False)
    dumps = set(dumps)

    def dram(name, shape, dtype):
        kind = "ExternalOutput" if name in dumps else "Internal"
        return nc.dram_tensor(name, shape, dtype, kind=kind)

    EI = "ExternalInput"
    x_in = nc.dram_tensor("x", [T, D], F32, kind=EI).ap()
    wg_img = nc.dram_tensor("wg_img", [32, 128, 2048], F32, kind=EI).ap()
    wh_img = nc.dram_tensor("wh_img", [33, 128, 2048], F32, kind=EI).ap()
    wd_img = nc.dram_tensor("wd_img", [128, 1024], F32, kind=EI).ap()
    wo_img = nc.dram_tensor("wo_img", [16, 128, 4096], F32, kind=EI).ap()
    convw_in = nc.dram_tensor("convw_r", [128, 132], F32, kind=EI).ap()
    convb_in = nc.dram_tensor("convb_r", [128, 33], F32, kind=EI).ap()
    dtb_in = nc.dram_tensor("dtb", [64, 1], F32, kind=EI).ap()
    alog_in = nc.dram_tensor("alog", [64, 1], F32, kind=EI).ap()
    dcol_in = nc.dram_tensor("dcol", [64, 1], F32, kind=EI).ap()
    nw_in = nc.dram_tensor("nw_r", [128, 32], F32, kind=EI).ap()
    qup_in = nc.dram_tensor("q_up", [HF, D], I32, kind=EI).ap()
    scup_in = nc.dram_tensor("scup_r", [128, 64 * 32], F32, kind=EI).ap()
    bup_in = nc.dram_tensor("bup_r", [128, 64], F32, kind=EI).ap()
    qdn_in = nc.dram_tensor("q_dn", [D, HF], I32, kind=EI).ap()
    scdn_in = nc.dram_tensor("scdn_r", [128, 16 * 128], F32, kind=EI).ap()
    bdn_in = nc.dram_tensor("bdn_r", [128, 16], F32, kind=EI).ap()
    aup_in = nc.dram_tensor("aup_r", [128, 256], F32, kind=EI).ap()
    bupl_in = nc.dram_tensor("b_upl", [16, HF], F32, kind=EI).ap()
    adn_in = nc.dram_tensor("adn_r", [128, 1024], F32, kind=EI).ap()
    bdnl_in = nc.dram_tensor("b_dnl", [16, D], F32, kind=EI).ap()
    isodd_in = nc.dram_tensor("is_odd", [128, 1], F32, kind=EI).ap()

    identb_c = nc.inline_tensor(np.eye(128, dtype=BF), "identbc").ap()
    identf_c = nc.inline_tensor(np.eye(128, dtype=np.float32), "identfc").ap()
    e_all_np = np.zeros((64, 64 * 128), np.float32)
    for h in range(64):
        e_all_np[h, h * 128:(h + 1) * 128] = 1.0
    eall_c = nc.inline_tensor(e_all_np, "eallc").ap()
    mask_np = np.triu(np.ones((128, 128), np.float32))
    mask_c = nc.inline_tensor(mask_np, "maskc").ap()
    ones1f_c = nc.inline_tensor(np.ones((1, 128), np.float32), "ones1fc").ap()

    xfm_hbm = dram("xfm_hbm", [CONVD, T], BF16).ap()
    s_hbm = dram("s_hbm", [NCH, 64, NH * HD], F32).ap()
    ex1_in = nc.dram_tensor("ex1_in", [128, 99], F32, kind="Internal").ap()
    ex1_out = nc.dram_tensor("ex1_out", [256, 99], F32, kind="Internal").ap()
    ex2_in = nc.dram_tensor("ex2_in", [64, 4096], F32, kind="Internal").ap()
    ex2_out = nc.dram_tensor("ex2_out", [128, 4096], F32, kind="Internal").ap()
    yn_hbm = dram("yn_hbm", [T, INTER], BF16).ap()
    l1x_hbm = dram("l1x_hbm", [T, D], BF16).ap()
    yg_hbm = dram("yg_hbm", [INTER, T], BF16).ap()
    attn_hbm = dram("attn_hbm", [D, T], F32).ap()
    x4_hbm = dram("x4_hbm", [T, D], F32).ap()
    l3x_hbm = dram("l3x_hbm", [T, D], BF16).ap()
    wupn_hbm = dram("wupn_hbm", [HF, D], BF16).ap()
    wupi_hbm = dram("wupi_hbm", [64, 128, 2048], BF16).ap()
    wdnn_hbm = dram("wdnn_hbm", [D, HF], BF16).ap()
    wdni_hbm = dram("wdni_hbm", [16, 128, 64, 128], BF16).ap()
    y2_hbm = dram("y2_hbm", [D, T], F32).ap()
    out_hbm = nc.dram_tensor("out", [T, D], F32, kind="ExternalOutput").ap()

    fw = FW(nc)

    with ExitStack() as ctx:
        def sb(name, shape, dtype):
            return ctx.enter_context(nc.sbuf_tensor(name, shape, dtype))

        convw = sb("convw", [128, 132], F32)
        convb = sb("convb", [128, 33], F32)
        dtb = sb("dtbs", [64, 1], F32)
        acol = sb("acol", [64, 1], F32)
        dbc = sb("dbc", [128, 64], F32)
        nw = sb("nws", [128, 32], F32)
        isodd = sb("isodds", [128, 1], F32)
        identb = sb("identbs", [128, 128], BF16)
        identf = sb("identfs", [128, 128], F32)
        maskc = sb("maskcs", [128, 128], F32)
        ones1f = sb("ones1fs", [1, 128], F32)
        zeros64 = sb("zeros64", [64, 128], F32)
        bup = sb("bups", [128, 64], F32)
        bdn = sb("bdns", [128, 16], F32)
        aupb = sb("aupb", [128, 256], BF16)
        adnb = sb("adnb", [128, 1024], BF16)
        stat = sb("stat", [128, 128], F32)
        small = sb("small", [128, 16], F32)
        rowb = sb("rowb", [1, 64], F32)
        fixb = sb("fixb", [128, 12], F32)

        POOLW = 50176
        poolt = sb("poolt", [128, POOLW], F32)
        pool = Pool(poolt)

        pool.push()
        ldt = pool.f32(T, parts=64)
        cuma = pool.f32(T, parts=64)
        b_fm = pool.bf16(T, parts=64)
        c_fm = pool.bf16(T, parts=64)
        fixtok = pool.bf16(CONVD, parts=3)
        sendtail = pool.f32(99)
        firstc = pool.f32(99)
        fixfb = pool.bf16(99)

        PA = ctx.enter_context(nc.psum_tensor("PA", [128, 2048], F32))
        PB = ctx.enter_context(nc.psum_tensor("PB", [128, 2048], F32))

        def ps(b, lo=0, hi=512, parts=128):
            tns = PA if b < 4 else PB
            b = b % 4
            return tns[0:parts, b * 512 + lo:b * 512 + hi]

        def pa(b, lo=0, hi=512, parts=128):
            return ps(b, lo, hi, parts)

        def pb(b, lo=0, hi=512, parts=128):
            return ps(b + 4, lo, hi, parts)

        block = ctx.enter_context(nc.Block())

        cst = fw.dsem("cst", batch=True)
        cstg = fw.dsem("cstg", batch=True)
        for dst, src in [(convw, convw_in), (convb, convb_in), (dtb, dtb_in),
                         (nw, nw_in), (isodd, isodd_in), (identb, identb_c),
                         (identf, identf_c), (maskc, mask_c), (ones1f, ones1f_c),
                         (bup, bup_in), (bdn, bdn_in)]:
            fw.op("q", (lambda d, s_: lambda e: e.dma_start(out=d[:], in_=s_))(dst, src),
                  writes=[d_name(dst)], dma=cst)
        for dst, src in [(aupb, aup_in), (adnb, adn_in)]:
            fw.op("g", (lambda d, s_: lambda e: e.dma_start(out=d[:], in_=s_))(dst, src),
                  writes=[d_name(dst)], dma=cstg)
        fw.op("g", lambda e: e.dma_start(out=small[0:64, 0:1], in_=alog_in),
              writes=["alog"], dma=cstg)
        fw.op("g", lambda e: e.dma_start(out=small[0:64, 1:2], in_=dcol_in),
              writes=["dcolv"], dma=cstg)
        fw.op("v", lambda e: e.memset(zeros64[:], 0.0), writes=["zeros64"])
        fw.op("s", lambda e: e.activation(out=acol[:], in_=small[0:64, 0:1],
                                          func=AF.Exp),
              reads=["alog"], writes=["acol0"])
        fw.op("v", lambda e: e.tensor_scalar_mul(acol[:], acol[:], -1.0),
              reads=["acol0"], writes=["acol"])
        fw.op("t", lambda e: e.transpose(pa(7, 0, 64, parts=1), small[0:64, 1:2],
                                         identf[0:64, 0:64]),
              reads=["dcolv", d_name(identf)], writes=[("PS", 7)])
        fw.op("v", lambda e: e.tensor_copy(out=rowb[:], in_=pa(7, 0, 64, parts=1)),
              reads=[("PS", 7)], writes=["rowb"])
        fw.op("t", lambda e: e.matmul(pa(6, 0, 64), ones1f[:], rowb[:],
                                      start=True, stop=True),
              reads=["rowb", d_name(ones1f)], writes=[("PS", 6)])
        fw.op("v", lambda e: e.tensor_copy(out=dbc[:], in_=pa(6, 0, 64)),
              reads=[("PS", 6)], writes=["dbc"])

        pool.push()
        ln1fm = pool.bf16(16 * 2048)
        xtile = [pool.f32(2048) for _ in range(2)]
        lntok = [pool.bf16(2048) for _ in range(2)]
        hbc = pool.f32(T + 3)
        cacc = pool.f32(T)
        xbf = [pool.bf16(T) for _ in range(2)]
        wbuf = [pool.bf16(2048) for _ in range(3)]
        dtraw = pool.f32(T, parts=64)
        dtmp = cacc[0:64, :]
        qbuf = [pool.bf16(2048) for _ in range(2)]
        wnat = [pool.bf16(2048) for _ in range(2)]
        scup = pool.f32(64 * 32)

        fw.op("q", lambda e: e.dma_start(out=scup, in_=scup_in),
              writes=["scup"], dma=cst)
        fw.op("v", lambda e: e.memset(hbc[:, 0:3], 0.0), writes=["hbc03"])

        def ln1fm_k(k):
            return ln1fm[:, k * 2048:(k + 1) * 2048]

        def emit_lnstats(src_key, src_ap, eps, c0):
            skeys = src_key if isinstance(src_key, list) else [src_key]
            for gidx in range(4):
                fw.op("v", (lambda g_: lambda e: e.bn_stats(
                    stat[:, c0 + 4 + g_ * 6:c0 + 10 + g_ * 6],
                    src_ap[:, g_ * 512:(g_ + 1) * 512]))(gidx),
                    reads=skeys, writes=[("st", c0, gidx)])
            fw.op("v", lambda e: e.bn_aggr(stat[:, c0 + 2:c0 + 4],
                                           stat[:, c0 + 4:c0 + 28]),
                  reads=[("st", c0, g) for g in range(4)], writes=[("mv", c0)])
            fw.op("v", (lambda: lambda e: e.tensor_scalar(
                out=stat[:, c0 + 28:c0 + 29], in0=stat[:, c0 + 3:c0 + 4],
                scalar1=1.0, scalar2=eps, op0=ALU.mult, op1=ALU.add))(),
                reads=[("mv", c0)], writes=[("ve", c0)])
            fw.op("v", lambda e: e.reciprocal(stat[:, c0 + 29:c0 + 30],
                                              stat[:, c0 + 28:c0 + 29]),
                  reads=[("ve", c0)], writes=[("rc", c0)])
            fw.op("s", lambda e: e.activation(out=stat[:, c0:c0 + 1],
                                              in_=stat[:, c0 + 29:c0 + 30],
                                              func=AF.Sqrt),
                  reads=[("rc", c0)], writes=[("rs", c0)])
            fw.op("v", lambda e: e.tensor_scalar(
                out=stat[:, c0 + 1:c0 + 2], in0=stat[:, c0 + 2:c0 + 3],
                scalar1=stat[:, c0:c0 + 1], scalar2=-1.0,
                op0=ALU.mult, op1=ALU.mult),
                reads=[("rs", c0), ("mv", c0)], writes=[("nb", c0)])

        if phases >= 1:
            for tt in range(16):
                xt = xtile[tt % 2]
                xk = ("xt", tt % 2)
                fw.op("q", (lambda tt_, xt_: lambda e: e.dma_start(
                    out=xt_, in_=x_in[tt_ * 128:(tt_ + 1) * 128, :]))(tt, xt),
                    writes=[xk], dma=fw.dsem(f"xt{tt % 2}"))
                emit_lnstats(xk, xt, 1e-5, 0)
                lt = lntok[tt % 2]
                lk = ("lt", tt % 2)
                fw.op("s", (lambda xt_, lt_: lambda e: e.activation(
                    out=lt_, in_=xt_, func=AF.Identity,
                    bias=stat[:, 1:2], scale=stat[:, 0:1]))(xt, lt),
                    reads=[xk, ("rs", 0), ("nb", 0)], writes=[lk])
                fw.op("q", (lambda tt_, lt_: lambda e: e.dma_start(
                    out=l1x_hbm[tt_ * 128:(tt_ + 1) * 128, :], in_=lt_))(tt, lt),
                    reads=[lk], writes=[("l1x", tt)], dma=fw.dsem(f"l1xS{tt % 2}"))
                for k in range(16):
                    bnk = k % 2
                    slot = (k // 2) % 4
                    fw.op("t", (lambda lt_, k_, b_, s_: lambda e: e.transpose(
                        ps(b_).bitcast(BF16)[:, s_ * 128:(s_ + 1) * 128],
                        lt_[:, k_ * 128:(k_ + 1) * 128], identb[:]))(lt, k, bnk, slot),
                        reads=[lk, d_name(identb)], writes=[("PS", bnk)])
                    fw.op("v", (lambda tt_, k_, b_, s_: lambda e: e.tensor_copy(
                        out=ln1fm_k(k_)[:, tt_ * 128:(tt_ + 1) * 128],
                        in_=ps(b_).bitcast(BF16)[:, s_ * 128:(s_ + 1) * 128]))(tt, k, bnk, slot),
                        reads=[("PS", bnk)], writes=[("ln1fm", k, tt)])

        def emit_qup_chunk(o):
            qb = qbuf[o % 2]
            wn = wnat[o % 2]
            fw.op("g", (lambda o_, qb_: lambda e: e.dma_start(
                out=qb_, in_=qup_in[o_ * 128:(o_ + 1) * 128, :]))(o, qb),
                writes=[("qb", o % 2)], dma=fw.dsem(f"qup{o % 2}"))
            fw.op("v", (lambda o_, qb_, wn_: lambda e: e.scalar_tensor_tensor(
                out=wn_.rearrange("p (b k) -> p b k", b=32),
                in0=qb_.rearrange("p (b k) -> p b k", b=32),
                scalar=-7.5,
                in1=scup[:, o_ * 32:(o_ + 1) * 32].unsqueeze(2).broadcast_to([128, 32, 64]),
                op0=ALU.add, op1=ALU.mult))(o, qb, wn),
                reads=[("qb", o % 2), "scup"], writes=[("wn", o % 2)])
            fw.op("q", (lambda o_, wn_: lambda e: e.dma_start(
                out=wupn_hbm[o_ * 128:(o_ + 1) * 128, :], in_=wn_))(o, wn),
                reads=[("wn", o % 2)], writes=[("wupn", o)],
                dma=fw.dsem(f"wupnS{o % 2}"))

        def a1_mms(o, img_src, psum_sel, ofree=128):
            wb = wbuf[o % 3]
            wbk = ("wb", o % 3)
            fw.op("g", (lambda s_, wb_, n_: lambda e: e.dma_start(
                out=wb_[:, 0:16 * n_], in_=s_))(img_src, wb, ofree),
                writes=[wbk], dma=fw.dsem(f"wb{o % 3}"))
            Pt = psum_sel
            for tt in range(4):
                for k in range(16):
                    first, last = k == 0, k == 15
                    rd = []
                    if first:
                        rd = [wbk] + [("ln1fm", kk, t4) for kk in range(16)
                                      for t4 in range(tt * 4, tt * 4 + 4)]
                    elif last:
                        rd = [wbk]
                    fw.op("t", (lambda wb_, k_, tt_, f_, l_, p_, n_: lambda e: e.matmul(
                        (pa if p_ == 0 else pb)(tt_, parts=n_),
                        wb_[:, k_ * n_:(k_ + 1) * n_],
                        ln1fm_k(k_)[:, tt_ * 512:(tt_ + 1) * 512],
                        start=f_, stop=l_))(wb, k, tt, first, last, Pt, ofree),
                        reads=rd,
                        writes=([("PS", Pt * 4 + tt)] if (first or last) else []))

        if phases >= 2:
            qo = 0
            for o in range(34):
                Pt = o % 2
                Pn = "PA" if Pt == 0 else "PB"
                if o < 33:
                    a1_mms(o, wh_img[o], Pt)
                else:
                    a1_mms(o, wd_img, Pt, ofree=64)
                for _ in range(2):
                    if qo < 64:
                        emit_qup_chunk(qo)
                        qo += 1
                if o < 33:
                    fw.op("v", (lambda p_: lambda e: e.tensor_copy(
                        out=hbc[:, 3:3 + T], in_=(PA if p_ == 0 else PB)[:, 0:T]))(Pt),
                        reads=[("PS", Pt * 4 + tt) for tt in range(4)],
                        writes=["hbcw"])
                    fw.op("v", (lambda o_: lambda e: e.tensor_copy(
                        out=sendtail[:, o_ * 3:(o_ + 1) * 3], in_=hbc[:, 2048:2051]))(o),
                        reads=["hbcw"], writes=[("stl", o)])
                    fw.op("v", (lambda o_: lambda e: e.tensor_copy(
                        out=firstc[:, o_ * 3:(o_ + 1) * 3], in_=hbc[:, 3:6]))(o),
                        reads=["hbcw"], writes=[("fst", o)])
                    fw.op("s", (lambda o_: lambda e: e.activation(
                        out=cacc, in_=hbc[:, 0:T], func=AF.Identity,
                        scale=convw[:, o_ * 4:o_ * 4 + 1],
                        bias=convb[:, o_:o_ + 1]))(o),
                        reads=["hbcw", "hbc03", d_name(convw), d_name(convb)],
                        writes=["cacc"])
                    for kk in range(1, 4):
                        fw.op("v", (lambda o_, kk_: lambda e: e.scalar_tensor_tensor(
                            out=cacc, in0=hbc[:, kk_:T + kk_],
                            scalar=convw[:, o_ * 4 + kk_:o_ * 4 + kk_ + 1],
                            in1=cacc, op0=ALU.mult, op1=ALU.add))(o, kk),
                            reads=["hbcw"], rw=["cacc"])
                    xb = xbf[o % 2]
                    fw.op("s", (lambda xb_: lambda e: e.activation(
                        out=xb_, in_=cacc, func=AF.Silu))(xb),
                        reads=["cacc"], writes=[("xbf", o % 2)])
                    fw.op("q", (lambda o_, xb_: lambda e: e.dma_start(
                        out=xfm_hbm[o_ * 128:(o_ + 1) * 128, :], in_=xb_))(o, xb),
                        reads=[("xbf", o % 2)], writes=[("xfm", o)],
                        dma=fw.dsem(f"xfmS{o % 2}"))
                else:
                    fw.op("v", (lambda p_: lambda e: e.tensor_copy(
                        out=dtraw, in_=(PA if p_ == 0 else PB)[0:64, 0:T]))(Pt),
                        reads=[("PS", Pt * 4 + tt) for tt in range(4)], writes=["dtraw"])
            fw.op("q", lambda e: e.dma_start(out=b_fm[:], in_=xfm_hbm[4096:4160, :]),
                  reads=[("xfm", 32)], writes=["bfm"], dma=fw.dsem("bS"))
            fw.op("q", lambda e: e.dma_start(out=c_fm[:], in_=xfm_hbm[4160:4224, :]),
                  reads=[("xfm", 32)], writes=["cfm"], dma=fw.dsem("cS"))
            dtmp2 = hbc[0:64, 0:2048]
            fw.op("v", lambda e: e.tensor_scalar(
                out=dtraw, in0=dtraw, scalar1=dtb[:, 0:1], scalar2=None,
                op0=ALU.add),
                reads=[d_name(dtb)], rw=["dtraw"])
            fw.op("v", lambda e: e.tensor_scalar(
                out=dtmp, in0=dtraw, scalar1=30.0, scalar2=None, op0=ALU.min),
                reads=["dtraw"], writes=["dtc", "cacc"])
            fw.op("s", lambda e: e.activation(out=dtmp, in_=dtmp, func=AF.Exp),
                  rw=["dtc", "cacc"], writes=["dte"])
            fw.op("s", lambda e: e.activation(out=dtmp, in_=dtmp, func=AF.Ln,
                                              bias=1.0),
                  rw=["dte", "cacc"], writes=["dtp0"])
            fw.op("v", lambda e: e.tensor_scalar(
                out=dtmp2, in0=dtraw, scalar1=-30.0, scalar2=0.0,
                op0=ALU.add, op1=ALU.max),
                reads=["dtraw"], writes=["dtr30", "hbcw", "hbc03"])
            fw.op("v", lambda e: e.tensor_add(dtmp, dtmp, dtmp2),
                  reads=["dtr30", "dtp0"], rw=["dte", "cacc"], writes=["dtp"])
            fw.op("v", lambda e: e.tensor_scalar(
                out=dtraw, in0=dtmp, scalar1=acol[:, 0:1], scalar2=None,
                op0=ALU.mult),
                reads=["dtp", "acol"], writes=["a_fm"])
            fw.op("v", lambda e: e.tensor_scalar(
                out=dtmp, in0=dtmp, scalar1=1e-30, scalar2=None, op0=ALU.max),
                rw=["dtp", "cacc"], writes=["dtpc"])
            fw.op("s", lambda e: e.activation(out=ldt[:], in_=dtmp, func=AF.Ln),
                  reads=["dtpc"], writes=["ldt"])
            for q in range(NCH):
                fw.op("v", (lambda q_: lambda e: e.tensor_tensor_scan(
                    out=cuma[:, q_ * 128:(q_ + 1) * 128],
                    data0=dtraw[:, q_ * 128:(q_ + 1) * 128],
                    data1=zeros64[:], initial=0.0,
                    op0=ALU.add, op1=ALU.add))(q),
                    reads=["a_fm", "zeros64"], writes=[("cuma", q)])

        if phases >= 3:
            fw.op("q", lambda e: e.dma_start(out=ex1_in[0:128, 0:99], in_=sendtail[:]),
                  reads=[("stl", o) for o in range(33)], writes=["ex1i"],
                  dma=fw.dsem("ex1S"))
            fw.op("g", lambda e: e.collective_compute(
                "AllGather", ALU.bypass, replica_groups=RG,
                ins=[ex1_in], outs=[ex1_out]),
                reads=["ex1i"], writes=["ex1o"])
            fw.op("q", lambda e: e.dma_start(out=sendtail[:], in_=ex1_out[0:128, 0:99]),
                  reads=["ex1o"],
                  writes=["rtail"] + [("stl", o) for o in range(33)], dma="ex1S")
            fw.op("v", lambda e: e.tensor_scalar(
                out=sendtail[:], in0=sendtail[:], scalar1=isodd[:, 0:1],
                scalar2=None, op0=ALU.mult),
                reads=[d_name(isodd)], rw=["rtail"])
            for o in range(33):
                fw.op("v", (lambda o_: lambda e: e.tensor_copy(
                    out=fixb[:, 0:3], in_=sendtail[:, o_ * 3:(o_ + 1) * 3]))(o),
                    reads=["rtail"], writes=["fixb1"])
                fw.op("v", (lambda o_: lambda e: e.tensor_copy(
                    out=fixb[:, 3:6], in_=firstc[:, o_ * 3:(o_ + 1) * 3]))(o),
                    reads=[("fst", o)], writes=["fixb2"])
                fw.op("s", (lambda o_: lambda e: e.activation(
                    out=fixb[:, 6:9], in_=fixb[:, 0:3],
                    func=AF.Identity, scale=convw[:, o_ * 4:o_ * 4 + 1],
                    bias=convb[:, o_:o_ + 1]))(o),
                    reads=["fixb1"], writes=["facc"])
                for kk in range(1, 4):
                    fw.op("v", (lambda o_, kk_: lambda e: e.scalar_tensor_tensor(
                        out=fixb[:, 6:9],
                        in0=fixb[:, kk_:kk_ + 3],
                        scalar=convw[:, o_ * 4 + kk_:o_ * 4 + kk_ + 1],
                        in1=fixb[:, 6:9],
                        op0=ALU.mult, op1=ALU.add))(o, kk),
                        reads=["fixb1", "fixb2"], rw=["facc"])
                fw.op("s", (lambda o_: lambda e: e.activation(
                    out=fixfb[:, o_ * 3:(o_ + 1) * 3],
                    in_=fixb[:, 6:9], func=AF.Silu))(o),
                    reads=["facc"], writes=[("fixf", o)])
                fw.op("t", (lambda o_: lambda e: e.transpose(
                    pa(5).bitcast(BF16)[0:3, 0:128],
                    fixfb[:, o_ * 3:(o_ + 1) * 3], identb[:]))(o),
                    reads=[("fixf", o), d_name(identb)], writes=[("PS", 5)])
                fw.op("v", (lambda o_: lambda e: e.tensor_copy(
                    out=fixtok[0:3, o_ * 128:(o_ + 1) * 128],
                    in_=pa(5).bitcast(BF16)[0:3, 0:128]))(o),
                    reads=[("PS", 5)], writes=[("ftk", o)])
            fw.op("v", lambda e: e.tensor_copy(out=b_fm[:, 0:3],
                                               in_=fixfb[0:64, 96:99]),
                  reads=[("fixf", 32)], rw=["bfm"])
            fw.op("t", lambda e: e.transpose(
                pa(5).bitcast(BF16)[0:64, 0:3],
                fixtok[0:3, 4160:4224], identb[0:3, 0:3]),
                reads=[("ftk", 32), d_name(identb)], writes=[("PS", 5)])
            fw.op("v", lambda e: e.tensor_copy(out=c_fm[:, 0:3],
                                               in_=pa(5).bitcast(BF16)[0:64, 0:3]),
                  reads=[("PS", 5)], rw=["cfm"])

        dsd = fw.dsem("dmpS", batch=True)

        def dump(nm, src_ap, rkeys):
            if nm not in dumps:
                return
            t_ = nc.dram_tensor("dmp_" + nm, list(src_ap.shape),
                                src_ap.dtype, kind="ExternalOutput").ap()
            fw.op("q", (lambda a_, t2: lambda e: e.dma_start(out=t2, in_=a_))(
                src_ap, t_), reads=rkeys, dma=dsd)

        if phases >= 2:
            dump("ln1fm", ln1fm[:], [("ln1fm", k, tt) for k in range(16)
                                     for tt in range(16)])
            dump("cuma", cuma[:], [("cuma", q) for q in range(NCH)])
            dump("ldt", ldt[:], ["ldt"])
            dump("a_fm", dtraw[:], ["a_fm"])
        if phases >= 3:
            dump("bfm", b_fm[:], ["bfm"])
            dump("cfm", c_fm[:], ["cfm"])
            dump("fixtok", fixtok[:], [("ftk", o) for o in range(33)])

        fw.barrier()
        pool.pop()

        if phases >= 4:
            pool.push()
            xtok = [pool.bf16(CONVD) for _ in range(2)]
            xwb = [pool.bf16(4096) for _ in range(2)]
            sstate = pool.f32(4096, parts=64)
            btokb = [pool.bf16(64) for _ in range(2)]
            wexpf = pool.f32(128, parts=64)
            wexpt = pool.f32(64)
            dacbb = [pool.f32(64, parts=64) for _ in range(2)]
            qd2 = [pool.bf16(2048) for _ in range(2)]
            wdn2 = [pool.bf16(2048) for _ in range(2)]
            scdn = pool.f32(2048)
            fw.op("q", lambda e: e.dma_start(out=scdn, in_=scdn_in),
                  writes=["scdn"], dma=fw.dsem("scdnS"))
            fw.op("v", lambda e: e.memset(sstate, 0.0),
                  writes=[("sst", w2) for w2 in range(4)])

            qdi = 0

            def emit_qdn_piece(idx):
                o, cq = idx // 4, idx % 4
                qb = qd2[idx % 2]
                wn = wdn2[idx % 2]
                fw.op("g", (lambda o_, cq_, qb_: lambda e: e.dma_start(
                    out=qb_, in_=qdn_in[o_ * 128:(o_ + 1) * 128,
                                        cq_ * 2048:(cq_ + 1) * 2048]))(o, cq, qb),
                    writes=[("qd", idx % 2)], dma=fw.dsem(f"qdn{idx % 2}"))
                fw.op("v", (lambda o_, cq_, qb_, wn_: lambda e: e.scalar_tensor_tensor(
                    out=wn_.rearrange("p (b k) -> p b k", b=32),
                    in0=qb_.rearrange("p (b k) -> p b k", b=32),
                    scalar=-7.5,
                    in1=scdn[:, o_ * 128 + cq_ * 32:o_ * 128 + (cq_ + 1) * 32]
                        .unsqueeze(2).broadcast_to([128, 32, 64]),
                    op0=ALU.add, op1=ALU.mult))(o, cq, qb, wn),
                    reads=[("qd", idx % 2), "scdn"], writes=[("wdn", idx % 2)])
                fw.op("q", (lambda o_, cq_, wn_: lambda e: e.dma_start(
                    out=wdnn_hbm[o_ * 128:(o_ + 1) * 128,
                                 cq_ * 2048:(cq_ + 1) * 2048], in_=wn_))(o, cq, wn),
                    reads=[("wdn", idx % 2)], writes=[("wdnn", idx)],
                    dma=fw.dsem(f"wdnnS{idx % 2}"))

            XKEYS = [("xfm", o) for o in range(33)]

            def emit_prep4(q):
                xi = q % 2
                fw.op("v", (lambda q_: lambda e: e.tensor_sub(
                    wexpf, ldt[:, q_ * 128:(q_ + 1) * 128],
                    cuma[:, q_ * 128:(q_ + 1) * 128]))(q),
                    reads=["ldt", ("cuma", q)], writes=["wef"])
                fw.op("v", (lambda q_: lambda e: e.tensor_scalar(
                    out=wexpf, in0=wexpf,
                    scalar1=cuma[:, q_ * 128 + 127:q_ * 128 + 128],
                    scalar2=None, op0=ALU.add))(q),
                    reads=[("cuma", q)], rw=["wef"])
                fw.op("s", lambda e: e.activation(out=wexpf, in_=wexpf, func=AF.Exp),
                      rw=["wef"])
                tg = f"xtkG{xi}"
                fw.dma_begin(tg)
                for r2 in range(2):
                    fw.op("q", (lambda q_, xi_, r3: lambda e: e.dma_start_transpose(
                        xtok[xi_][:, r3 * 2112:(r3 + 1) * 2112],
                        xfm_hbm[r3 * 2112:(r3 + 1) * 2112,
                                q_ * 128:(q_ + 1) * 128]))(q, xi, r2),
                        reads=(XKEYS if (q == 0 and r2 == 0) else [])
                        + ([("xwm", xi)] if q >= 2 else []),
                        writes=[("xtkp", xi, r2)], dma=tg, group=True)
                fw.dma_end(tg, extra_keys=[("xtok", xi)])
                if q == 0:
                    fw.op("v", (lambda xi_: lambda e: e.tensor_copy(
                        out=xtok[xi_][0:3, :], in_=fixtok[0:3, :]))(xi),
                        rw=[("xtok", xi)])
                fw.op("t", lambda e: e.transpose(ps(6, 0, 64), wexpf,
                                                 identf[0:64, 0:64]),
                      reads=["wef", d_name(identf)], writes=[("PS", 6)])
                fw.op("v", lambda e: e.tensor_copy(out=wexpt, in_=ps(6, 0, 64)),
                      reads=[("PS", 6)], writes=["wet"])
                fw.op("v", (lambda xi_: lambda e: e.tensor_mul(
                    xwb[xi_].rearrange("p (b k) -> p b k", b=64),
                    xtok[xi_][:, 0:4096].rearrange("p (b k) -> p b k", b=64),
                    wexpt.unsqueeze(2).broadcast_to([128, 64, 64])))(xi),
                    reads=[("xtok", xi), "wet"], writes=[("xw", xi), ("xwm", xi)])
                fw.op("t", (lambda q_: lambda e: e.transpose(
                    ps(7, 0, 32).bitcast(BF16)[:, 0:64],
                    b_fm[:, q_ * 128:(q_ + 1) * 128], identb[0:64, 0:64]))(q),
                    reads=["bfm", d_name(identb)], writes=[("PS", 7)])
                fw.op("v", (lambda xi_: lambda e: e.tensor_copy(
                    out=btokb[xi_], in_=ps(7, 0, 32).bitcast(BF16)[:, 0:64]))(xi),
                    reads=[("PS", 7)], writes=[("btk", xi)])
                fw.op("t", (lambda q_: lambda e: e.transpose(
                    ps(6, 64, 128, parts=1),
                    cuma[:, q_ * 128 + 127:q_ * 128 + 128],
                    identf[0:64, 0:64]))(q),
                    reads=[("cuma", q), d_name(identf)], writes=[("PS", 6)])
                fw.op("v", lambda e: e.tensor_copy(out=rowb[0:1, 0:64],
                                                   in_=ps(6, 64, 128, parts=1)),
                      reads=[("PS", 6)], writes=["rowb"])
                fw.op("t", lambda e: e.matmul(ps(6, 128, 192, parts=64),
                                              ones1f[0:1, 0:64], rowb[0:1, 0:64],
                                              start=True, stop=True),
                      reads=["rowb", d_name(ones1f)], writes=[("PS", 6)])
                fw.op("s", (lambda xi_: lambda e: e.activation(
                    out=dacbb[xi_], in_=ps(6, 128, 192, parts=64),
                    func=AF.Exp))(xi),
                    reads=[("PS", 6)], writes=[("dacb", xi)])

            def emit_waves4(q, wlist):
                xi = q % 2
                for w in wlist:
                    bbase = (w % 2) * 2
                    for hh in range(16):
                        h = w * 16 + hh
                        b_ = bbase + hh // 8
                        lo = (hh % 8) * 64
                        fw.op("t", (lambda h_, b2, lo_, xi_: lambda e: e.matmul(
                            ps(b2, lo_, lo_ + 64, parts=64),
                            btokb[xi_], xwb[xi_][:, h_ * 64:(h_ + 1) * 64],
                            start=True, stop=True))(h, b_, lo, xi),
                            reads=([("btk", xi), ("xw", xi)]
                                   if hh in (0, 15) else []),
                            writes=([("PS", b2) for b2 in (bbase, bbase + 1)]
                                    if hh == 0 else
                                    ([("PS", b_)] if hh % 8 == 7 else [])))
                    fw.op("v", (lambda w_, xi_: lambda e: e.tensor_mul(
                        sstate[:, w_ * 1024:(w_ + 1) * 1024].rearrange(
                            "p (b k) -> p b k", b=16),
                        sstate[:, w_ * 1024:(w_ + 1) * 1024].rearrange(
                            "p (b k) -> p b k", b=16),
                        dacbb[xi_][:, w_ * 16:(w_ + 1) * 16].unsqueeze(2)
                        .broadcast_to([64, 16, 64])))(w, xi),
                        reads=[("dacb", xi)], rw=[("sst", w)])
                    fw.op("v", (lambda w_: lambda e: e.tensor_add(
                        sstate[:, w_ * 1024:(w_ + 1) * 1024],
                        sstate[:, w_ * 1024:(w_ + 1) * 1024],
                        PA[0:64, (w_ % 2) * 1024:(w_ % 2) * 1024 + 1024]))(w),
                        reads=[("PS", bbase), ("PS", bbase + 1)], rw=[("sst", w)])
                    if w % 2 == 1:
                        half = w // 2
                        fw.op("q", (lambda q2, h2: lambda e: e.dma_start(
                            out=s_hbm[q2][:, h2 * 2048:(h2 + 1) * 2048],
                            in_=sstate[:, h2 * 2048:(h2 + 1) * 2048]))(q, half),
                            reads=[("sst", half * 2), ("sst", half * 2 + 1)],
                            writes=[("shbm", q, half)],
                            dma=fw.dsem(f"shS{half}"))

            for q in range(NCH):
                if q == 0:
                    emit_prep4(0)
                emit_waves4(q, [0, 1])
                if q + 1 < NCH:
                    emit_prep4(q + 1)
                emit_waves4(q, [2, 3])
                for _ in range(4):
                    if qdi < 64:
                        emit_qdn_piece(qdi)
                        qdi += 1
            dump("sfin", sstate, [("sst", w2) for w2 in range(4)])
            fw.op("q", lambda e: e.dma_start(out=ex2_in, in_=sstate),
                  reads=[("sst", w2) for w2 in range(4)], writes=["ex2i"],
                  dma=fw.dsem("ex2S"))
            fw.op("g", lambda e: e.collective_compute(
                "AllGather", ALU.bypass, replica_groups=RG,
                ins=[ex2_in], outs=[ex2_out]),
                reads=["ex2i"], writes=["ex2o"])
            fw.barrier()
            pool.pop()

        if phases >= 5:
            pool.push()
            xtok = [pool.bf16(CONVD) for _ in range(2)]
            ysb = pool.f32(4096)
            yntok = pool.bf16(4096)
            sinb = pool.f32(4096, parts=64)
            seffb = [pool.bf16(4096, parts=64) for _ in range(2)]
            sloc = [pool.bf16(4096, parts=64) for _ in range(2)]
            cbmb = [pool.bf16(128) for _ in range(2)]
            diall = pool.bf16(64 * 128)
            dclg = [pool.f32(512) for _ in range(3)]
            expg = [pool.f32(512) for _ in range(3)]
            msTg = [pool.bf16(512) for _ in range(16)]
            cumatb = [pool.f32(64) for _ in range(2)]
            etokb = [pool.f32(64) for _ in range(2)]
            gqb = [pool.f32(64, parts=64) for _ in range(2)]
            ccol = pool.f32(1, parts=64)
            tmpcb = [pool.f32(128, parts=64) for _ in range(2)]
            wt2 = [pool.bf16(2048) for _ in range(2)]
            wtp = [pool.bf16(2048) for _ in range(4)]

            def emit_wupT(k):
                for q2 in range(4):
                    wt = wtp[q2]
                    fw.op("q", (lambda k_, q3, w_: lambda e: e.dma_start_transpose(
                        w_, wupn_hbm[q3 * 2048:(q3 + 1) * 2048,
                                     k_ * 128:(k_ + 1) * 128]))(k, q2, wt),
                        reads=([("wupn", oo) for oo in range(64)]
                               if (k == 0 and q2 == 0) else [])
                        + ([("wupiq", k - 1, q2)] if k >= 1 else []),
                        writes=[("wtrp", q2)], dma=fw.dsem(f"wtr{q2}"))
                    dst = wupi_hbm.rearrange("o p j -> p o j")[
                        :, q2 * 16:(q2 + 1) * 16, k * 128:(k + 1) * 128]
                    fw.op("g", (lambda d_, k_, q3, w_: lambda e: e.dma_start(
                        out=d_, in_=w_.rearrange("p (o j) -> p o j", o=16),
                        allow_slow_non_contiguous=True))(dst, k, q2, wt),
                        reads=[("wtrp", q2)], writes=[("wupiq", k, q2)]
                        + ([("wupik", kk) for kk in range(16)] if k == 15 and q2 == 3
                           else []),
                        dma=fw.dsem(f"wupi{q2}"))

            def emit_wdnT(o):
                w_ = wt2[o % 2]
                fw.op("q", (lambda o_, w2: lambda e: e.dma_start_transpose(
                    w2, wdnn_hbm[:, o_ * 128:(o_ + 1) * 128]))(o, w_),
                    reads=([("wdnn", ii) for ii in range(64)] if o == 0 else []),
                    writes=[("wt2", o % 2)], dma=fw.dsem(f"wt2{o % 2}"))
                dst = wdni_hbm.rearrange("g p o j -> p g o j")[:, :, o, :]
                fw.op("g", (lambda d_, o_, w2: lambda e: e.dma_start(
                    out=d_, in_=w2.rearrange("p (g j) -> p g j", g=16),
                    allow_slow_non_contiguous=True))(dst, o, w_),
                    reads=[("wt2", o % 2)], writes=[("wdni", o)],
                    dma=fw.dsem(f"wdi{o % 2}"))

            for h in range(NH):
                fw.op("v", (lambda h_: lambda e: e.tensor_scalar(
                    out=diall[:, h_ * 128:(h_ + 1) * 128], in0=identb[:],
                    scalar1=dbc[:, h_:h_ + 1], scalar2=None, op0=ALU.mult))(h),
                    reads=[d_name(identb), "dbc"], writes=[("dia", h)])
            fw.op("q", lambda e: e.dma_start(out=sinb, in_=ex2_out[0:64, :]),
                  writes=["sin0"], dma=fw.dsem("sinS"))
            fw.op("v", lambda e: e.tensor_scalar(
                out=sinb, in0=sinb, scalar1=isodd[0:64, 0:1], scalar2=None,
                op0=ALU.mult),
                reads=["sin0", d_name(isodd)], writes=["sin"])
            fw.op("v", lambda e: e.memset(sloc[0], 0.0), writes=[("sloc", 0)])
            fw.op("v", lambda e: e.memset(ccol, 0.0), writes=["ccol"])
            wdni_cnt = [0]

            def emit_xtok5(q):
                xi = q % 2
                tg = f"xtkG{xi}"
                fw.dma_begin(tg)
                for r2 in range(2):
                    fw.op("q", (lambda q_, xi_, r3: lambda e: e.dma_start_transpose(
                        xtok[xi_][:, r3 * 2112:(r3 + 1) * 2112],
                        xfm_hbm[r3 * 2112:(r3 + 1) * 2112,
                                q_ * 128:(q_ + 1) * 128]))(q, xi, r2),
                        reads=([("xtk5", xi)] if q >= 2 else []),
                        writes=[("xtkp", xi, r2)], dma=tg, group=True)
                fw.dma_end(tg, extra_keys=[("xtok", xi)])
                if q == 0:
                    fw.op("v", (lambda xi_: lambda e: e.tensor_copy(
                        out=xtok[xi_][0:3, :], in_=fixtok[0:3, :]))(xi),
                        rw=[("xtok", xi)])

            def emit_prep5(q):
                pr = q % 2
                if q >= 1:
                    fw.op("g", (lambda q_, pr_: lambda e: e.dma_start(
                        out=sloc[pr_], in_=s_hbm[q_ - 1]))(q, pr),
                        writes=[("sloc", pr)], dma=fw.dsem(f"slc{pr}"))
                    fw.op("v", (lambda q_: lambda e: e.tensor_scalar(
                        out=ccol, in0=ccol,
                        scalar1=cuma[:, (q_ - 1) * 128 + 127:(q_ - 1) * 128 + 128],
                        scalar2=None, op0=ALU.add))(q),
                        rw=["ccol"])
                fw.op("t", lambda e: e.transpose(ps(0, 0, 64, parts=1), ccol,
                                                 identf[0:64, 0:64]),
                      reads=["ccol", d_name(identf)], writes=[("PS", 0)])
                fw.op("v", lambda e: e.tensor_copy(out=rowb[0:1, 0:64],
                                                   in_=ps(0, 0, 64, parts=1)),
                      reads=[("PS", 0)], writes=["rowb"])
                fw.op("t", lambda e: e.matmul(ps(0, 64, 128, parts=64),
                                              ones1f[0:1, 0:64], rowb[0:1, 0:64],
                                              start=True, stop=True),
                      reads=["rowb", d_name(ones1f)], writes=[("PS", 0)])
                fw.op("s", (lambda pr_: lambda e: e.activation(
                    out=gqb[pr_], in_=ps(0, 64, 128, parts=64),
                    func=AF.Exp))(pr),
                    reads=[("PS", 0)], writes=[("gq", pr)])
                fw.op("v", (lambda pr_: lambda e: e.tensor_mul(
                    seffb[pr_].rearrange("p (b k) -> p b k", b=64),
                    sinb.rearrange("p (b k) -> p b k", b=64),
                    gqb[pr_].unsqueeze(2).broadcast_to([64, 64, 64])))(pr),
                    reads=["sin", ("gq", pr)], writes=[("seff", pr)])
                fw.op("v", (lambda pr_: lambda e: e.tensor_add(
                    seffb[pr_], seffb[pr_], sloc[pr_]))(pr),
                    reads=[("sloc", pr)], rw=[("seff", pr)])
                fw.op("v", (lambda q_, pr_: lambda e: e.tensor_sub(
                    tmpcb[pr_], ldt[:, q_ * 128:(q_ + 1) * 128],
                    cuma[:, q_ * 128:(q_ + 1) * 128]))(q, pr),
                    reads=["ldt", ("cuma", q)], writes=[("tmpc", pr)])
                fw.op("t", (lambda q_: lambda e: e.transpose(
                    ps(0, 192, 256), cuma[:, q_ * 128:(q_ + 1) * 128],
                    identf[0:64, 0:64]))(q),
                    reads=[("cuma", q), d_name(identf)], writes=[("PS", 0)])
                fw.op("v", (lambda pr_: lambda e: e.tensor_copy(
                    out=cumatb[pr_], in_=ps(0, 192, 256)))(pr),
                    reads=[("PS", 0)], writes=[("cumat", pr)])
                fw.op("s", (lambda pr_: lambda e: e.activation(
                    out=etokb[pr_], in_=cumatb[pr_], func=AF.Exp))(pr),
                    reads=[("cumat", pr)], writes=[("etok", pr)])
                fw.op("t", (lambda q_: lambda e: e.matmul(
                    ps(1, 0, 128), b_fm[:, q_ * 128:(q_ + 1) * 128],
                    c_fm[:, q_ * 128:(q_ + 1) * 128],
                    start=True, stop=True))(q),
                    reads=["bfm", "cfm"], writes=[("PS", 1)])
                fw.op("v", (lambda pr_: lambda e: e.tensor_mul(
                    cbmb[pr_], ps(1, 0, 128), maskc[:]))(pr),
                    reads=[("PS", 1), d_name(maskc)], writes=[("cbm", pr)])

            def ecol(h_):
                return identf[0:64, h_:h_ + 1].broadcast_to([64, 128])

            def emit_M5(q):
                pr = q % 2
                for g4 in range(16):
                    dp = 6 + (g4 % 2)
                    gi = g4 % 3
                    gm = g4
                    for hh in range(4):
                        h = g4 * 4 + hh
                        fw.op("t", (lambda q_, h_, dp_, s_: lambda e: e.matmul(
                            ps(dp_, s_ * 128, (s_ + 1) * 128), ecol(h_),
                            cuma[:, q_ * 128:(q_ + 1) * 128],
                            start=True, stop=False))(q, h, dp, hh),
                            reads=([("cuma", q), d_name(identf)] if hh == 0 else []),
                            writes=([("PS", dp)] if hh == 0 else []))
                        fw.op("t", (lambda h_, dp_, s_, pr_: lambda e: e.matmul(
                            ps(dp_, s_ * 128, (s_ + 1) * 128), tmpcb[pr_], ecol(h_),
                            start=False, stop=True))(h, dp, hh, pr),
                            reads=([("tmpc", pr)] if hh == 0 else []),
                            writes=([("PS", dp)] if hh == 3 else []))
                    fw.op("v", (lambda dp_, g_: lambda e: e.tensor_scalar(
                        out=dclg[g_], in0=ps(dp_, 0, 512),
                        scalar1=80.0, scalar2=None, op0=ALU.min))(dp, gi),
                        reads=[("PS", dp)], writes=[("dcl", gi)])
                    fw.op("s", (lambda g_: lambda e: e.activation(
                        out=expg[g_], in_=dclg[g_], func=AF.Exp))(gi),
                        reads=[("dcl", gi)], writes=[("expg", gi)])
                    fw.op("g", (lambda g_, gm_, pr_: lambda e: e.tensor_mul(
                        msTg[gm_].rearrange("p (a b) -> p a b", a=4),
                        expg[g_].rearrange("p (a b) -> p a b", a=4),
                        cbmb[pr_].unsqueeze(1).broadcast_to([128, 4, 128])))(
                        gi, gm, pr),
                        reads=[("expg", gi), ("cbm", pr)], writes=[("msT", gm)])
                    fw.op("g", (lambda gm_, g4_: lambda e: e.tensor_add(
                        msTg[gm_], msTg[gm_],
                        diall[:, g4_ * 512:(g4_ + 1) * 512]))(gm, g4),
                        reads=[("dia", g4 * 4)], rw=[("msT", gm)])

            def emit_Y5(q):
                pr = q % 2
                xi = q % 2
                for g4 in range(16):
                    gm = g4
                    bi = 2 + ((g4 // 2) % 2) * 2
                    for hh in range(4):
                        h = g4 * 4 + hh
                        lo = (h % 8) * 64
                        fw.op("t", (lambda h_, b_, lo_, xi_, g_, s_: lambda e: e.matmul(
                            ps(b_, lo_, lo_ + 64), msTg[g_][:, s_ * 128:(s_ + 1) * 128],
                            xtok[xi_][:, h_ * 64:(h_ + 1) * 64],
                            start=True, stop=True))(h, bi, lo, xi, gm, hh),
                            reads=[("msT", gm), ("xtok", xi)],
                            rw=([("xtk5", xi)] if (g4 == 15 and hh == 3) else []),
                            writes=([("PS", bi)]
                                    if ((g4 % 2 == 0 and hh == 0) or
                                        (g4 % 2 == 1 and hh == 3)) else []))
                        fw.op("t", (lambda q_, h_, b_, lo_, pr_: lambda e: e.matmul(
                            ps(b_, lo_, lo_ + 64), c_fm[:, q_ * 128:(q_ + 1) * 128],
                            seffb[pr_][:, h_ * 64:(h_ + 1) * 64],
                            start=True, stop=True))(q, h, bi + 1, lo, pr),
                            reads=([("seff", pr), "cfm"]
                                   if (h < 4 or h == 63) else []),
                            writes=([("PS", bi + 1)]
                                    if ((g4 % 2 == 0 and hh == 0) or
                                        (g4 % 2 == 1 and hh == 3)) else []))
                    if g4 % 2 == 1:
                        g8 = g4 // 2
                        fw.op("v", (lambda g8_, b_, pr_: lambda e: e.tensor_mul(
                            ysb[:, g8_ * 512:(g8_ + 1) * 512]
                            .rearrange("p (a b) -> p a b", a=8),
                            ps(b_ + 1, 0, 512).rearrange("p (a b) -> p a b", a=8),
                            etokb[pr_][:, g8_ * 8:(g8_ + 1) * 8].unsqueeze(2)
                            .broadcast_to([128, 8, 64])))(g8, bi, pr),
                            reads=[("PS", bi + 1), ("etok", pr)],
                            writes=[("ysb", g8)])
                        fw.op("v", (lambda g8_, b_: lambda e: e.tensor_add(
                            ysb[:, g8_ * 512:(g8_ + 1) * 512],
                            ysb[:, g8_ * 512:(g8_ + 1) * 512],
                            ps(b_, 0, 512)))(g8, bi),
                            reads=[("PS", bi)], rw=[("ysb", g8)])

            def emit_rms5(q):
                fw.op("s", lambda e: e.activation(out=yntok, in_=ysb, func=AF.Square,
                                                  accum_out=stat[:, 120:121]),
                      reads=[("ysb", g2) for g2 in range(8)], writes=["ss", "yn"])
                fw.op("v", lambda e: e.tensor_scalar(
                    out=stat[:, 121:122], in0=stat[:, 120:121], scalar1=1.0 / 4096,
                    scalar2=1e-6, op0=ALU.mult, op1=ALU.add),
                    reads=["ss"], writes=["ssv"])
                fw.op("v", lambda e: e.reciprocal(stat[:, 122:123], stat[:, 121:122]),
                      reads=["ssv"], writes=["ssr"])
                fw.op("s", lambda e: e.activation(out=stat[:, 123:124],
                                                  in_=stat[:, 122:123], func=AF.Sqrt),
                      reads=["ssr"], writes=["rms"])
                fw.op("s", lambda e: e.activation(out=yntok, in_=ysb,
                                                  func=AF.Identity,
                                                  scale=stat[:, 123:124]),
                      reads=[("ysb", g2) for g2 in range(8)] + ["rms"],
                      rw=["yn"])
                fw.op("q", (lambda q_: lambda e: e.dma_start(
                    out=yn_hbm[q_ * 128:(q_ + 1) * 128, :], in_=yntok))(q),
                    reads=["yn"], writes=[("ynh", q)], dma=fw.dsem(f"ynS{q % 2}"))

            for q in range(NCH):
                if q == 0:
                    emit_xtok5(0)
                    emit_prep5(0)
                emit_M5(q)
                if q + 1 < NCH:
                    emit_prep5(q + 1)
                    emit_xtok5(q + 1)
                emit_Y5(q)
                emit_rms5(q)
                for _ in range(4):
                    if wdni_cnt[0] < 64:
                        emit_wdnT(wdni_cnt[0])
                        wdni_cnt[0] += 1
                emit_wupT(q)
            fw.barrier()
            pool.pop()
            pool.pop()

        if phases >= 6:
            pool.push()
            stgb = [pool.f32(2048) for _ in range(2)]
            xtl = pool.f32(2048)
            l3tv = xtl.bitcast(BF16)[:, 0:2048]

            def emit_p7_tile(tt):
                ti = tt % 2
                stg = stgb[ti]
                pb = ti * 4
                gs = f"asS{ti}"
                fw.dma_begin(gs)
                for hh2 in range(2):
                    srcap = attn_hbm.rearrange("(k p) t -> p k t", p=128)[
                        :, hh2 * 8:(hh2 + 1) * 8, tt * 128:(tt + 1) * 128]
                    fw.op("g", (lambda s_, h2, st_: lambda e: e.dma_start(
                        out=st_[:, h2 * 1024:(h2 + 1) * 1024]
                        .rearrange("p (k j) -> p k j", k=8),
                        in_=s_, allow_slow_non_contiguous=True))(srcap, hh2, stg),
                        reads=[("attn", k2, tt // 8) for k2 in range(hh2 * 8, hh2 * 8 + 8)],
                        writes=[("ast", ti, k2) for k2 in range(hh2 * 8, hh2 * 8 + 8)]
                        + ([("ln2o", ti), ("x4t", ti)] if hh2 == 0 else []),
                        dma=gs, group=True)
                fw.dma_end(gs)
                for k in range(16):
                    bnk = pb + k // 4
                    fw.op("t", (lambda k_, b_, st_: lambda e: e.transpose(
                        ps(b_, (k_ % 4) * 128, (k_ % 4) * 128 + 128),
                        st_[:, k_ * 128:(k_ + 1) * 128], identf[:]))(k, bnk, stg),
                        reads=[("ast", ti, k), d_name(identf)], writes=[("PS", bnk)])
                for b_ in range(4):
                    fw.op("v", (lambda b2, st_: lambda e: e.tensor_copy(
                        out=st_[:, b2 * 512:(b2 + 1) * 512],
                        in_=ps(pb + b2, 0, 512)))(b_, stg),
                        reads=[("PS", pb + b_)],
                        writes=[("atok", ti, b_)]
                        + [("ast", ti, 4 * b_ + i2) for i2 in range(4)]
                        + ([("x4t", ti), ("ln2o", ti)] if b_ == 0 else []))
                emit_lnstats([("atok", ti, b2) for b2 in range(4)], stg, 1e-5, 32)
                fw.op("g", (lambda tt_, x_: lambda e: e.dma_start(
                    out=x_, in_=x_in[tt_ * 128:(tt_ + 1) * 128, :]))(tt, xtl),
                    writes=["xtl"], dma=fw.dsem("x7S"))
                fw.op("s", (lambda st_: lambda e: e.activation(
                    out=st_, in_=st_, func=AF.Identity,
                    bias=stat[:, 33:34], scale=stat[:, 32:33]))(stg),
                    reads=[("rs", 32), ("nb", 32)],
                    rw=[("atok", ti, b2) for b2 in range(4)],
                    writes=[("ln2o", ti)] + [("ast", ti, k2) for k2 in range(16)])
                fw.op("v", (lambda st_, x_: lambda e: e.tensor_add(
                    st_, x_, st_))(stg, xtl),
                    reads=["xtl"], rw=[("ln2o", ti)], writes=[("x4t", ti)])
                emit_lnstats([("x4t", ti)], stg, 1e-5, 64)
                fw.op("s", (lambda st_, l_: lambda e: e.activation(
                    out=l_, in_=st_, func=AF.Identity,
                    bias=stat[:, 65:66], scale=stat[:, 64:65]))(stg, l3tv),
                    reads=[("x4t", ti), ("rs", 64), ("nb", 64)],
                    writes=[("l3t", ti)], rw=["xtl"])
                fw.op("q", (lambda tt_, st_: lambda e: e.dma_start(
                    out=x4_hbm[tt_ * 128:(tt_ + 1) * 128, :], in_=st_))(tt, stg),
                    reads=[("x4t", ti)], writes=[("x4", tt)], dma=fw.dsem(f"x4S{ti}"))
                fw.op("q", (lambda tt_, l_: lambda e: e.dma_start(
                    out=l3x_hbm[tt_ * 128:(tt_ + 1) * 128, :], in_=l_))(tt, l3tv),
                    reads=[("l3t", ti), "xtl"], writes=[("l3", tt)],
                    dma=fw.dsem(f"l3S{ti}"))

            def emit_p9_tile(tt):
                ti = tt % 2
                stg = stgb[ti]
                pb = ti * 4
                gs = f"ysS{ti}"
                fw.dma_begin(gs)
                for hh2 in range(2):
                    srcap = y2_hbm.rearrange("(k p) t -> p k t", p=128)[
                        :, hh2 * 8:(hh2 + 1) * 8, tt * 128:(tt + 1) * 128]
                    fw.op("g", (lambda s_, h2, st_: lambda e: e.dma_start(
                        out=st_[:, h2 * 1024:(h2 + 1) * 1024]
                        .rearrange("p (k j) -> p k j", k=8),
                        in_=s_, allow_slow_non_contiguous=True))(srcap, hh2, stg),
                        reads=[("y2", k2, tt // 8) for k2 in range(hh2 * 8, hh2 * 8 + 8)],
                        writes=[("yst", ti, k2) for k2 in range(hh2 * 8, hh2 * 8 + 8)]
                        + ([("ln4o", ti), ("outt", ti)] if hh2 == 0 else []),
                        dma=gs, group=True)
                fw.dma_end(gs)
                for k in range(16):
                    bnk = pb + k // 4
                    fw.op("t", (lambda k_, b_, st_: lambda e: e.transpose(
                        ps(b_, (k_ % 4) * 128, (k_ % 4) * 128 + 128),
                        st_[:, k_ * 128:(k_ + 1) * 128], identf[:]))(k, bnk, stg),
                        reads=[("yst", ti, k), d_name(identf)], writes=[("PS", bnk)])
                for b_ in range(4):
                    fw.op("v", (lambda b2, st_: lambda e: e.tensor_copy(
                        out=st_[:, b2 * 512:(b2 + 1) * 512],
                        in_=ps(pb + b2, 0, 512)))(b_, stg),
                        reads=[("PS", pb + b_)],
                        writes=[("y2t", ti, b_)]
                        + [("yst", ti, 4 * b_ + i2) for i2 in range(4)]
                        + ([("ln4o", ti), ("outt", ti)] if b_ == 0 else []))
                emit_lnstats([("y2t", ti, b2) for b2 in range(4)], stg, 1e-5, 96)
                fw.op("g", (lambda tt_, x_: lambda e: e.dma_start(
                    out=x_, in_=x4_hbm[tt_ * 128:(tt_ + 1) * 128, :]))(tt, xtl),
                    reads=[("x4", tt)], writes=["xtl"], dma=fw.dsem("x49S"))
                fw.op("s", (lambda st_: lambda e: e.activation(
                    out=st_, in_=st_, func=AF.Identity,
                    bias=stat[:, 97:98], scale=stat[:, 96:97]))(stg),
                    reads=[("rs", 96), ("nb", 96)],
                    rw=[("y2t", ti, b2) for b2 in range(4)],
                    writes=[("ln4o", ti)] + [("yst", ti, k2) for k2 in range(16)])
                fw.op("v", (lambda st_, x_: lambda e: e.tensor_add(
                    st_, x_, st_))(stg, xtl),
                    reads=["xtl"], rw=[("ln4o", ti)], writes=[("outt", ti)])
                fw.op("q", (lambda tt_, st_: lambda e: e.dma_start(
                    out=out_hbm[tt_ * 128:(tt_ + 1) * 128, :], in_=st_))(tt, stg),
                    reads=[("outt", ti)], writes=[("out", tt)],
                    dma=fw.dsem(f"outS{ti}"))

            pool.push()
            l1fm = pool.bf16(16 * 2048)
            wbuf6 = [pool.bf16(2048) for _ in range(3)]
            gtile = pool.bf16(2048)
            yfm = [pool.bf16(2048) for _ in range(2)]
            ygout = pool.bf16(2048)

            def l1fm_k(k):
                return l1fm[:, k * 2048:(k + 1) * 2048]

            fw.dma_begin("l1fS")
            for k in range(16):
                fw.op("q", (lambda k_: lambda e: e.dma_start_transpose(
                    l1fm_k(k_), l1x_hbm[:, k_ * 128:(k_ + 1) * 128]))(k),
                    reads=[("l1x", tt2) for tt2 in range(16)] if k == 0 else [],
                    writes=[("l1f", k)], dma="l1fS", group=True)
            fw.dma_end("l1fS")
            for o in range(32):
                wb = wbuf6[o % 3]
                wbk = ("wb6", o % 3)
                fw.op("g", (lambda o_, wb_: lambda e: e.dma_start(
                    out=wb_, in_=wg_img[o_]))(o, wb),
                    writes=[wbk], dma=fw.dsem(f"wg{o % 3}"))
                Pt = o % 2
                for tt in range(4):
                    for k in range(16):
                        first, last = k == 0, k == 15
                        rd = []
                        if first:
                            rd = [wbk] + [("l1f", kk) for kk in range(16)]
                        elif last:
                            rd = [wbk]
                        fw.op("t", (lambda wb_, k_, tt_, f_, l_, p_: lambda e: e.matmul(
                            ps(p_ * 4 + tt_), wb_[:, k_ * 128:(k_ + 1) * 128],
                            l1fm_k(k_)[:, tt_ * 512:(tt_ + 1) * 512],
                            start=f_, stop=l_))(wb, k, tt, first, last, Pt),
                            reads=rd,
                            writes=([("PS", Pt * 4 + tt)] if (first or last) else []))
                fw.op("s", (lambda p_: lambda e: e.activation(
                    out=gtile, in_=(PA if p_ == 0 else PB)[:, 0:T],
                    func=AF.Silu))(Pt),
                    reads=[("PS", Pt * 4 + tt) for tt in range(4)],
                    writes=["gtile"])
                yf = yfm[o % 2]
                fw.op("q", (lambda o_, yf_: lambda e: e.dma_start_transpose(
                    yf_, yn_hbm[:, o_ * 128:(o_ + 1) * 128]))(o, yf),
                    reads=[("ynh", q2) for q2 in range(NCH)],
                    writes=[("yfm", o % 2)], dma=fw.dsem(f"yfm{o % 2}"))
                fw.op("v", (lambda o_, yf_: lambda e: e.tensor_scalar(
                    out=ygout, in0=yf_, scalar1=nw[:, o_:o_ + 1], scalar2=None,
                    op0=ALU.mult))(o, yf),
                    reads=[("yfm", o % 2), d_name(nw)], writes=["ygo"])
                fw.op("v", lambda e: e.tensor_mul(ygout, ygout, gtile),
                      reads=["gtile"], rw=["ygo"])
                fw.op("q", (lambda o_: lambda e: e.dma_start(
                    out=yg_hbm[o_ * 128:(o_ + 1) * 128, :], in_=ygout))(o),
                    reads=["ygo"], writes=[("ygh", o)], dma=fw.dsem(f"ygS{o % 2}"))
            fw.barrier()
            pool.pop()

            pool.push()
            ygh = pool.bf16(32 * 1024)
            wobuf = [pool.bf16(4096) for _ in range(2)]
            attnev = pool.f32(1024)
            for H in range(2):
                fw.dma_begin("ygtS")
                for k in range(32):
                    fw.op("q", (lambda k_, H_: lambda e: e.dma_start(
                        out=ygh[:, k_ * 1024:(k_ + 1) * 1024],
                        in_=yg_hbm[k_ * 128:(k_ + 1) * 128,
                                   H_ * 1024:(H_ + 1) * 1024]))(k, H),
                        reads=[("ygh", oo) for oo in range(32)] if k == 0 else [],
                        writes=[("ygt", k)], dma="ygtS", group=True)
                fw.dma_end("ygtS")
                for o in range(16):
                    wb = wobuf[o % 2]
                    wbk = ("wo", o % 2)
                    fw.op("g", (lambda o_, wb_: lambda e: e.dma_start(
                        out=wb_, in_=wo_img[o_]))(o, wb),
                        writes=[wbk], dma=fw.dsem(f"wo{o % 2}"))
                    slot = o % 4
                    bk = [slot * 2, slot * 2 + 1]
                    for ts_ in range(2):
                        for k in range(32):
                            first, last = k == 0, k == 31
                            rd = []
                            if first or last:
                                rd = [wbk] + [("ygt", kk) for kk in range(32)]
                            fw.op("t", (lambda wb_, k_, t2, f_, l_, b2: lambda e: e.matmul(
                                ps(b2), wb_[:, k_ * 128:(k_ + 1) * 128],
                                ygh[:, k_ * 1024 + t2 * 512:k_ * 1024 + (t2 + 1) * 512],
                                start=f_, stop=l_))(wb, k, ts_, first, last, bk[ts_]),
                                reads=rd,
                                writes=([("PS", bk[ts_])] if (first or last) else []))
                    fw.op("v", (lambda b0: lambda e: e.tensor_copy(
                        out=attnev, in_=ps(b0, 0, 1024)))(bk[0]),
                        reads=[("PS", bk[0]), ("PS", bk[1])], writes=["aev"])
                    fw.op("q", (lambda o_, H_: lambda e: e.dma_start(
                        out=attn_hbm[o_ * 128:(o_ + 1) * 128,
                                     H_ * 1024:(H_ + 1) * 1024], in_=attnev))(o, H),
                        reads=["aev"], writes=[("attn", o, H)],
                        dma=fw.dsem(f"atS{o % 2}"))
                for tt in range(H * 8, H * 8 + 8):
                    emit_p7_tile(tt)
            fw.barrier()
            pool.pop()

        if phases >= 8:
            pool.push()
            x2sb = pool.bf16(64 * 1024)

            def x2_o(o):
                return x2sb[:, o * 1024:(o + 1) * 1024]

            for H in range(2):
                pool.push()
                l3fm = pool.bf16(16 * 1024)
                upw = [pool.bf16(2048) for _ in range(2)]
                bupl = pool.bf16(HF // 8, parts=16)
                t1b = pool.bf16(1024, parts=16)
                fw.op("g", lambda e: e.dma_start(out=bupl, in_=bupl_in[:, 0:HF // 8]),
                      writes=["bupl"], dma=fw.dsem("buplS"))
                fw.dma_begin("l3fS")
                for k in range(16):
                    fw.op("q", (lambda k_, H_: lambda e: e.dma_start_transpose(
                        l3fm[:, k_ * 1024:(k_ + 1) * 1024],
                        l3x_hbm[H_ * 1024:(H_ + 1) * 1024,
                                k_ * 128:(k_ + 1) * 128]))(k, H),
                        reads=[("l3", tt2) for tt2 in range(H * 8, H * 8 + 8)],
                        writes=[("l3f", k)], dma="l3fS", group=True)
                fw.dma_end("l3fS")
                for j in range(2):
                    for k in range(16):
                        first, last = k == 0, k == 15
                        fw.op("t", (lambda k_, j_, f_, l_: lambda e: e.matmul(
                            ps(j_, 0, 512, parts=16),
                            aupb[:, k_ * 16:k_ * 16 + 16],
                            l3fm[:, k_ * 1024 + j_ * 512:k_ * 1024 + (j_ + 1) * 512],
                            start=f_, stop=l_))(k, j, first, last),
                            reads=([d_name(aupb)] + [("l3f", kk) for kk in range(16)]
                                   if first else []),
                            writes=([("PS", j)] if (first or last) else []))
                fw.op("v", lambda e: e.tensor_copy(
                    out=t1b, in_=PA[0:16, 0:1024]),
                    reads=[("PS", 0), ("PS", 1)], writes=["t1b"])
                for o in range(64):
                    if o % 8 == 0 and o > 0:
                        fw.op("g", (lambda o_: lambda e: e.dma_start(
                            out=bupl, in_=bupl_in[:, o_ * 128:(o_ + 8) * 128]))(o),
                            rw=["bupl"], dma="buplS")
                    uw = upw[o % 2]
                    uwk = ("uw", o % 2)
                    fw.op("q", (lambda o_, u_: lambda e: e.dma_start(
                        out=u_, in_=wupi_hbm[o_]))(o, uw),
                        reads=([("wupik", kk) for kk in range(16)] if o == 0 else []),
                        writes=[uwk], dma=fw.dsem(f"uw{o % 2}"))
                    slot = 2 + (o % 2) * 4
                    for j in range(2):
                        b2 = slot + j
                        for k in range(16):
                            first, last = k == 0, k == 15
                            rd = [uwk] + ([("l3f", kk) for kk in range(16)]
                                          if (first or last) else [])
                            if not (first or last):
                                rd = []
                            fw.op("t", (lambda u_, k_, j_, f_, b3: lambda e: e.matmul(
                                ps(b3), u_[:, k_ * 128:(k_ + 1) * 128],
                                l3fm[:, k_ * 1024 + j_ * 512:k_ * 1024 + (j_ + 1) * 512],
                                start=f_, stop=False))(uw, k, j, first, b2),
                                reads=rd,
                                writes=([("PS", b2)] if first else []))
                        fw.op("t", (lambda o_, j_, b3: lambda e: e.matmul(
                            ps(b3), bupl[:, (o_ % 8) * 128:(o_ % 8 + 1) * 128],
                            t1b[:, j_ * 512:(j_ + 1) * 512],
                            start=False, stop=True))(o, j, b2),
                            reads=["t1b", "bupl"], writes=[("PS", b2)])
                    fw.op("s", (lambda o_, s_: lambda e: e.activation(
                        out=x2_o(o_), in_=(PA if s_ == 2 else PB)[:, 1024:2048],
                        func=AF.Relu, bias=bup[:, o_:o_ + 1]))(o, slot),
                        reads=[("PS", slot), ("PS", slot + 1), d_name(bup)],
                        writes=[("x2", o)])
                fw.barrier()
                pool.pop()
                pool.push()
                wdg = [pool.bf16(64 * 128) for _ in range(2)]
                bdnl = pool.bf16(D, parts=16)
                t2b = pool.bf16(1024, parts=16)
                y2ev = pool.f32(1024)
                fw.op("g", lambda e: e.dma_start(out=bdnl, in_=bdnl_in),
                      writes=["bdnl"], dma=fw.dsem("bdnlS"))
                for j in range(2):
                    for o in range(64):
                        first, last = o == 0, o == 63
                        fw.op("t", (lambda o_, j_, f_, l_: lambda e: e.matmul(
                            ps(j_, 0, 512, parts=16),
                            adnb[:, o_ * 16:(o_ + 1) * 16],
                            x2_o(o_)[:, j_ * 512:(j_ + 1) * 512],
                            start=f_, stop=l_))(o, j, first, last),
                            reads=([d_name(adnb)] + [("x2", oo) for oo in range(64)]
                                   if first else ([("x2", 63)] if last else [])),
                            writes=([("PS", j)] if (first or last) else []))
                fw.op("v", lambda e: e.tensor_copy(
                    out=t2b, in_=PA[0:16, 0:1024]),
                    reads=[("PS", 0), ("PS", 1)], writes=["t2b"])
                for d_ in range(16):
                    wg_ = wdg[d_ % 2]
                    fw.op("s", (lambda d2, w2: lambda e: e.dma_start(
                        out=w2, in_=wdni_hbm[d2].rearrange("p o j -> p (o j)")))(
                        d_, wg_),
                        reads=([("wdni", ii) for ii in range(64)]
                               if (H == 0 and d_ == 0) else []),
                        writes=[("wdg", d_ % 2)], dma=fw.dsem(f"wdgS{d_ % 2}"))
                    slot = 2 + (d_ % 2) * 4
                    for j in range(2):
                        b2 = slot + j
                        for o in range(64):
                            first = o == 0
                            fw.op("t", (lambda o_, j_, f_, b3, w2: lambda e: e.matmul(
                                ps(b3), w2[:, o_ * 128:(o_ + 1) * 128],
                                x2_o(o_)[:, j_ * 512:(j_ + 1) * 512],
                                start=f_, stop=False))(o, j, first, b2, wg_),
                                reads=([("wdg", d_ % 2)]
                                       if (first or o == 63) else []),
                                writes=([("PS", b2)] if first else []))
                        fw.op("t", (lambda d2, j_, b3: lambda e: e.matmul(
                            ps(b3), bdnl[:, d2 * 128:(d2 + 1) * 128],
                            t2b[:, j_ * 512:(j_ + 1) * 512],
                            start=False, stop=True))(d_, j, b2),
                            reads=["t2b", "bdnl"], writes=[("PS", b2)])
                    fw.op("s", (lambda d2, s_: lambda e: e.activation(
                        out=y2ev, in_=(PA if s_ == 2 else PB)[:, 1024:2048],
                        func=AF.Identity, bias=bdn[:, d2:d2 + 1]))(d_, slot),
                        reads=[("PS", slot), ("PS", slot + 1), d_name(bdn)],
                        writes=["y2e"])
                    fw.op("q", (lambda d2, H_: lambda e: e.dma_start(
                        out=y2_hbm[d2 * 128:(d2 + 1) * 128,
                                   H_ * 1024:(H_ + 1) * 1024], in_=y2ev))(d_, H),
                        reads=["y2e"], writes=[("y2", d_, H)],
                        dma=fw.dsem(f"y2S{d_ % 2}"))
                fw.barrier()
                pool.pop()
                for tt in range(H * 8, H * 8 + 8):
                    emit_p9_tile(tt)
            pool.pop()
            pool.pop()

        fw.finalize()
        fw.emit(block, ctx)
    return nc


def d_name(t):
    return t.tensor.name if hasattr(t, "tensor") else t.name



def _host_prep(inputs):
    f = np.float32
    W = np.asarray(inputs["L0_attn_in_proj_w"], f)
    wg = np.ascontiguousarray(
        W[:4096].reshape(32, 128, 16, 128).transpose(0, 3, 2, 1)).reshape(32, 128, 2048)
    wh = np.ascontiguousarray(
        W[4096:8320].reshape(33, 128, 16, 128).transpose(0, 3, 2, 1)).reshape(33, 128, 2048)
    wd = np.ascontiguousarray(
        W[8320:8384].reshape(64, 16, 128).transpose(2, 1, 0)).reshape(128, 1024)
    Wo = np.asarray(inputs["L0_attn_out_proj_w"], f)
    wo = np.ascontiguousarray(
        Wo.reshape(16, 128, 32, 128).transpose(0, 3, 2, 1)).reshape(16, 128, 4096)
    d = {
        "wg_img": wg, "wh_img": wh, "wd_img": wd, "wo_img": wo,
        "convw_r": np.ascontiguousarray(
            np.asarray(inputs["L0_attn_conv_w"], f).reshape(33, 128, 4)
            .transpose(1, 0, 2)).reshape(128, 132),
        "convb_r": np.ascontiguousarray(
            np.asarray(inputs["L0_attn_conv_b"], f).reshape(33, 128).T),
        "dtb": np.asarray(inputs["L0_attn_dt_bias"], f).reshape(64, 1),
        "alog": np.asarray(inputs["L0_attn_A_log"], f).reshape(64, 1),
        "dcol": np.asarray(inputs["L0_attn_D"], f).reshape(64, 1),
        "nw_r": np.ascontiguousarray(
            np.asarray(inputs["L0_attn_norm_weight"], f).reshape(32, 128).T),
        "q_up": np.asarray(inputs["L0_ffn_w_up_q"], np.int32),
        "scup_r": np.ascontiguousarray(
            np.asarray(inputs["L0_ffn_w_up_scale"], f).reshape(64, 128, 32)
            .transpose(1, 0, 2)).reshape(128, 64 * 32),
        "bup_r": np.ascontiguousarray(
            np.asarray(inputs["L0_ffn_b_up"], f).reshape(64, 128).T),
        "q_dn": np.asarray(inputs["L0_ffn_w_down_q"], np.int32),
        "scdn_r": np.ascontiguousarray(
            np.asarray(inputs["L0_ffn_w_down_scale"], f).reshape(16, 128, 128)
            .transpose(1, 0, 2)).reshape(128, 16 * 128),
        "bdn_r": np.ascontiguousarray(
            np.asarray(inputs["L0_ffn_b_down"], f).reshape(16, 128).T),
        "aup_r": np.ascontiguousarray(
            np.asarray(inputs["L0_ffn_w_up_lora_a"], f).reshape(16, 128, 16)
            .transpose(1, 0, 2)).reshape(128, 256),
        "b_upl": np.asarray(inputs["L0_ffn_w_up_lora_b"], f),
        "adn_r": np.ascontiguousarray(
            np.asarray(inputs["L0_ffn_w_down_lora_a"], f).reshape(64, 128, 16)
            .transpose(1, 0, 2)).reshape(128, 1024),
        "b_dnl": np.asarray(inputs["L0_ffn_w_down_lora_b"], f),
    }
    return {k: np.ascontiguousarray(v) for k, v in d.items()}


def make_in_maps(inputs, shared=None):
    if shared is None:
        shared = _host_prep(inputs)
    x = np.ascontiguousarray(
        np.asarray(inputs["L0_attn_hidden_states"], np.float32).reshape(16384, 2048))
    in_maps = []
    for c in range(NCORES):
        m = dict(shared)
        m["x"] = np.ascontiguousarray(x[c * T:(c + 1) * T])
        m["is_odd"] = np.full((128, 1), 1.0 if c % 2 else 0.0, np.float32)
        in_maps.append(m)
    return in_maps


def kernel(**inputs):
    nc = build_program()
    in_maps = make_in_maps(inputs)
    res = run_bass_kernel_spmd(nc, in_maps, core_ids=list(range(NCORES)))
    out = np.concatenate([np.asarray(r["out"]) for r in res.results], axis=0)
    return out.reshape(4, 4096, 2048)


# version 13
# speedup vs baseline: 1.0818x; 1.0818x over previous
import numpy as np
import ml_dtypes
from contextlib import ExitStack

import concourse.bass as bass
import concourse.mybir as mybir
from concourse.bass_utils import run_bass_kernel_spmd

F32 = mybir.dt.float32
BF16 = mybir.dt.bfloat16
I32 = mybir.dt.int32
AF = mybir.ActivationFunctionType
ALU = mybir.AluOpType
AX = mybir.AxisListType
BF = ml_dtypes.bfloat16

T = 2048
D = 2048
INTER = 4096
NH = 64
HD = 64
NS = 64
CONVD = 4224
HF = 8192
NCH = 16
NCORES = 8
RG = [[0, 1], [2, 3], [4, 5], [6, 7]]

ENGS = ("t", "v", "s", "g", "q")


class FW:

    def __init__(self, nc):
        self.nc = nc
        self.ops = {E: [] for E in ENGS}
        self.writer = {}
        self.readers = {}
        self.hw = {E: {} for E in ENGS}
        self.dcnt = {}
        self.dbatch = {}

    def dsem(self, name, batch=False):
        if name not in self.dcnt:
            self.dcnt[name] = 0
            self.dbatch[name] = batch
        return name

    def dma_begin(self, name):
        self.dsem(name)
        self.pending = getattr(self, "pending", {})
        self.pending[name] = []

    def dma_end(self, name, extra_keys=()):
        tok = ("d", name, self.dcnt[name])
        for wks, rks in self.pending.pop(name):
            for k in wks:
                self.writer[k] = tok
                self.readers[k] = []
            for k in rks:
                self.readers.setdefault(k, []).append(tok)
        for k in extra_keys:
            self.writer[k] = tok
            self.readers[k] = []
        return tok

    def op(self, E, fn, reads=(), writes=(), rw=(), dma=None, group=False):
        waits = []

        def need(tok):
            if tok is None:
                return
            if tok[0] == "e":
                _, E2, seq = tok
                if seq <= self.hw[E].get(("e", E2), -1):
                    return
                self.hw[E][("e", E2)] = seq
                self.ops[E2][seq]["mark"] = True
                waits.append(tok)
            elif tok[0] == "d":
                _, nm, val = tok
                if val <= self.hw[E].get(("d", nm), 0):
                    return
                self.hw[E][("d", nm)] = val
                waits.append(tok)
            else:
                if self.hw[E].get(("dF", tok[1])):
                    return
                self.hw[E][("dF", tok[1])] = True
                waits.append(tok)

        reads = tuple(reads) + tuple(rw)
        writes = tuple(writes) + tuple(rw)
        for k in reads:
            need(self.writer.get(k))
        for k in writes:
            need(self.writer.get(k))
            for t_ in self.readers.get(k, ()):
                need(t_)
        seq = len(self.ops[E])
        self.ops[E].append({"fn": fn, "waits": waits, "mark": False, "dma": dma})
        if dma is not None:
            self.dcnt[dma] += 16
            if group:
                self.pending[dma].append((writes, reads))
                return None
            tok = ("dF", dma) if self.dbatch[dma] else ("d", dma, self.dcnt[dma])
        else:
            tok = ("e", E, seq)
        for k in reads:
            self.readers.setdefault(k, []).append(tok)
        for k in writes:
            self.writer[k] = tok
            self.readers[k] = []
        return tok

    def barrier(self):
        def qfn(eng):
            return eng.nop()
        toks = []
        for E in ENGS:
            if E == "q" or not self.ops[E]:
                continue
            seq = len(self.ops[E]) - 1
            while seq >= 0 and self.ops[E][seq]["dma"] is not None:
                seq -= 1
            if seq >= 0:
                toks.append(("e", E, seq))
        dtoks = []
        for nm, c in self.dcnt.items():
            if c > self.hw["q"].get(("d", nm), 0) or (self.dbatch[nm] and c):
                dtoks.append(("d", nm, c) if not self.dbatch[nm] else ("dF2", nm, c))
        waits = []
        for tok in toks:
            _, E2, seq = tok
            if seq <= self.hw["q"].get(("e", E2), -1):
                continue
            self.hw["q"][("e", E2)] = seq
            self.ops[E2][seq]["mark"] = True
            waits.append(tok)
        for tok in dtoks:
            if tok[0] == "dF2":
                _, nm, c = tok
                waits.append(("d", nm, c))
                self.hw["q"][("d", nm)] = c
            else:
                _, nm, c = tok
                if c <= self.hw["q"].get(("d", nm), 0):
                    continue
                self.hw["q"][("d", nm)] = c
                waits.append(tok)
        seq = len(self.ops["q"])
        self.ops["q"].append({"fn": qfn, "waits": waits, "mark": True, "dma": None})
        qtok = ("e", "q", seq)
        for E in ENGS:
            if E == "q":
                continue
            seqE = len(self.ops[E])
            self.ops[E].append({"fn": lambda eng: eng.nop(),
                               "waits": [qtok], "mark": False, "dma": None})
            self.hw[E][("e", "q")] = seq
        self.writer = {}
        self.readers = {}
        return qtok

    def emit(self, block, ctx):
        esem = {E: ctx.enter_context(self.nc.semaphore("es_" + E)) for E in ENGS}
        dsem = {nm: ctx.enter_context(self.nc.semaphore("ds_" + nm))
                for nm in self.dcnt}
        prefix = {}
        for E in ENGS:
            c = 0
            pl = []
            for o in self.ops[E]:
                if o["mark"]:
                    c += 1
                pl.append(c)
            prefix[E] = pl

        def mk(E):
            def run(eng):
                for o in self.ops[E]:
                    for tok in o["waits"]:
                        if tok[0] == "e":
                            eng.wait_ge(esem[tok[1]], prefix[tok[1]][tok[2]])
                        elif tok[0] == "d":
                            eng.wait_ge(dsem[tok[1]], tok[2])
                        else:
                            eng.wait_ge(dsem[tok[1]], self.dcnt[tok[1]])
                    inst = o["fn"](eng)
                    if o["dma"] is not None:
                        inst.then_inc(dsem[o["dma"]], 16)
                    elif o["mark"]:
                        inst.then_inc(esem[E], 1)
            return run

        block.tensor(mk("t"))
        block.vector(mk("v"))
        block.scalar(mk("s"))
        block.gpsimd(mk("g"))
        block.sync(mk("q"))

    def finalize(self, extra_sems=()):
        waits = []
        for nm, c in self.dcnt.items():
            if c > self.hw["q"].get(("d", nm), 0):
                waits.append(("d", nm, c))
                self.hw["q"][("d", nm)] = c
        self.ops["q"].append({"fn": lambda eng: eng.nop(), "waits": waits,
                              "mark": False, "dma": None})


class Pool:

    def __init__(self, base_f32):
        self.base = base_f32
        self.n = base_f32.shape[1]
        self.off = 0
        self.stack = []

    def push(self):
        self.stack.append(self.off)

    def pop(self):
        self.off = self.stack.pop()

    def f32(self, words, parts=128):
        o = self.off
        self.off = o + ((words + 15) // 16) * 16
        assert self.off <= self.n, f"pool overflow {self.off} > {self.n}"
        return self.base[0:parts, o:o + words]

    def bf16(self, cols, parts=128):
        w = (cols + 1) // 2
        return self.f32(w, parts).bitcast(BF16)


def build_program(phases=99, dumps=()):
    nc = bass.Bass("TRN2", target_bir_lowering=False, debug=False)
    dumps = set(dumps)

    def dram(name, shape, dtype):
        kind = "ExternalOutput" if name in dumps else "Internal"
        return nc.dram_tensor(name, shape, dtype, kind=kind)

    EI = "ExternalInput"
    x_in = nc.dram_tensor("x", [T, D], F32, kind=EI).ap()
    wg_img = nc.dram_tensor("wg_img", [32, 128, 2048], F32, kind=EI).ap()
    wh_img = nc.dram_tensor("wh_img", [33, 128, 2048], F32, kind=EI).ap()
    wd_img = nc.dram_tensor("wd_img", [128, 1024], F32, kind=EI).ap()
    wo_img = nc.dram_tensor("wo_img", [16, 128, 4096], F32, kind=EI).ap()
    convw_in = nc.dram_tensor("convw_r", [128, 132], F32, kind=EI).ap()
    convb_in = nc.dram_tensor("convb_r", [128, 33], F32, kind=EI).ap()
    dtb_in = nc.dram_tensor("dtb", [64, 1], F32, kind=EI).ap()
    alog_in = nc.dram_tensor("alog", [64, 1], F32, kind=EI).ap()
    dcol_in = nc.dram_tensor("dcol", [64, 1], F32, kind=EI).ap()
    nw_in = nc.dram_tensor("nw_r", [128, 32], F32, kind=EI).ap()
    qup_in = nc.dram_tensor("q_up", [HF, D], I32, kind=EI).ap()
    scup_in = nc.dram_tensor("scup_r", [128, 64 * 32], F32, kind=EI).ap()
    bup_in = nc.dram_tensor("bup_r", [128, 64], F32, kind=EI).ap()
    qdn_in = nc.dram_tensor("q_dn", [D, HF], I32, kind=EI).ap()
    scdn_in = nc.dram_tensor("scdn_r", [128, 16 * 128], F32, kind=EI).ap()
    bdn_in = nc.dram_tensor("bdn_r", [128, 16], F32, kind=EI).ap()
    aup_in = nc.dram_tensor("aup_r", [128, 256], F32, kind=EI).ap()
    bupl_in = nc.dram_tensor("b_upl", [16, HF], F32, kind=EI).ap()
    adn_in = nc.dram_tensor("adn_r", [128, 1024], F32, kind=EI).ap()
    bdnl_in = nc.dram_tensor("b_dnl", [16, D], F32, kind=EI).ap()
    isodd_in = nc.dram_tensor("is_odd", [128, 1], F32, kind=EI).ap()

    identb_c = nc.inline_tensor(np.eye(128, dtype=BF), "identbc").ap()
    identf_c = nc.inline_tensor(np.eye(128, dtype=np.float32), "identfc").ap()
    e_all_np = np.zeros((64, 64 * 128), np.float32)
    for h in range(64):
        e_all_np[h, h * 128:(h + 1) * 128] = 1.0
    eall_c = nc.inline_tensor(e_all_np, "eallc").ap()
    mask_np = np.triu(np.ones((128, 128), np.float32))
    mask_c = nc.inline_tensor(mask_np, "maskc").ap()
    ones1f_c = nc.inline_tensor(np.ones((1, 128), np.float32), "ones1fc").ap()

    xfm_hbm = dram("xfm_hbm", [CONVD, T], BF16).ap()
    s_hbm = dram("s_hbm", [NCH, 64, NH * HD], F32).ap()
    ex1_in = nc.dram_tensor("ex1_in", [128, 99], F32, kind="Internal").ap()
    ex1_out = nc.dram_tensor("ex1_out", [256, 99], F32, kind="Internal").ap()
    ex2_in = nc.dram_tensor("ex2_in", [64, 4096], F32, kind="Internal").ap()
    ex2_out = nc.dram_tensor("ex2_out", [128, 4096], F32, kind="Internal").ap()
    yn_hbm = dram("yn_hbm", [T, INTER], BF16).ap()
    l1x_hbm = dram("l1x_hbm", [T, D], BF16).ap()
    yg_hbm = dram("yg_hbm", [INTER, T], BF16).ap()
    attn_hbm = dram("attn_hbm", [D, T], F32).ap()
    x4_hbm = dram("x4_hbm", [T, D], F32).ap()
    l3x_hbm = dram("l3x_hbm", [T, D], BF16).ap()
    wupn_hbm = dram("wupn_hbm", [HF, D], BF16).ap()
    wupi_hbm = dram("wupi_hbm", [64, 128, 2048], BF16).ap()
    wdnn_hbm = dram("wdnn_hbm", [D, HF], BF16).ap()
    wdni_hbm = dram("wdni_hbm", [16, 128, 64, 128], BF16).ap()
    y2_hbm = dram("y2_hbm", [D, T], F32).ap()
    out_hbm = nc.dram_tensor("out", [T, D], F32, kind="ExternalOutput").ap()

    fw = FW(nc)

    with ExitStack() as ctx:
        def sb(name, shape, dtype):
            return ctx.enter_context(nc.sbuf_tensor(name, shape, dtype))

        convw = sb("convw", [128, 132], F32)
        convb = sb("convb", [128, 33], F32)
        dtb = sb("dtbs", [64, 1], F32)
        acol = sb("acol", [64, 1], F32)
        dbc = sb("dbc", [128, 64], F32)
        nw = sb("nws", [128, 32], F32)
        isodd = sb("isodds", [128, 1], F32)
        identb = sb("identbs", [128, 128], BF16)
        identf = sb("identfs", [128, 128], F32)
        maskc = sb("maskcs", [128, 128], F32)
        ones1f = sb("ones1fs", [1, 128], F32)
        zeros64 = sb("zeros64", [64, 128], F32)
        bup = sb("bups", [128, 64], F32)
        bdn = sb("bdns", [128, 16], F32)
        aupb = sb("aupb", [128, 256], BF16)
        adnb = sb("adnb", [128, 1024], BF16)
        stat = sb("stat", [128, 128], F32)
        small = sb("small", [128, 16], F32)
        rowb = sb("rowb", [1, 64], F32)
        fixb = sb("fixb", [128, 12], F32)

        POOLW = 50176
        poolt = sb("poolt", [128, POOLW], F32)
        pool = Pool(poolt)

        pool.push()
        ldt = pool.f32(T, parts=64)
        cuma = pool.f32(T, parts=64)
        b_fm = pool.bf16(T, parts=64)
        c_fm = pool.bf16(T, parts=64)
        fixtok = pool.bf16(CONVD, parts=3)
        sendtail = pool.f32(99)
        firstc = pool.f32(99)
        fixfb = pool.bf16(99)

        PA = ctx.enter_context(nc.psum_tensor("PA", [128, 2048], F32))
        PB = ctx.enter_context(nc.psum_tensor("PB", [128, 2048], F32))

        def ps(b, lo=0, hi=512, parts=128):
            tns = PA if b < 4 else PB
            b = b % 4
            return tns[0:parts, b * 512 + lo:b * 512 + hi]

        def pa(b, lo=0, hi=512, parts=128):
            return ps(b, lo, hi, parts)

        def pb(b, lo=0, hi=512, parts=128):
            return ps(b + 4, lo, hi, parts)

        block = ctx.enter_context(nc.Block())

        cst = fw.dsem("cst", batch=True)
        cstg = fw.dsem("cstg", batch=True)
        for dst, src in [(convw, convw_in), (convb, convb_in), (dtb, dtb_in),
                         (nw, nw_in), (isodd, isodd_in), (identb, identb_c),
                         (identf, identf_c), (maskc, mask_c), (ones1f, ones1f_c),
                         (bup, bup_in), (bdn, bdn_in)]:
            fw.op("q", (lambda d, s_: lambda e: e.dma_start(out=d[:], in_=s_))(dst, src),
                  writes=[d_name(dst)], dma=cst)
        for dst, src in [(aupb, aup_in), (adnb, adn_in)]:
            fw.op("g", (lambda d, s_: lambda e: e.dma_start(out=d[:], in_=s_))(dst, src),
                  writes=[d_name(dst)], dma=cstg)
        fw.op("g", lambda e: e.dma_start(out=small[0:64, 0:1], in_=alog_in),
              writes=["alog"], dma=cstg)
        fw.op("g", lambda e: e.dma_start(out=small[0:64, 1:2], in_=dcol_in),
              writes=["dcolv"], dma=cstg)
        fw.op("v", lambda e: e.memset(zeros64[:], 0.0), writes=["zeros64"])
        fw.op("s", lambda e: e.activation(out=acol[:], in_=small[0:64, 0:1],
                                          func=AF.Exp),
              reads=["alog"], writes=["acol0"])
        fw.op("v", lambda e: e.tensor_scalar_mul(acol[:], acol[:], -1.0),
              reads=["acol0"], writes=["acol"])
        fw.op("t", lambda e: e.transpose(pa(7, 0, 64, parts=1), small[0:64, 1:2],
                                         identf[0:64, 0:64]),
              reads=["dcolv", d_name(identf)], writes=[("PS", 7)])
        fw.op("v", lambda e: e.tensor_copy(out=rowb[:], in_=pa(7, 0, 64, parts=1)),
              reads=[("PS", 7)], writes=["rowb"])
        fw.op("t", lambda e: e.matmul(pa(6, 0, 64), ones1f[:], rowb[:],
                                      start=True, stop=True),
              reads=["rowb", d_name(ones1f)], writes=[("PS", 6)])
        fw.op("v", lambda e: e.tensor_copy(out=dbc[:], in_=pa(6, 0, 64)),
              reads=[("PS", 6)], writes=["dbc"])

        pool.push()
        ln1fm = pool.bf16(16 * 2048)
        xtile = [pool.f32(2048) for _ in range(2)]
        lntok = [pool.bf16(2048) for _ in range(2)]
        hbc = pool.f32(T + 3)
        cacc = pool.f32(T)
        xbf = [pool.bf16(T) for _ in range(2)]
        wbuf = [pool.bf16(2048) for _ in range(3)]
        dtraw = pool.f32(T, parts=64)
        dtmp = cacc[0:64, :]
        qbuf = [pool.bf16(2048) for _ in range(2)]
        wnat = [pool.bf16(2048) for _ in range(2)]
        scup = pool.f32(64 * 32)

        fw.op("q", lambda e: e.dma_start(out=scup, in_=scup_in),
              writes=["scup"], dma=cst)
        fw.op("v", lambda e: e.memset(hbc[:, 0:3], 0.0), writes=["hbc03"])

        def ln1fm_k(k):
            return ln1fm[:, k * 2048:(k + 1) * 2048]

        def emit_lnstats(src_key, src_ap, eps, c0):
            skeys = src_key if isinstance(src_key, list) else [src_key]
            for gidx in range(4):
                fw.op("v", (lambda g_: lambda e: e.bn_stats(
                    stat[:, c0 + 4 + g_ * 6:c0 + 10 + g_ * 6],
                    src_ap[:, g_ * 512:(g_ + 1) * 512]))(gidx),
                    reads=skeys, writes=[("st", c0, gidx)])
            fw.op("v", lambda e: e.bn_aggr(stat[:, c0 + 2:c0 + 4],
                                           stat[:, c0 + 4:c0 + 28]),
                  reads=[("st", c0, g) for g in range(4)], writes=[("mv", c0)])
            fw.op("v", (lambda: lambda e: e.tensor_scalar(
                out=stat[:, c0 + 28:c0 + 29], in0=stat[:, c0 + 3:c0 + 4],
                scalar1=1.0, scalar2=eps, op0=ALU.mult, op1=ALU.add))(),
                reads=[("mv", c0)], writes=[("ve", c0)])
            fw.op("v", lambda e: e.reciprocal(stat[:, c0 + 29:c0 + 30],
                                              stat[:, c0 + 28:c0 + 29]),
                  reads=[("ve", c0)], writes=[("rc", c0)])
            fw.op("s", lambda e: e.activation(out=stat[:, c0:c0 + 1],
                                              in_=stat[:, c0 + 29:c0 + 30],
                                              func=AF.Sqrt),
                  reads=[("rc", c0)], writes=[("rs", c0)])
            fw.op("v", lambda e: e.tensor_scalar(
                out=stat[:, c0 + 1:c0 + 2], in0=stat[:, c0 + 2:c0 + 3],
                scalar1=stat[:, c0:c0 + 1], scalar2=-1.0,
                op0=ALU.mult, op1=ALU.mult),
                reads=[("rs", c0), ("mv", c0)], writes=[("nb", c0)])

        if phases >= 1:
            for tt in range(16):
                xt = xtile[tt % 2]
                xk = ("xt", tt % 2)
                fw.op("q", (lambda tt_, xt_: lambda e: e.dma_start(
                    out=xt_, in_=x_in[tt_ * 128:(tt_ + 1) * 128, :]))(tt, xt),
                    writes=[xk], dma=fw.dsem(f"xt{tt % 2}"))
                emit_lnstats(xk, xt, 1e-5, 0)
                lt = lntok[tt % 2]
                lk = ("lt", tt % 2)
                fw.op("s", (lambda xt_, lt_: lambda e: e.activation(
                    out=lt_, in_=xt_, func=AF.Identity,
                    bias=stat[:, 1:2], scale=stat[:, 0:1]))(xt, lt),
                    reads=[xk, ("rs", 0), ("nb", 0)], writes=[lk])
                fw.op("q", (lambda tt_, lt_: lambda e: e.dma_start(
                    out=l1x_hbm[tt_ * 128:(tt_ + 1) * 128, :], in_=lt_))(tt, lt),
                    reads=[lk], writes=[("l1x", tt)], dma=fw.dsem(f"l1xS{tt % 2}"))
                for k in range(16):
                    bnk = k % 2
                    slot = (k // 2) % 4
                    fw.op("t", (lambda lt_, k_, b_, s_: lambda e: e.transpose(
                        ps(b_).bitcast(BF16)[:, s_ * 128:(s_ + 1) * 128],
                        lt_[:, k_ * 128:(k_ + 1) * 128], identb[:]))(lt, k, bnk, slot),
                        reads=[lk, d_name(identb)], writes=[("PS", bnk)])
                    fw.op("v", (lambda tt_, k_, b_, s_: lambda e: e.tensor_copy(
                        out=ln1fm_k(k_)[:, tt_ * 128:(tt_ + 1) * 128],
                        in_=ps(b_).bitcast(BF16)[:, s_ * 128:(s_ + 1) * 128]))(tt, k, bnk, slot),
                        reads=[("PS", bnk)], writes=[("ln1fm", k, tt)])

        def emit_qup_chunk(o):
            qb = qbuf[o % 2]
            wn = wnat[o % 2]
            fw.op("g", (lambda o_, qb_: lambda e: e.dma_start(
                out=qb_, in_=qup_in[o_ * 128:(o_ + 1) * 128, :]))(o, qb),
                writes=[("qb", o % 2)], dma=fw.dsem(f"qup{o % 2}"))
            fw.op("v", (lambda o_, qb_, wn_: lambda e: e.scalar_tensor_tensor(
                out=wn_.rearrange("p (b k) -> p b k", b=32),
                in0=qb_.rearrange("p (b k) -> p b k", b=32),
                scalar=-7.5,
                in1=scup[:, o_ * 32:(o_ + 1) * 32].unsqueeze(2).broadcast_to([128, 32, 64]),
                op0=ALU.add, op1=ALU.mult))(o, qb, wn),
                reads=[("qb", o % 2), "scup"], writes=[("wn", o % 2)])
            fw.op("q", (lambda o_, wn_: lambda e: e.dma_start(
                out=wupn_hbm[o_ * 128:(o_ + 1) * 128, :], in_=wn_))(o, wn),
                reads=[("wn", o % 2)], writes=[("wupn", o)],
                dma=fw.dsem(f"wupnS{o % 2}"))

        def a1_mms(o, img_src, psum_sel, ofree=128):
            wb = wbuf[o % 3]
            wbk = ("wb", o % 3)
            fw.op("g", (lambda s_, wb_, n_: lambda e: e.dma_start(
                out=wb_[:, 0:16 * n_], in_=s_))(img_src, wb, ofree),
                writes=[wbk], dma=fw.dsem(f"wb{o % 3}"))
            Pt = psum_sel
            for tt in range(4):
                for k in range(16):
                    first, last = k == 0, k == 15
                    rd = []
                    if first:
                        rd = [wbk] + [("ln1fm", kk, t4) for kk in range(16)
                                      for t4 in range(tt * 4, tt * 4 + 4)]
                    elif last:
                        rd = [wbk]
                    fw.op("t", (lambda wb_, k_, tt_, f_, l_, p_, n_: lambda e: e.matmul(
                        (pa if p_ == 0 else pb)(tt_, parts=n_),
                        wb_[:, k_ * n_:(k_ + 1) * n_],
                        ln1fm_k(k_)[:, tt_ * 512:(tt_ + 1) * 512],
                        start=f_, stop=l_))(wb, k, tt, first, last, Pt, ofree),
                        reads=rd,
                        writes=([("PS", Pt * 4 + tt)] if (first or last) else []))

        if phases >= 2:
            qo = 0
            for o in range(34):
                Pt = o % 2
                Pn = "PA" if Pt == 0 else "PB"
                if o < 33:
                    a1_mms(o, wh_img[o], Pt)
                else:
                    a1_mms(o, wd_img, Pt, ofree=64)
                for _ in range(2):
                    if qo < 64:
                        emit_qup_chunk(qo)
                        qo += 1
                if o < 33:
                    fw.op("v", (lambda p_: lambda e: e.tensor_copy(
                        out=hbc[:, 3:3 + T], in_=(PA if p_ == 0 else PB)[:, 0:T]))(Pt),
                        reads=[("PS", Pt * 4 + tt) for tt in range(4)],
                        writes=["hbcw"])
                    fw.op("v", (lambda o_: lambda e: e.tensor_copy(
                        out=sendtail[:, o_ * 3:(o_ + 1) * 3], in_=hbc[:, 2048:2051]))(o),
                        reads=["hbcw"], writes=[("stl", o)])
                    fw.op("v", (lambda o_: lambda e: e.tensor_copy(
                        out=firstc[:, o_ * 3:(o_ + 1) * 3], in_=hbc[:, 3:6]))(o),
                        reads=["hbcw"], writes=[("fst", o)])
                    fw.op("s", (lambda o_: lambda e: e.activation(
                        out=cacc, in_=hbc[:, 0:T], func=AF.Identity,
                        scale=convw[:, o_ * 4:o_ * 4 + 1],
                        bias=convb[:, o_:o_ + 1]))(o),
                        reads=["hbcw", "hbc03", d_name(convw), d_name(convb)],
                        writes=["cacc"])
                    for kk in range(1, 4):
                        fw.op("v", (lambda o_, kk_: lambda e: e.scalar_tensor_tensor(
                            out=cacc, in0=hbc[:, kk_:T + kk_],
                            scalar=convw[:, o_ * 4 + kk_:o_ * 4 + kk_ + 1],
                            in1=cacc, op0=ALU.mult, op1=ALU.add))(o, kk),
                            reads=["hbcw"], rw=["cacc"])
                    xb = xbf[o % 2]
                    fw.op("s", (lambda xb_: lambda e: e.activation(
                        out=xb_, in_=cacc, func=AF.Silu))(xb),
                        reads=["cacc"], writes=[("xbf", o % 2)])
                    fw.op("q", (lambda o_, xb_: lambda e: e.dma_start(
                        out=xfm_hbm[o_ * 128:(o_ + 1) * 128, :], in_=xb_))(o, xb),
                        reads=[("xbf", o % 2)], writes=[("xfm", o)],
                        dma=fw.dsem(f"xfmS{o % 2}"))
                else:
                    fw.op("v", (lambda p_: lambda e: e.tensor_copy(
                        out=dtraw, in_=(PA if p_ == 0 else PB)[0:64, 0:T]))(Pt),
                        reads=[("PS", Pt * 4 + tt) for tt in range(4)], writes=["dtraw"])
            fw.op("q", lambda e: e.dma_start(out=b_fm[:], in_=xfm_hbm[4096:4160, :]),
                  reads=[("xfm", 32)], writes=["bfm"], dma=fw.dsem("bS"))
            fw.op("q", lambda e: e.dma_start(out=c_fm[:], in_=xfm_hbm[4160:4224, :]),
                  reads=[("xfm", 32)], writes=["cfm"], dma=fw.dsem("cS"))
            dtmp2 = hbc[0:64, 0:2048]
            fw.op("v", lambda e: e.tensor_scalar(
                out=dtraw, in0=dtraw, scalar1=dtb[:, 0:1], scalar2=None,
                op0=ALU.add),
                reads=[d_name(dtb)], rw=["dtraw"])
            fw.op("v", lambda e: e.tensor_scalar(
                out=dtmp, in0=dtraw, scalar1=30.0, scalar2=None, op0=ALU.min),
                reads=["dtraw"], writes=["dtc", "cacc"])
            fw.op("s", lambda e: e.activation(out=dtmp, in_=dtmp, func=AF.Exp),
                  rw=["dtc", "cacc"], writes=["dte"])
            fw.op("s", lambda e: e.activation(out=dtmp, in_=dtmp, func=AF.Ln,
                                              bias=1.0),
                  rw=["dte", "cacc"], writes=["dtp0"])
            fw.op("v", lambda e: e.tensor_scalar(
                out=dtmp2, in0=dtraw, scalar1=-30.0, scalar2=0.0,
                op0=ALU.add, op1=ALU.max),
                reads=["dtraw"], writes=["dtr30", "hbcw", "hbc03"])
            fw.op("v", lambda e: e.tensor_add(dtmp, dtmp, dtmp2),
                  reads=["dtr30", "dtp0"], rw=["dte", "cacc"], writes=["dtp"])
            fw.op("v", lambda e: e.tensor_scalar(
                out=dtraw, in0=dtmp, scalar1=acol[:, 0:1], scalar2=None,
                op0=ALU.mult),
                reads=["dtp", "acol"], writes=["a_fm"])
            fw.op("v", lambda e: e.tensor_scalar(
                out=dtmp, in0=dtmp, scalar1=1e-30, scalar2=None, op0=ALU.max),
                rw=["dtp", "cacc"], writes=["dtpc"])
            fw.op("s", lambda e: e.activation(out=ldt[:], in_=dtmp, func=AF.Ln),
                  reads=["dtpc"], writes=["ldt"])
            for q in range(NCH):
                fw.op("v", (lambda q_: lambda e: e.tensor_tensor_scan(
                    out=cuma[:, q_ * 128:(q_ + 1) * 128],
                    data0=dtraw[:, q_ * 128:(q_ + 1) * 128],
                    data1=zeros64[:], initial=0.0,
                    op0=ALU.add, op1=ALU.add))(q),
                    reads=["a_fm", "zeros64"], writes=[("cuma", q)])

        if phases >= 3:
            fw.op("q", lambda e: e.dma_start(out=ex1_in[0:128, 0:99], in_=sendtail[:]),
                  reads=[("stl", o) for o in range(33)], writes=["ex1i"],
                  dma=fw.dsem("ex1S"))
            fw.op("g", lambda e: e.collective_compute(
                "AllGather", ALU.bypass, replica_groups=RG,
                ins=[ex1_in], outs=[ex1_out]),
                reads=["ex1i"], writes=["ex1o"])
            fw.op("q", lambda e: e.dma_start(out=sendtail[:], in_=ex1_out[0:128, 0:99]),
                  reads=["ex1o"],
                  writes=["rtail"] + [("stl", o) for o in range(33)], dma="ex1S")
            fw.op("v", lambda e: e.tensor_scalar(
                out=sendtail[:], in0=sendtail[:], scalar1=isodd[:, 0:1],
                scalar2=None, op0=ALU.mult),
                reads=[d_name(isodd)], rw=["rtail"])
            pool.push()
            bufall = pool.f32(198)
            tmpb = pool.f32(99)
            facc = pool.f32(99)
            bav = bufall.rearrange("p (o t) -> p o t", o=33)
            cwv = convw.rearrange("p (o k) -> p o k", k=4)
            fw.op("v", lambda e: e.tensor_copy(
                out=bav[:, :, 0:3], in_=sendtail.rearrange(
                    "p (o t) -> p o t", o=33)),
                reads=["rtail"], writes=["bufA"])
            fw.op("v", lambda e: e.tensor_copy(
                out=bav[:, :, 3:6], in_=firstc.rearrange(
                    "p (o t) -> p o t", o=33)),
                reads=[("fst", o) for o in range(33)], writes=["bufB"])
            fv = facc.rearrange("p (o t) -> p o t", o=33)
            tv = tmpb.rearrange("p (o t) -> p o t", o=33)
            fw.op("v", lambda e: e.tensor_mul(
                fv, bav[:, :, 0:3],
                cwv[:, :, 0].unsqueeze(2).broadcast_to([128, 33, 3])),
                reads=["bufA", d_name(convw)], writes=["facc"])
            fw.op("v", lambda e: e.tensor_add(
                fv, fv, convb[:].unsqueeze(2).broadcast_to([128, 33, 3])),
                reads=[d_name(convb)], rw=["facc"])
            for kk in range(1, 4):
                fw.op("v", (lambda kk_: lambda e: e.tensor_mul(
                    tv, bufall.rearrange("p (o t) -> p o t", o=33,
                                         )[:, :, kk_:kk_ + 3],
                    cwv[:, :, kk_].unsqueeze(2).broadcast_to([128, 33, 3])))(kk),
                    reads=["bufA", "bufB", d_name(convw)], writes=["tmpb"])
                fw.op("v", lambda e: e.tensor_add(fv, fv, tv),
                      reads=["tmpb"], rw=["facc"])
            fw.op("s", lambda e: e.activation(
                out=fixfb[:, 0:99], in_=facc, func=AF.Silu),
                reads=["facc"], writes=["fixf"] + [("fixf", o) for o in range(33)])
            for o in range(33):
                r, bnk = o // 16, 4 + (o // 8) % 2
                fw.op("t", (lambda o_, b_: lambda e: e.transpose(
                    ps(b_).bitcast(BF16)[0:3, (o_ % 8) * 128:(o_ % 8) * 128 + 128],
                    fixfb[:, o_ * 3:(o_ + 1) * 3], identb[:]))(o, bnk),
                    reads=["fixf"]
                    + ([("ftkE", bnk, o // 16 - 1)] if o >= 16 else []),
                    writes=[("PSf", bnk, o // 16)])
                if o % 8 == 7 or o == 32:
                    g8 = o // 8
                    lo_ = (g8 % 2) * 8 * 128
                    n_ = (o % 8 + 1) * 128
                    fw.op("v", (lambda o_, b_, g8_, n2: lambda e: e.tensor_copy(
                        out=fixtok[0:3, (g8_ * 8) * 128:(g8_ * 8) * 128 + n2],
                        in_=ps(b_).bitcast(BF16)[0:3, 0:n2]))(o, bnk, g8, n_),
                        reads=[("PSf", bnk, o // 16)],
                        writes=[("ftkE", bnk, o // 16)]
                        + [("ftk", oo) for oo in range(g8 * 8, o + 1)])
            pool.pop()
            fw.op("v", lambda e: e.tensor_copy(out=b_fm[:, 0:3],
                                               in_=fixfb[0:64, 96:99]),
                  reads=[("fixf", 32)], rw=["bfm"])
            fw.op("t", lambda e: e.transpose(
                pa(5).bitcast(BF16)[0:64, 0:3],
                fixtok[0:3, 4160:4224], identb[0:3, 0:3]),
                reads=[("ftk", 32), d_name(identb)], writes=[("PS", 5)])
            fw.op("v", lambda e: e.tensor_copy(out=c_fm[:, 0:3],
                                               in_=pa(5).bitcast(BF16)[0:64, 0:3]),
                  reads=[("PS", 5)], rw=["cfm"])

        dsd = fw.dsem("dmpS", batch=True)

        def dump(nm, src_ap, rkeys):
            if nm not in dumps:
                return
            t_ = nc.dram_tensor("dmp_" + nm, list(src_ap.shape),
                                src_ap.dtype, kind="ExternalOutput").ap()
            fw.op("q", (lambda a_, t2: lambda e: e.dma_start(out=t2, in_=a_))(
                src_ap, t_), reads=rkeys, dma=dsd)

        if phases >= 2:
            dump("ln1fm", ln1fm[:], [("ln1fm", k, tt) for k in range(16)
                                     for tt in range(16)])
            dump("cuma", cuma[:], [("cuma", q) for q in range(NCH)])
            dump("ldt", ldt[:], ["ldt"])
            dump("a_fm", dtraw[:], ["a_fm"])
        if phases >= 3:
            dump("bfm", b_fm[:], ["bfm"])
            dump("cfm", c_fm[:], ["cfm"])
            dump("fixtok", fixtok[:], [("ftk", o) for o in range(33)])

        fw.barrier()
        pool.pop()

        if phases >= 4:
            pool.push()
            xtok = [pool.bf16(CONVD) for _ in range(2)]
            xwb = [pool.bf16(4096) for _ in range(2)]
            sstate = pool.f32(4096, parts=64)
            btokb = [pool.bf16(64) for _ in range(2)]
            wexpf = pool.f32(128, parts=64)
            wexpt = pool.f32(64)
            dacbb = [pool.f32(64, parts=64) for _ in range(2)]
            qd2 = [pool.bf16(2048) for _ in range(2)]
            wdn2 = [pool.bf16(2048) for _ in range(2)]
            scdn = pool.f32(2048)
            wtrb6 = [pool.bf16(8192) for _ in range(2)]

            def emit_wupT(k):
                wt = wtrb6[k % 2]
                tg = f"wtrG{k % 2}"
                fw.dma_begin(tg)
                for q2 in range(4):
                    fw.op("s", (lambda k_, q3, w_: lambda e: e.dma_start_transpose(
                        w_[:, q3 * 2048:(q3 + 1) * 2048],
                        wupn_hbm[q3 * 2048:(q3 + 1) * 2048,
                                 k_ * 128:(k_ + 1) * 128]))(k, q2, wt),
                        reads=([("wupn", oo) for oo in range(64)]
                               if (k == 0 and q2 == 0) else [])
                        + ([("wupiq", k - 2, q2)] if k >= 2 else []),
                        writes=[("wtrp", k % 2, q2)], dma=tg, group=True)
                fw.dma_end(tg, extra_keys=[("wtr", k % 2)])
                gs = f"wupiG{k % 2}"
                fw.dma_begin(gs)
                for qq in range(4):
                    dst = wupi_hbm.rearrange("o p j -> p o j")[
                        :, qq * 16:(qq + 1) * 16, k * 128:(k + 1) * 128]
                    fw.op("g", (lambda d_, k_, q2, w_: lambda e: e.dma_start(
                        out=d_, in_=w_[:, q2 * 2048:(q2 + 1) * 2048]
                        .rearrange("p (o j) -> p o j", o=16),
                        allow_slow_non_contiguous=True))(dst, k, qq, wt),
                        reads=[("wtr", k % 2)], writes=[("wupiq", k, qq)],
                        dma=gs, group=True)
                fw.dma_end(gs, extra_keys=[("wupik", k)])

            fw.op("q", lambda e: e.dma_start(out=scdn, in_=scdn_in),
                  writes=["scdn"], dma=fw.dsem("scdnS"))
            fw.op("v", lambda e: e.memset(sstate, 0.0),
                  writes=[("sst", w2) for w2 in range(4)])

            qdi = 0

            def emit_qdn_piece(idx):
                o, cq = idx // 4, idx % 4
                qb = qd2[idx % 2]
                wn = wdn2[idx % 2]
                fw.op("g", (lambda o_, cq_, qb_: lambda e: e.dma_start(
                    out=qb_, in_=qdn_in[o_ * 128:(o_ + 1) * 128,
                                        cq_ * 2048:(cq_ + 1) * 2048]))(o, cq, qb),
                    writes=[("qd", idx % 2)], dma=fw.dsem(f"qdn{idx % 2}"))
                fw.op("v", (lambda o_, cq_, qb_, wn_: lambda e: e.scalar_tensor_tensor(
                    out=wn_.rearrange("p (b k) -> p b k", b=32),
                    in0=qb_.rearrange("p (b k) -> p b k", b=32),
                    scalar=-7.5,
                    in1=scdn[:, o_ * 128 + cq_ * 32:o_ * 128 + (cq_ + 1) * 32]
                        .unsqueeze(2).broadcast_to([128, 32, 64]),
                    op0=ALU.add, op1=ALU.mult))(o, cq, qb, wn),
                    reads=[("qd", idx % 2), "scdn"], writes=[("wdn", idx % 2)])
                fw.op("q", (lambda o_, cq_, wn_: lambda e: e.dma_start(
                    out=wdnn_hbm[o_ * 128:(o_ + 1) * 128,
                                 cq_ * 2048:(cq_ + 1) * 2048], in_=wn_))(o, cq, wn),
                    reads=[("wdn", idx % 2)], writes=[("wdnn", idx)],
                    dma=fw.dsem(f"wdnnS{idx % 2}"))

            XKEYS = [("xfm", o) for o in range(33)]

            def emit_prep4(q):
                xi = q % 2
                fw.op("v", (lambda q_: lambda e: e.tensor_sub(
                    wexpf, ldt[:, q_ * 128:(q_ + 1) * 128],
                    cuma[:, q_ * 128:(q_ + 1) * 128]))(q),
                    reads=["ldt", ("cuma", q)], writes=["wef"])
                fw.op("v", (lambda q_: lambda e: e.tensor_scalar(
                    out=wexpf, in0=wexpf,
                    scalar1=cuma[:, q_ * 128 + 127:q_ * 128 + 128],
                    scalar2=None, op0=ALU.add))(q),
                    reads=[("cuma", q)], rw=["wef"])
                fw.op("s", lambda e: e.activation(out=wexpf, in_=wexpf, func=AF.Exp),
                      rw=["wef"])
                tg = f"xtkG{xi}"
                fw.dma_begin(tg)
                for r2 in range(2):
                    fw.op("s", (lambda q_, xi_, r3: lambda e: e.dma_start_transpose(
                        xtok[xi_][:, r3 * 2112:(r3 + 1) * 2112],
                        xfm_hbm[r3 * 2112:(r3 + 1) * 2112,
                                q_ * 128:(q_ + 1) * 128]))(q, xi, r2),
                        reads=(XKEYS if (q == 0 and r2 == 0) else [])
                        + ([("xwm", xi)] if q >= 2 else []),
                        writes=[("xtkp", xi, r2)], dma=tg, group=True)
                fw.dma_end(tg, extra_keys=[("xtok", xi)])
                if q == 0:
                    fw.op("v", (lambda xi_: lambda e: e.tensor_copy(
                        out=xtok[xi_][0:3, :], in_=fixtok[0:3, :]))(xi),
                        rw=[("xtok", xi)])
                fw.op("t", lambda e: e.transpose(ps(6, 0, 64), wexpf,
                                                 identf[0:64, 0:64]),
                      reads=["wef", d_name(identf)], writes=[("PS", 6)])
                fw.op("v", lambda e: e.tensor_copy(out=wexpt, in_=ps(6, 0, 64)),
                      reads=[("PS", 6)], writes=["wet"])
                fw.op("v", (lambda xi_: lambda e: e.tensor_mul(
                    xwb[xi_].rearrange("p (b k) -> p b k", b=64),
                    xtok[xi_][:, 0:4096].rearrange("p (b k) -> p b k", b=64),
                    wexpt.unsqueeze(2).broadcast_to([128, 64, 64])))(xi),
                    reads=[("xtok", xi), "wet"], writes=[("xw", xi), ("xwm", xi)])
                fw.op("t", (lambda q_: lambda e: e.transpose(
                    ps(7, 0, 32).bitcast(BF16)[:, 0:64],
                    b_fm[:, q_ * 128:(q_ + 1) * 128], identb[0:64, 0:64]))(q),
                    reads=["bfm", d_name(identb)], writes=[("PS", 7)])
                fw.op("v", (lambda xi_: lambda e: e.tensor_copy(
                    out=btokb[xi_], in_=ps(7, 0, 32).bitcast(BF16)[:, 0:64]))(xi),
                    reads=[("PS", 7)], writes=[("btk", xi)])
                fw.op("t", (lambda q_: lambda e: e.transpose(
                    ps(6, 64, 128, parts=1),
                    cuma[:, q_ * 128 + 127:q_ * 128 + 128],
                    identf[0:64, 0:64]))(q),
                    reads=[("cuma", q), d_name(identf)], writes=[("PS", 6)])
                fw.op("v", lambda e: e.tensor_copy(out=rowb[0:1, 0:64],
                                                   in_=ps(6, 64, 128, parts=1)),
                      reads=[("PS", 6)], writes=["rowb"])
                fw.op("t", lambda e: e.matmul(ps(6, 128, 192, parts=64),
                                              ones1f[0:1, 0:64], rowb[0:1, 0:64],
                                              start=True, stop=True),
                      reads=["rowb", d_name(ones1f)], writes=[("PS", 6)])
                fw.op("s", (lambda xi_: lambda e: e.activation(
                    out=dacbb[xi_], in_=ps(6, 128, 192, parts=64),
                    func=AF.Exp))(xi),
                    reads=[("PS", 6)], writes=[("dacb", xi)])

            def emit_waves4(q, wlist):
                xi = q % 2
                for w in wlist:
                    bbase = (w % 2) * 2
                    for hh in range(16):
                        h = w * 16 + hh
                        b_ = bbase + hh // 8
                        lo = (hh % 8) * 64
                        fw.op("t", (lambda h_, b2, lo_, xi_: lambda e: e.matmul(
                            ps(b2, lo_, lo_ + 64, parts=64),
                            btokb[xi_], xwb[xi_][:, h_ * 64:(h_ + 1) * 64],
                            start=True, stop=True))(h, b_, lo, xi),
                            reads=([("btk", xi), ("xw", xi)]
                                   if hh in (0, 15) else []),
                            writes=([("PS", b2) for b2 in (bbase, bbase + 1)]
                                    if hh == 0 else
                                    ([("PS", b_)] if hh % 8 == 7 else [])))
                    fw.op("v", (lambda w_, xi_: lambda e: e.tensor_mul(
                        sstate[:, w_ * 1024:(w_ + 1) * 1024].rearrange(
                            "p (b k) -> p b k", b=16),
                        sstate[:, w_ * 1024:(w_ + 1) * 1024].rearrange(
                            "p (b k) -> p b k", b=16),
                        dacbb[xi_][:, w_ * 16:(w_ + 1) * 16].unsqueeze(2)
                        .broadcast_to([64, 16, 64])))(w, xi),
                        reads=[("dacb", xi)], rw=[("sst", w)])
                    fw.op("v", (lambda w_: lambda e: e.tensor_add(
                        sstate[:, w_ * 1024:(w_ + 1) * 1024],
                        sstate[:, w_ * 1024:(w_ + 1) * 1024],
                        PA[0:64, (w_ % 2) * 1024:(w_ % 2) * 1024 + 1024]))(w),
                        reads=[("PS", bbase), ("PS", bbase + 1)], rw=[("sst", w)])
                    if w % 2 == 1:
                        half = w // 2
                        fw.op("q", (lambda q2, h2: lambda e: e.dma_start(
                            out=s_hbm[q2][:, h2 * 2048:(h2 + 1) * 2048],
                            in_=sstate[:, h2 * 2048:(h2 + 1) * 2048]))(q, half),
                            reads=[("sst", half * 2), ("sst", half * 2 + 1)],
                            writes=[("shbm", q, half)],
                            dma=fw.dsem(f"shS{half}"))

            for q in range(NCH):
                if q == 0:
                    emit_prep4(0)
                emit_waves4(q, [0, 1])
                if q + 1 < NCH:
                    emit_prep4(q + 1)
                emit_waves4(q, [2, 3])
                for _ in range(4):
                    if qdi < 64:
                        emit_qdn_piece(qdi)
                        qdi += 1
                if q < 16:
                    emit_wupT(q)
            dump("sfin", sstate, [("sst", w2) for w2 in range(4)])
            fw.op("q", lambda e: e.dma_start(out=ex2_in, in_=sstate),
                  reads=[("sst", w2) for w2 in range(4)], writes=["ex2i"],
                  dma=fw.dsem("ex2S"))
            fw.op("g", lambda e: e.collective_compute(
                "AllGather", ALU.bypass, replica_groups=RG,
                ins=[ex2_in], outs=[ex2_out]),
                reads=["ex2i"], writes=["ex2o"])
            fw.barrier()
            pool.pop()

        if phases >= 5:
            pool.push()
            xtok = [pool.bf16(CONVD) for _ in range(2)]
            ysb = pool.f32(4096)
            yntok = pool.bf16(4096)
            sinb = pool.f32(4096, parts=64)
            seffb = [pool.bf16(4096, parts=64) for _ in range(2)]
            sloc = [pool.bf16(4096, parts=64) for _ in range(2)]
            cbmb = [pool.bf16(128) for _ in range(2)]
            diall = pool.bf16(64 * 128)
            dclg = [pool.f32(512) for _ in range(4)]
            expg = [pool.f32(512) for _ in range(4)]
            msTg = [pool.bf16(512) for _ in range(16)]
            cumatb = [pool.f32(64) for _ in range(2)]
            etokb = [pool.f32(64) for _ in range(2)]
            gqb = [pool.f32(64, parts=64) for _ in range(2)]
            ccol = pool.f32(1, parts=64)
            tmpcb = [pool.f32(128, parts=64) for _ in range(2)]
            wt2 = [pool.bf16(2048) for _ in range(2)]

            def emit_wdnT(o):
                w_ = wt2[o % 2]
                fw.op("s", (lambda o_, w2: lambda e: e.dma_start_transpose(
                    w2, wdnn_hbm[:, o_ * 128:(o_ + 1) * 128]))(o, w_),
                    reads=([("wdnn", ii) for ii in range(64)] if o == 0 else []),
                    writes=[("wt2", o % 2)], dma=fw.dsem(f"wt2{o % 2}"))
                dst = wdni_hbm.rearrange("g p o j -> p g o j")[:, :, o, :]
                fw.op("q", (lambda d_, o_, w2: lambda e: e.dma_start(
                    out=d_, in_=w2.rearrange("p (g j) -> p g j", g=16),
                    allow_slow_non_contiguous=True))(dst, o, w_),
                    reads=[("wt2", o % 2)], writes=[("wdni", o)],
                    dma=fw.dsem(f"wdi{o % 2}"))

            for h in range(NH):
                fw.op("v", (lambda h_: lambda e: e.tensor_scalar(
                    out=diall[:, h_ * 128:(h_ + 1) * 128], in0=identb[:],
                    scalar1=dbc[:, h_:h_ + 1], scalar2=None, op0=ALU.mult))(h),
                    reads=[d_name(identb), "dbc"], writes=[("dia", h)])
            fw.op("q", lambda e: e.dma_start(out=sinb, in_=ex2_out[0:64, :]),
                  writes=["sin0"], dma=fw.dsem("sinS"))
            fw.op("v", lambda e: e.tensor_scalar(
                out=sinb, in0=sinb, scalar1=isodd[0:64, 0:1], scalar2=None,
                op0=ALU.mult),
                reads=["sin0", d_name(isodd)], writes=["sin"])
            fw.op("v", lambda e: e.memset(sloc[0], 0.0), writes=[("sloc", 0)])
            fw.op("v", lambda e: e.memset(ccol, 0.0), writes=["ccol"])
            wdni_cnt = [0]

            def emit_xtok5(q):
                xi = q % 2
                tg = f"xtkG{xi}"
                fw.dma_begin(tg)
                for r2 in range(2):
                    fw.op("s", (lambda q_, xi_, r3: lambda e: e.dma_start_transpose(
                        xtok[xi_][:, r3 * 2112:(r3 + 1) * 2112],
                        xfm_hbm[r3 * 2112:(r3 + 1) * 2112,
                                q_ * 128:(q_ + 1) * 128]))(q, xi, r2),
                        reads=([("xtk5", xi)] if q >= 2 else []),
                        writes=[("xtkp", xi, r2)], dma=tg, group=True)
                fw.dma_end(tg, extra_keys=[("xtok", xi)])
                if q == 0:
                    fw.op("v", (lambda xi_: lambda e: e.tensor_copy(
                        out=xtok[xi_][0:3, :], in_=fixtok[0:3, :]))(xi),
                        rw=[("xtok", xi)])

            def emit_prep5(q):
                pr = q % 2
                if q >= 1:
                    fw.op("g", (lambda q_, pr_: lambda e: e.dma_start(
                        out=sloc[pr_], in_=s_hbm[q_ - 1]))(q, pr),
                        writes=[("sloc", pr)], dma=fw.dsem(f"slc{pr}"))
                    fw.op("v", (lambda q_: lambda e: e.tensor_scalar(
                        out=ccol, in0=ccol,
                        scalar1=cuma[:, (q_ - 1) * 128 + 127:(q_ - 1) * 128 + 128],
                        scalar2=None, op0=ALU.add))(q),
                        rw=["ccol"])
                fw.op("t", lambda e: e.transpose(ps(0, 0, 64, parts=1), ccol,
                                                 identf[0:64, 0:64]),
                      reads=["ccol", d_name(identf)], writes=[("PS", 0)])
                fw.op("v", lambda e: e.tensor_copy(out=rowb[0:1, 0:64],
                                                   in_=ps(0, 0, 64, parts=1)),
                      reads=[("PS", 0)], writes=["rowb"])
                fw.op("t", lambda e: e.matmul(ps(0, 64, 128, parts=64),
                                              ones1f[0:1, 0:64], rowb[0:1, 0:64],
                                              start=True, stop=True),
                      reads=["rowb", d_name(ones1f)], writes=[("PS", 0)])
                fw.op("s", (lambda pr_: lambda e: e.activation(
                    out=gqb[pr_], in_=ps(0, 64, 128, parts=64),
                    func=AF.Exp))(pr),
                    reads=[("PS", 0)], writes=[("gq", pr)])
                fw.op("v", (lambda pr_: lambda e: e.tensor_mul(
                    seffb[pr_].rearrange("p (b k) -> p b k", b=64),
                    sinb.rearrange("p (b k) -> p b k", b=64),
                    gqb[pr_].unsqueeze(2).broadcast_to([64, 64, 64])))(pr),
                    reads=["sin", ("gq", pr)], writes=[("seff", pr)])
                fw.op("v", (lambda pr_: lambda e: e.tensor_add(
                    seffb[pr_], seffb[pr_], sloc[pr_]))(pr),
                    reads=[("sloc", pr)], rw=[("seff", pr)])
                fw.op("v", (lambda q_, pr_: lambda e: e.tensor_sub(
                    tmpcb[pr_], ldt[:, q_ * 128:(q_ + 1) * 128],
                    cuma[:, q_ * 128:(q_ + 1) * 128]))(q, pr),
                    reads=["ldt", ("cuma", q)], writes=[("tmpc", pr)])
                fw.op("t", (lambda q_: lambda e: e.transpose(
                    ps(0, 192, 256), cuma[:, q_ * 128:(q_ + 1) * 128],
                    identf[0:64, 0:64]))(q),
                    reads=[("cuma", q), d_name(identf)], writes=[("PS", 0)])
                fw.op("v", (lambda pr_: lambda e: e.tensor_copy(
                    out=cumatb[pr_], in_=ps(0, 192, 256)))(pr),
                    reads=[("PS", 0)], writes=[("cumat", pr)])
                fw.op("s", (lambda pr_: lambda e: e.activation(
                    out=etokb[pr_], in_=cumatb[pr_], func=AF.Exp))(pr),
                    reads=[("cumat", pr)], writes=[("etok", pr)])
                fw.op("t", (lambda q_: lambda e: e.matmul(
                    ps(1, 0, 128), b_fm[:, q_ * 128:(q_ + 1) * 128],
                    c_fm[:, q_ * 128:(q_ + 1) * 128],
                    start=True, stop=True))(q),
                    reads=["bfm", "cfm"], writes=[("PS", 1)])
                fw.op("v", (lambda pr_: lambda e: e.tensor_mul(
                    cbmb[pr_], ps(1, 0, 128), maskc[:]))(pr),
                    reads=[("PS", 1), d_name(maskc)], writes=[("cbm", pr)])

            def ecol(h_):
                return identf[0:64, h_:h_ + 1].broadcast_to([64, 128])

            def emit_M5(q):
                pr = q % 2
                for g4 in range(16):
                    dp = 6 + (g4 % 2)
                    gi = g4 % 4
                    gm = g4
                    for hh in range(4):
                        h = g4 * 4 + hh
                        fw.op("t", (lambda q_, h_, dp_, s_: lambda e: e.matmul(
                            ps(dp_, s_ * 128, (s_ + 1) * 128), ecol(h_),
                            cuma[:, q_ * 128:(q_ + 1) * 128],
                            start=True, stop=False))(q, h, dp, hh),
                            reads=([("cuma", q), d_name(identf)] if hh == 0 else []),
                            writes=([("PS", dp)] if hh == 0 else []))
                        fw.op("t", (lambda h_, dp_, s_, pr_: lambda e: e.matmul(
                            ps(dp_, s_ * 128, (s_ + 1) * 128), tmpcb[pr_], ecol(h_),
                            start=False, stop=True))(h, dp, hh, pr),
                            reads=([("tmpc", pr)] if hh == 0 else []),
                            writes=([("PS", dp)] if hh == 3 else []))
                    fw.op("v", (lambda dp_, g_: lambda e: e.tensor_scalar(
                        out=dclg[g_], in0=ps(dp_, 0, 512),
                        scalar1=80.0, scalar2=None, op0=ALU.min))(dp, gi),
                        reads=[("PS", dp)], writes=[("dcl", gi)])
                    fw.op("s", (lambda g_: lambda e: e.activation(
                        out=expg[g_], in_=dclg[g_], func=AF.Exp))(gi),
                        reads=[("dcl", gi)], writes=[("expg", gi)])
                    fw.op("g", (lambda g_, gm_, pr_: lambda e: e.tensor_mul(
                        msTg[gm_].rearrange("p (a b) -> p a b", a=4),
                        expg[g_].rearrange("p (a b) -> p a b", a=4),
                        cbmb[pr_].unsqueeze(1).broadcast_to([128, 4, 128])))(
                        gi, gm, pr),
                        reads=[("expg", gi), ("cbm", pr)], writes=[("msT", gm)])
                    fw.op("g", (lambda gm_, g4_: lambda e: e.tensor_add(
                        msTg[gm_], msTg[gm_],
                        diall[:, g4_ * 512:(g4_ + 1) * 512]))(gm, g4),
                        reads=[("dia", g4 * 4)], rw=[("msT", gm)])

            def emit_Y5(q):
                pr = q % 2
                xi = q % 2
                for g4 in range(16):
                    gm = g4
                    bi = 2 + ((g4 // 2) % 2) * 2
                    for hh in range(4):
                        h = g4 * 4 + hh
                        lo = (h % 8) * 64
                        fw.op("t", (lambda h_, b_, lo_, xi_, g_, s_: lambda e: e.matmul(
                            ps(b_, lo_, lo_ + 64), msTg[g_][:, s_ * 128:(s_ + 1) * 128],
                            xtok[xi_][:, h_ * 64:(h_ + 1) * 64],
                            start=True, stop=True))(h, bi, lo, xi, gm, hh),
                            reads=[("msT", gm), ("xtok", xi)],
                            rw=([("xtk5", xi)] if (g4 == 15 and hh == 3) else []),
                            writes=([("PS", bi)]
                                    if ((g4 % 2 == 0 and hh == 0) or
                                        (g4 % 2 == 1 and hh == 3)) else []))
                        fw.op("t", (lambda q_, h_, b_, lo_, pr_: lambda e: e.matmul(
                            ps(b_, lo_, lo_ + 64), c_fm[:, q_ * 128:(q_ + 1) * 128],
                            seffb[pr_][:, h_ * 64:(h_ + 1) * 64],
                            start=True, stop=True))(q, h, bi + 1, lo, pr),
                            reads=([("seff", pr), "cfm"]
                                   if (h < 4 or h == 63) else []),
                            writes=([("PS", bi + 1)]
                                    if ((g4 % 2 == 0 and hh == 0) or
                                        (g4 % 2 == 1 and hh == 3)) else []))
                    if g4 % 2 == 1:
                        g8 = g4 // 2
                        fw.op("v", (lambda g8_, b_, pr_: lambda e: e.tensor_mul(
                            ysb[:, g8_ * 512:(g8_ + 1) * 512]
                            .rearrange("p (a b) -> p a b", a=8),
                            ps(b_ + 1, 0, 512).rearrange("p (a b) -> p a b", a=8),
                            etokb[pr_][:, g8_ * 8:(g8_ + 1) * 8].unsqueeze(2)
                            .broadcast_to([128, 8, 64])))(g8, bi, pr),
                            reads=[("PS", bi + 1), ("etok", pr)],
                            writes=[("ysb", g8)])
                        fw.op("v", (lambda g8_, b_: lambda e: e.tensor_add(
                            ysb[:, g8_ * 512:(g8_ + 1) * 512],
                            ysb[:, g8_ * 512:(g8_ + 1) * 512],
                            ps(b_, 0, 512)))(g8, bi),
                            reads=[("PS", bi)], rw=[("ysb", g8)])

            def emit_rms5(q):
                fw.op("s", lambda e: e.activation(out=yntok, in_=ysb, func=AF.Square,
                                                  accum_out=stat[:, 120:121]),
                      reads=[("ysb", g2) for g2 in range(8)], writes=["ss", "yn"])
                fw.op("v", lambda e: e.tensor_scalar(
                    out=stat[:, 121:122], in0=stat[:, 120:121], scalar1=1.0 / 4096,
                    scalar2=1e-6, op0=ALU.mult, op1=ALU.add),
                    reads=["ss"], writes=["ssv"])
                fw.op("v", lambda e: e.reciprocal(stat[:, 122:123], stat[:, 121:122]),
                      reads=["ssv"], writes=["ssr"])
                fw.op("s", lambda e: e.activation(out=stat[:, 123:124],
                                                  in_=stat[:, 122:123], func=AF.Sqrt),
                      reads=["ssr"], writes=["rms"])
                fw.op("s", lambda e: e.activation(out=yntok, in_=ysb,
                                                  func=AF.Identity,
                                                  scale=stat[:, 123:124]),
                      reads=[("ysb", g2) for g2 in range(8)] + ["rms"],
                      rw=["yn"])
                fw.op("q", (lambda q_: lambda e: e.dma_start(
                    out=yn_hbm[q_ * 128:(q_ + 1) * 128, :], in_=yntok))(q),
                    reads=["yn"], writes=[("ynh", q)], dma=fw.dsem(f"ynS{q % 2}"))

            for q in range(NCH):
                if q == 0:
                    emit_xtok5(0)
                    emit_prep5(0)
                emit_M5(q)
                if q + 1 < NCH:
                    emit_xtok5(q + 1)
                    emit_prep5(q + 1)
                emit_Y5(q)
                emit_rms5(q)
                for _ in range(4):
                    if wdni_cnt[0] < 64:
                        emit_wdnT(wdni_cnt[0])
                        wdni_cnt[0] += 1
            fw.barrier()
            pool.pop()
            pool.pop()

        if phases >= 6:
            pool.push()
            stgb = [pool.f32(2048) for _ in range(2)]
            xtl = pool.f32(2048)
            l3tv = xtl.bitcast(BF16)[:, 0:2048]

            def emit_p7_tile(tt):
                ti = tt % 2
                stg = stgb[ti]
                pb = ti * 4
                gs = f"asS{ti}"
                fw.dma_begin(gs)
                for hh2 in range(2):
                    srcap = attn_hbm.rearrange("(k p) t -> p k t", p=128)[
                        :, hh2 * 8:(hh2 + 1) * 8, tt * 128:(tt + 1) * 128]
                    fw.op("g", (lambda s_, h2, st_: lambda e: e.dma_start(
                        out=st_[:, h2 * 1024:(h2 + 1) * 1024]
                        .rearrange("p (k j) -> p k j", k=8),
                        in_=s_, allow_slow_non_contiguous=True))(srcap, hh2, stg),
                        reads=[("attn", k2, tt // 8) for k2 in range(hh2 * 8, hh2 * 8 + 8)],
                        writes=[("ast", ti, k2) for k2 in range(hh2 * 8, hh2 * 8 + 8)]
                        + ([("ln2o", ti), ("x4t", ti)] if hh2 == 0 else []),
                        dma=gs, group=True)
                fw.dma_end(gs)
                for k in range(16):
                    bnk = pb + k // 4
                    fw.op("t", (lambda k_, b_, st_: lambda e: e.transpose(
                        ps(b_, (k_ % 4) * 128, (k_ % 4) * 128 + 128),
                        st_[:, k_ * 128:(k_ + 1) * 128], identf[:]))(k, bnk, stg),
                        reads=[("ast", ti, k), d_name(identf)], writes=[("PS", bnk)])
                for b_ in range(4):
                    fw.op("v", (lambda b2, st_: lambda e: e.tensor_copy(
                        out=st_[:, b2 * 512:(b2 + 1) * 512],
                        in_=ps(pb + b2, 0, 512)))(b_, stg),
                        reads=[("PS", pb + b_)],
                        writes=[("atok", ti, b_)]
                        + [("ast", ti, 4 * b_ + i2) for i2 in range(4)]
                        + ([("x4t", ti), ("ln2o", ti)] if b_ == 0 else []))
                emit_lnstats([("atok", ti, b2) for b2 in range(4)], stg, 1e-5, 32)
                fw.op("g", (lambda tt_, x_: lambda e: e.dma_start(
                    out=x_, in_=x_in[tt_ * 128:(tt_ + 1) * 128, :]))(tt, xtl),
                    writes=["xtl"], dma=fw.dsem("x7S"))
                fw.op("s", (lambda st_: lambda e: e.activation(
                    out=st_, in_=st_, func=AF.Identity,
                    bias=stat[:, 33:34], scale=stat[:, 32:33]))(stg),
                    reads=[("rs", 32), ("nb", 32)],
                    rw=[("atok", ti, b2) for b2 in range(4)],
                    writes=[("ln2o", ti)] + [("ast", ti, k2) for k2 in range(16)])
                fw.op("v", (lambda st_, x_: lambda e: e.tensor_add(
                    st_, x_, st_))(stg, xtl),
                    reads=["xtl"], rw=[("ln2o", ti)], writes=[("x4t", ti)])
                emit_lnstats([("x4t", ti)], stg, 1e-5, 64)
                fw.op("s", (lambda st_, l_: lambda e: e.activation(
                    out=l_, in_=st_, func=AF.Identity,
                    bias=stat[:, 65:66], scale=stat[:, 64:65]))(stg, l3tv),
                    reads=[("x4t", ti), ("rs", 64), ("nb", 64)],
                    writes=[("l3t", ti)], rw=["xtl"])
                fw.op("q", (lambda tt_, st_: lambda e: e.dma_start(
                    out=x4_hbm[tt_ * 128:(tt_ + 1) * 128, :], in_=st_))(tt, stg),
                    reads=[("x4t", ti)], writes=[("x4", tt)], dma=fw.dsem(f"x4S{ti}"))
                fw.op("q", (lambda tt_, l_: lambda e: e.dma_start(
                    out=l3x_hbm[tt_ * 128:(tt_ + 1) * 128, :], in_=l_))(tt, l3tv),
                    reads=[("l3t", ti), "xtl"], writes=[("l3", tt)],
                    dma=fw.dsem(f"l3S{ti}"))

            def emit_p9_tile(tt):
                ti = tt % 2
                stg = stgb[ti]
                pb = ti * 4
                gs = f"ysS{ti}"
                fw.dma_begin(gs)
                for hh2 in range(2):
                    srcap = y2_hbm.rearrange("(k p) t -> p k t", p=128)[
                        :, hh2 * 8:(hh2 + 1) * 8, tt * 128:(tt + 1) * 128]
                    fw.op("g", (lambda s_, h2, st_: lambda e: e.dma_start(
                        out=st_[:, h2 * 1024:(h2 + 1) * 1024]
                        .rearrange("p (k j) -> p k j", k=8),
                        in_=s_, allow_slow_non_contiguous=True))(srcap, hh2, stg),
                        reads=[("y2", k2, tt // 8) for k2 in range(hh2 * 8, hh2 * 8 + 8)],
                        writes=[("yst", ti, k2) for k2 in range(hh2 * 8, hh2 * 8 + 8)]
                        + ([("ln4o", ti), ("outt", ti)] if hh2 == 0 else []),
                        dma=gs, group=True)
                fw.dma_end(gs)
                for k in range(16):
                    bnk = pb + k // 4
                    fw.op("t", (lambda k_, b_, st_: lambda e: e.transpose(
                        ps(b_, (k_ % 4) * 128, (k_ % 4) * 128 + 128),
                        st_[:, k_ * 128:(k_ + 1) * 128], identf[:]))(k, bnk, stg),
                        reads=[("yst", ti, k), d_name(identf)], writes=[("PS", bnk)])
                for b_ in range(4):
                    fw.op("v", (lambda b2, st_: lambda e: e.tensor_copy(
                        out=st_[:, b2 * 512:(b2 + 1) * 512],
                        in_=ps(pb + b2, 0, 512)))(b_, stg),
                        reads=[("PS", pb + b_)],
                        writes=[("y2t", ti, b_)]
                        + [("yst", ti, 4 * b_ + i2) for i2 in range(4)]
                        + ([("ln4o", ti), ("outt", ti)] if b_ == 0 else []))
                emit_lnstats([("y2t", ti, b2) for b2 in range(4)], stg, 1e-5, 96)
                fw.op("g", (lambda tt_, x_: lambda e: e.dma_start(
                    out=x_, in_=x4_hbm[tt_ * 128:(tt_ + 1) * 128, :]))(tt, xtl),
                    reads=[("x4", tt)], writes=["xtl"], dma=fw.dsem("x49S"))
                fw.op("s", (lambda st_: lambda e: e.activation(
                    out=st_, in_=st_, func=AF.Identity,
                    bias=stat[:, 97:98], scale=stat[:, 96:97]))(stg),
                    reads=[("rs", 96), ("nb", 96)],
                    rw=[("y2t", ti, b2) for b2 in range(4)],
                    writes=[("ln4o", ti)] + [("yst", ti, k2) for k2 in range(16)])
                fw.op("v", (lambda st_, x_: lambda e: e.tensor_add(
                    st_, x_, st_))(stg, xtl),
                    reads=["xtl"], rw=[("ln4o", ti)], writes=[("outt", ti)])
                fw.op("q", (lambda tt_, st_: lambda e: e.dma_start(
                    out=out_hbm[tt_ * 128:(tt_ + 1) * 128, :], in_=st_))(tt, stg),
                    reads=[("outt", ti)], writes=[("out", tt)],
                    dma=fw.dsem(f"outS{ti}"))

            pool.push()
            l1fm = pool.bf16(16 * 2048)
            wbuf6 = [pool.bf16(2048) for _ in range(3)]
            gtile = pool.bf16(2048)
            yfm = [pool.bf16(2048) for _ in range(2)]
            ygout = pool.bf16(2048)

            def l1fm_k(k):
                return l1fm[:, k * 2048:(k + 1) * 2048]

            fw.dma_begin("l1fS")
            for k in range(16):
                fw.op("s", (lambda k_: lambda e: e.dma_start_transpose(
                    l1fm_k(k_), l1x_hbm[:, k_ * 128:(k_ + 1) * 128]))(k),
                    reads=[("l1x", tt2) for tt2 in range(16)] if k == 0 else [],
                    writes=[("l1f", k)], dma="l1fS", group=True)
            fw.dma_end("l1fS")
            for o in range(32):
                wb = wbuf6[o % 3]
                wbk = ("wb6", o % 3)
                fw.op("g", (lambda o_, wb_: lambda e: e.dma_start(
                    out=wb_, in_=wg_img[o_]))(o, wb),
                    writes=[wbk], dma=fw.dsem(f"wg{o % 3}"))
                Pt = o % 2
                for tt in range(4):
                    for k in range(16):
                        first, last = k == 0, k == 15
                        rd = []
                        if first:
                            rd = [wbk] + [("l1f", kk) for kk in range(16)]
                        elif last:
                            rd = [wbk]
                        fw.op("t", (lambda wb_, k_, tt_, f_, l_, p_: lambda e: e.matmul(
                            ps(p_ * 4 + tt_), wb_[:, k_ * 128:(k_ + 1) * 128],
                            l1fm_k(k_)[:, tt_ * 512:(tt_ + 1) * 512],
                            start=f_, stop=l_))(wb, k, tt, first, last, Pt),
                            reads=rd,
                            writes=([("PS", Pt * 4 + tt)] if (first or last) else []))
                fw.op("s", (lambda p_: lambda e: e.activation(
                    out=gtile, in_=(PA if p_ == 0 else PB)[:, 0:T],
                    func=AF.Silu))(Pt),
                    reads=[("PS", Pt * 4 + tt) for tt in range(4)],
                    writes=["gtile"])
                yf = yfm[o % 2]
                fw.op("s", (lambda o_, yf_: lambda e: e.dma_start_transpose(
                    yf_, yn_hbm[:, o_ * 128:(o_ + 1) * 128]))(o, yf),
                    reads=[("ynh", q2) for q2 in range(NCH)],
                    writes=[("yfm", o % 2)], dma=fw.dsem(f"yfm{o % 2}"))
                fw.op("v", (lambda o_, yf_: lambda e: e.tensor_scalar(
                    out=ygout, in0=yf_, scalar1=nw[:, o_:o_ + 1], scalar2=None,
                    op0=ALU.mult))(o, yf),
                    reads=[("yfm", o % 2), d_name(nw)], writes=["ygo"])
                fw.op("v", lambda e: e.tensor_mul(ygout, ygout, gtile),
                      reads=["gtile"], rw=["ygo"])
                fw.op("q", (lambda o_: lambda e: e.dma_start(
                    out=yg_hbm[o_ * 128:(o_ + 1) * 128, :], in_=ygout))(o),
                    reads=["ygo"], writes=[("ygh", o)], dma=fw.dsem(f"ygS{o % 2}"))
            fw.barrier()
            pool.pop()

            pool.push()
            ygh = pool.bf16(32 * 1024)
            wobuf = [pool.bf16(4096) for _ in range(2)]
            attnev = pool.f32(1024)
            for H in range(2):
                fw.dma_begin("ygtS")
                for k in range(32):
                    fw.op("q", (lambda k_, H_: lambda e: e.dma_start(
                        out=ygh[:, k_ * 1024:(k_ + 1) * 1024],
                        in_=yg_hbm[k_ * 128:(k_ + 1) * 128,
                                   H_ * 1024:(H_ + 1) * 1024]))(k, H),
                        reads=[("ygh", oo) for oo in range(32)] if k == 0 else [],
                        writes=[("ygt", k)], dma="ygtS", group=True)
                fw.dma_end("ygtS")
                for o in range(16):
                    wb = wobuf[o % 2]
                    wbk = ("wo", o % 2)
                    fw.op("g", (lambda o_, wb_: lambda e: e.dma_start(
                        out=wb_, in_=wo_img[o_]))(o, wb),
                        writes=[wbk], dma=fw.dsem(f"wo{o % 2}"))
                    slot = o % 4
                    bk = [slot * 2, slot * 2 + 1]
                    for ts_ in range(2):
                        for k in range(32):
                            first, last = k == 0, k == 31
                            rd = []
                            if first or last:
                                rd = [wbk] + [("ygt", kk) for kk in range(32)]
                            fw.op("t", (lambda wb_, k_, t2, f_, l_, b2: lambda e: e.matmul(
                                ps(b2), wb_[:, k_ * 128:(k_ + 1) * 128],
                                ygh[:, k_ * 1024 + t2 * 512:k_ * 1024 + (t2 + 1) * 512],
                                start=f_, stop=l_))(wb, k, ts_, first, last, bk[ts_]),
                                reads=rd,
                                writes=([("PS", bk[ts_])] if (first or last) else []))
                    fw.op("v", (lambda b0: lambda e: e.tensor_copy(
                        out=attnev, in_=ps(b0, 0, 1024)))(bk[0]),
                        reads=[("PS", bk[0]), ("PS", bk[1])], writes=["aev"])
                    fw.op("q", (lambda o_, H_: lambda e: e.dma_start(
                        out=attn_hbm[o_ * 128:(o_ + 1) * 128,
                                     H_ * 1024:(H_ + 1) * 1024], in_=attnev))(o, H),
                        reads=["aev"], writes=[("attn", o, H)],
                        dma=fw.dsem(f"atS{o % 2}"))
                for tt in range(H * 8, H * 8 + 8):
                    emit_p7_tile(tt)
            fw.barrier()
            pool.pop()

        if phases >= 8:
            pool.push()
            x2sb = pool.bf16(64 * 1024)

            def x2_o(o):
                return x2sb[:, o * 1024:(o + 1) * 1024]

            for H in range(2):
                pool.push()
                l3fm = pool.bf16(16 * 1024)
                upw = [pool.bf16(2048) for _ in range(2)]
                bupl = pool.bf16(HF // 8, parts=16)
                t1b = pool.bf16(1024, parts=16)
                fw.op("g", lambda e: e.dma_start(out=bupl, in_=bupl_in[:, 0:HF // 8]),
                      writes=["bupl"], dma=fw.dsem("buplS"))
                fw.dma_begin("l3fS")
                for k in range(16):
                    fw.op("s", (lambda k_, H_: lambda e: e.dma_start_transpose(
                        l3fm[:, k_ * 1024:(k_ + 1) * 1024],
                        l3x_hbm[H_ * 1024:(H_ + 1) * 1024,
                                k_ * 128:(k_ + 1) * 128]))(k, H),
                        reads=[("l3", tt2) for tt2 in range(H * 8, H * 8 + 8)],
                        writes=[("l3f", k)], dma="l3fS", group=True)
                fw.dma_end("l3fS")
                for j in range(2):
                    for k in range(16):
                        first, last = k == 0, k == 15
                        fw.op("t", (lambda k_, j_, f_, l_: lambda e: e.matmul(
                            ps(j_, 0, 512, parts=16),
                            aupb[:, k_ * 16:k_ * 16 + 16],
                            l3fm[:, k_ * 1024 + j_ * 512:k_ * 1024 + (j_ + 1) * 512],
                            start=f_, stop=l_))(k, j, first, last),
                            reads=([d_name(aupb)] + [("l3f", kk) for kk in range(16)]
                                   if first else []),
                            writes=([("PS", j)] if (first or last) else []))
                fw.op("v", lambda e: e.tensor_copy(
                    out=t1b, in_=PA[0:16, 0:1024]),
                    reads=[("PS", 0), ("PS", 1)], writes=["t1b"])
                for o in range(64):
                    if o % 8 == 0 and o > 0:
                        fw.op("g", (lambda o_: lambda e: e.dma_start(
                            out=bupl, in_=bupl_in[:, o_ * 128:(o_ + 8) * 128]))(o),
                            rw=["bupl"], dma="buplS")
                    uw = upw[o % 2]
                    uwk = ("uw", o % 2)
                    fw.op("q", (lambda o_, u_: lambda e: e.dma_start(
                        out=u_, in_=wupi_hbm[o_]))(o, uw),
                        reads=([("wupik", kk) for kk in range(16)] if o == 0 else []),
                        writes=[uwk], dma=fw.dsem(f"uw{o % 2}"))
                    slot = 2 + (o % 2) * 4
                    for j in range(2):
                        b2 = slot + j
                        for k in range(16):
                            first, last = k == 0, k == 15
                            rd = [uwk] + ([("l3f", kk) for kk in range(16)]
                                          if (first or last) else [])
                            if not (first or last):
                                rd = []
                            fw.op("t", (lambda u_, k_, j_, f_, b3: lambda e: e.matmul(
                                ps(b3), u_[:, k_ * 128:(k_ + 1) * 128],
                                l3fm[:, k_ * 1024 + j_ * 512:k_ * 1024 + (j_ + 1) * 512],
                                start=f_, stop=False))(uw, k, j, first, b2),
                                reads=rd,
                                writes=([("PS", b2)] if first else []))
                        fw.op("t", (lambda o_, j_, b3: lambda e: e.matmul(
                            ps(b3), bupl[:, (o_ % 8) * 128:(o_ % 8 + 1) * 128],
                            t1b[:, j_ * 512:(j_ + 1) * 512],
                            start=False, stop=True))(o, j, b2),
                            reads=["t1b", "bupl"], writes=[("PS", b2)])
                    fw.op("s", (lambda o_, s_: lambda e: e.activation(
                        out=x2_o(o_), in_=(PA if s_ == 2 else PB)[:, 1024:2048],
                        func=AF.Relu, bias=bup[:, o_:o_ + 1]))(o, slot),
                        reads=[("PS", slot), ("PS", slot + 1), d_name(bup)],
                        writes=[("x2", o)])
                fw.barrier()
                pool.pop()
                pool.push()
                wdg = [pool.bf16(64 * 128) for _ in range(2)]
                bdnl = pool.bf16(D, parts=16)
                t2b = pool.bf16(1024, parts=16)
                y2ev = pool.f32(1024)
                fw.op("g", lambda e: e.dma_start(out=bdnl, in_=bdnl_in),
                      writes=["bdnl"], dma=fw.dsem("bdnlS"))
                for j in range(2):
                    for o in range(64):
                        first, last = o == 0, o == 63
                        fw.op("t", (lambda o_, j_, f_, l_: lambda e: e.matmul(
                            ps(j_, 0, 512, parts=16),
                            adnb[:, o_ * 16:(o_ + 1) * 16],
                            x2_o(o_)[:, j_ * 512:(j_ + 1) * 512],
                            start=f_, stop=l_))(o, j, first, last),
                            reads=([d_name(adnb)] + [("x2", oo) for oo in range(64)]
                                   if first else ([("x2", 63)] if last else [])),
                            writes=([("PS", j)] if (first or last) else []))
                fw.op("v", lambda e: e.tensor_copy(
                    out=t2b, in_=PA[0:16, 0:1024]),
                    reads=[("PS", 0), ("PS", 1)], writes=["t2b"])
                for d_ in range(16):
                    wg_ = wdg[d_ % 2]
                    fw.op("s", (lambda d2, w2: lambda e: e.dma_start(
                        out=w2, in_=wdni_hbm[d2].rearrange("p o j -> p (o j)")))(
                        d_, wg_),
                        reads=([("wdni", ii) for ii in range(64)]
                               if (H == 0 and d_ == 0) else []),
                        writes=[("wdg", d_ % 2)], dma=fw.dsem(f"wdgS{d_ % 2}"))
                    slot = 2 + (d_ % 2) * 4
                    for j in range(2):
                        b2 = slot + j
                        for o in range(64):
                            first = o == 0
                            fw.op("t", (lambda o_, j_, f_, b3, w2: lambda e: e.matmul(
                                ps(b3), w2[:, o_ * 128:(o_ + 1) * 128],
                                x2_o(o_)[:, j_ * 512:(j_ + 1) * 512],
                                start=f_, stop=False))(o, j, first, b2, wg_),
                                reads=([("wdg", d_ % 2)]
                                       if (first or o == 63) else []),
                                writes=([("PS", b2)] if first else []))
                        fw.op("t", (lambda d2, j_, b3: lambda e: e.matmul(
                            ps(b3), bdnl[:, d2 * 128:(d2 + 1) * 128],
                            t2b[:, j_ * 512:(j_ + 1) * 512],
                            start=False, stop=True))(d_, j, b2),
                            reads=["t2b", "bdnl"], writes=[("PS", b2)])
                    fw.op("s", (lambda d2, s_: lambda e: e.activation(
                        out=y2ev, in_=(PA if s_ == 2 else PB)[:, 1024:2048],
                        func=AF.Identity, bias=bdn[:, d2:d2 + 1]))(d_, slot),
                        reads=[("PS", slot), ("PS", slot + 1), d_name(bdn)],
                        writes=["y2e"])
                    fw.op("q", (lambda d2, H_: lambda e: e.dma_start(
                        out=y2_hbm[d2 * 128:(d2 + 1) * 128,
                                   H_ * 1024:(H_ + 1) * 1024], in_=y2ev))(d_, H),
                        reads=["y2e"], writes=[("y2", d_, H)],
                        dma=fw.dsem(f"y2S{d_ % 2}"))
                fw.barrier()
                pool.pop()
                for tt in range(H * 8, H * 8 + 8):
                    emit_p9_tile(tt)
            pool.pop()
            pool.pop()

        fw.finalize()
        fw.emit(block, ctx)
    return nc


def d_name(t):
    return t.tensor.name if hasattr(t, "tensor") else t.name



def _host_prep(inputs):
    f = np.float32
    W = np.asarray(inputs["L0_attn_in_proj_w"], f)
    wg = np.ascontiguousarray(
        W[:4096].reshape(32, 128, 16, 128).transpose(0, 3, 2, 1)).reshape(32, 128, 2048)
    wh = np.ascontiguousarray(
        W[4096:8320].reshape(33, 128, 16, 128).transpose(0, 3, 2, 1)).reshape(33, 128, 2048)
    wd = np.ascontiguousarray(
        W[8320:8384].reshape(64, 16, 128).transpose(2, 1, 0)).reshape(128, 1024)
    Wo = np.asarray(inputs["L0_attn_out_proj_w"], f)
    wo = np.ascontiguousarray(
        Wo.reshape(16, 128, 32, 128).transpose(0, 3, 2, 1)).reshape(16, 128, 4096)
    d = {
        "wg_img": wg, "wh_img": wh, "wd_img": wd, "wo_img": wo,
        "convw_r": np.ascontiguousarray(
            np.asarray(inputs["L0_attn_conv_w"], f).reshape(33, 128, 4)
            .transpose(1, 0, 2)).reshape(128, 132),
        "convb_r": np.ascontiguousarray(
            np.asarray(inputs["L0_attn_conv_b"], f).reshape(33, 128).T),
        "dtb": np.asarray(inputs["L0_attn_dt_bias"], f).reshape(64, 1),
        "alog": np.asarray(inputs["L0_attn_A_log"], f).reshape(64, 1),
        "dcol": np.asarray(inputs["L0_attn_D"], f).reshape(64, 1),
        "nw_r": np.ascontiguousarray(
            np.asarray(inputs["L0_attn_norm_weight"], f).reshape(32, 128).T),
        "q_up": np.asarray(inputs["L0_ffn_w_up_q"], np.int32),
        "scup_r": np.ascontiguousarray(
            np.asarray(inputs["L0_ffn_w_up_scale"], f).reshape(64, 128, 32)
            .transpose(1, 0, 2)).reshape(128, 64 * 32),
        "bup_r": np.ascontiguousarray(
            np.asarray(inputs["L0_ffn_b_up"], f).reshape(64, 128).T),
        "q_dn": np.asarray(inputs["L0_ffn_w_down_q"], np.int32),
        "scdn_r": np.ascontiguousarray(
            np.asarray(inputs["L0_ffn_w_down_scale"], f).reshape(16, 128, 128)
            .transpose(1, 0, 2)).reshape(128, 16 * 128),
        "bdn_r": np.ascontiguousarray(
            np.asarray(inputs["L0_ffn_b_down"], f).reshape(16, 128).T),
        "aup_r": np.ascontiguousarray(
            np.asarray(inputs["L0_ffn_w_up_lora_a"], f).reshape(16, 128, 16)
            .transpose(1, 0, 2)).reshape(128, 256),
        "b_upl": np.asarray(inputs["L0_ffn_w_up_lora_b"], f),
        "adn_r": np.ascontiguousarray(
            np.asarray(inputs["L0_ffn_w_down_lora_a"], f).reshape(64, 128, 16)
            .transpose(1, 0, 2)).reshape(128, 1024),
        "b_dnl": np.asarray(inputs["L0_ffn_w_down_lora_b"], f),
    }
    return {k: np.ascontiguousarray(v) for k, v in d.items()}


def make_in_maps(inputs, shared=None):
    if shared is None:
        shared = _host_prep(inputs)
    x = np.ascontiguousarray(
        np.asarray(inputs["L0_attn_hidden_states"], np.float32).reshape(16384, 2048))
    in_maps = []
    for c in range(NCORES):
        m = dict(shared)
        m["x"] = np.ascontiguousarray(x[c * T:(c + 1) * T])
        m["is_odd"] = np.full((128, 1), 1.0 if c % 2 else 0.0, np.float32)
        in_maps.append(m)
    return in_maps


def kernel(**inputs):
    nc = build_program()
    in_maps = make_in_maps(inputs)
    res = run_bass_kernel_spmd(nc, in_maps, core_ids=list(range(NCORES)))
    out = np.concatenate([np.asarray(r["out"]) for r in res.results], axis=0)
    return out.reshape(4, 4096, 2048)


# version 14 (final)
# speedup vs baseline: 1.1485x; 1.0617x over previous
import numpy as np
import ml_dtypes
from contextlib import ExitStack

import concourse.bass as bass
import concourse.mybir as mybir
from concourse.bass_utils import run_bass_kernel_spmd

F32 = mybir.dt.float32
BF16 = mybir.dt.bfloat16
I32 = mybir.dt.int32
AF = mybir.ActivationFunctionType
ALU = mybir.AluOpType
AX = mybir.AxisListType
BF = ml_dtypes.bfloat16

T = 2048
D = 2048
INTER = 4096
NH = 64
HD = 64
NS = 64
CONVD = 4224
HF = 8192
NCH = 16
NCORES = 8
RG = [[0, 1], [2, 3], [4, 5], [6, 7]]

ENGS = ("t", "v", "s", "g", "q")


class FW:

    def __init__(self, nc):
        self.nc = nc
        self.ops = {E: [] for E in ENGS}
        self.writer = {}
        self.readers = {}
        self.hw = {E: {} for E in ENGS}
        self.dcnt = {}
        self.dbatch = {}

    def dsem(self, name, batch=False):
        if name not in self.dcnt:
            self.dcnt[name] = 0
            self.dbatch[name] = batch
        return name

    def dma_begin(self, name):
        self.dsem(name)
        self.pending = getattr(self, "pending", {})
        self.pending[name] = []

    def dma_end(self, name, extra_keys=()):
        tok = ("d", name, self.dcnt[name])
        for wks, rks in self.pending.pop(name):
            for k in wks:
                self.writer[k] = tok
                self.readers[k] = []
            for k in rks:
                self.readers.setdefault(k, []).append(tok)
        for k in extra_keys:
            self.writer[k] = tok
            self.readers[k] = []
        return tok

    def op(self, E, fn, reads=(), writes=(), rw=(), dma=None, group=False):
        waits = []

        def need(tok):
            if tok is None:
                return
            if tok[0] == "e":
                _, E2, seq = tok
                if seq <= self.hw[E].get(("e", E2), -1):
                    return
                self.hw[E][("e", E2)] = seq
                self.ops[E2][seq]["mark"] = True
                waits.append(tok)
            elif tok[0] == "d":
                _, nm, val = tok
                if val <= self.hw[E].get(("d", nm), 0):
                    return
                self.hw[E][("d", nm)] = val
                waits.append(tok)
            else:
                if self.hw[E].get(("dF", tok[1])):
                    return
                self.hw[E][("dF", tok[1])] = True
                waits.append(tok)

        reads = tuple(reads) + tuple(rw)
        writes = tuple(writes) + tuple(rw)
        for k in reads:
            need(self.writer.get(k))
        for k in writes:
            need(self.writer.get(k))
            for t_ in self.readers.get(k, ()):
                need(t_)
        seq = len(self.ops[E])
        self.ops[E].append({"fn": fn, "waits": waits, "mark": False, "dma": dma})
        if dma is not None:
            self.dcnt[dma] += 16
            if group:
                self.pending[dma].append((writes, reads))
                return None
            tok = ("dF", dma) if self.dbatch[dma] else ("d", dma, self.dcnt[dma])
        else:
            tok = ("e", E, seq)
        for k in reads:
            self.readers.setdefault(k, []).append(tok)
        for k in writes:
            self.writer[k] = tok
            self.readers[k] = []
        return tok

    def barrier(self):
        def qfn(eng):
            return eng.nop()
        toks = []
        for E in ENGS:
            if E == "q" or not self.ops[E]:
                continue
            seq = len(self.ops[E]) - 1
            while seq >= 0 and self.ops[E][seq]["dma"] is not None:
                seq -= 1
            if seq >= 0:
                toks.append(("e", E, seq))
        dtoks = []
        for nm, c in self.dcnt.items():
            if c > self.hw["q"].get(("d", nm), 0) or (self.dbatch[nm] and c):
                dtoks.append(("d", nm, c) if not self.dbatch[nm] else ("dF2", nm, c))
        waits = []
        for tok in toks:
            _, E2, seq = tok
            if seq <= self.hw["q"].get(("e", E2), -1):
                continue
            self.hw["q"][("e", E2)] = seq
            self.ops[E2][seq]["mark"] = True
            waits.append(tok)
        for tok in dtoks:
            if tok[0] == "dF2":
                _, nm, c = tok
                waits.append(("d", nm, c))
                self.hw["q"][("d", nm)] = c
            else:
                _, nm, c = tok
                if c <= self.hw["q"].get(("d", nm), 0):
                    continue
                self.hw["q"][("d", nm)] = c
                waits.append(tok)
        seq = len(self.ops["q"])
        self.ops["q"].append({"fn": qfn, "waits": waits, "mark": True, "dma": None})
        qtok = ("e", "q", seq)
        for E in ENGS:
            if E == "q":
                continue
            seqE = len(self.ops[E])
            self.ops[E].append({"fn": lambda eng: eng.nop(),
                               "waits": [qtok], "mark": False, "dma": None})
            self.hw[E][("e", "q")] = seq
        self.writer = {}
        self.readers = {}
        return qtok

    def emit(self, block, ctx):
        esem = {E: ctx.enter_context(self.nc.semaphore("es_" + E)) for E in ENGS}
        dsem = {nm: ctx.enter_context(self.nc.semaphore("ds_" + nm))
                for nm in self.dcnt}
        prefix = {}
        for E in ENGS:
            c = 0
            pl = []
            for o in self.ops[E]:
                if o["mark"]:
                    c += 1
                pl.append(c)
            prefix[E] = pl

        def mk(E):
            def run(eng):
                for o in self.ops[E]:
                    for tok in o["waits"]:
                        if tok[0] == "e":
                            eng.wait_ge(esem[tok[1]], prefix[tok[1]][tok[2]])
                        elif tok[0] == "d":
                            eng.wait_ge(dsem[tok[1]], tok[2])
                        else:
                            eng.wait_ge(dsem[tok[1]], self.dcnt[tok[1]])
                    inst = o["fn"](eng)
                    if o["dma"] is not None:
                        inst.then_inc(dsem[o["dma"]], 16)
                    elif o["mark"]:
                        inst.then_inc(esem[E], 1)
            return run

        block.tensor(mk("t"))
        block.vector(mk("v"))
        block.scalar(mk("s"))
        block.gpsimd(mk("g"))
        block.sync(mk("q"))

    def finalize(self, extra_sems=()):
        waits = []
        for nm, c in self.dcnt.items():
            if c > self.hw["q"].get(("d", nm), 0):
                waits.append(("d", nm, c))
                self.hw["q"][("d", nm)] = c
        self.ops["q"].append({"fn": lambda eng: eng.nop(), "waits": waits,
                              "mark": False, "dma": None})


class Pool:

    def __init__(self, base_f32):
        self.base = base_f32
        self.n = base_f32.shape[1]
        self.off = 0
        self.stack = []

    def push(self):
        self.stack.append(self.off)

    def pop(self):
        self.off = self.stack.pop()

    def f32(self, words, parts=128):
        o = self.off
        self.off = o + ((words + 15) // 16) * 16
        assert self.off <= self.n, f"pool overflow {self.off} > {self.n}"
        return self.base[0:parts, o:o + words]

    def bf16(self, cols, parts=128):
        w = (cols + 1) // 2
        return self.f32(w, parts).bitcast(BF16)


def build_program(phases=99, dumps=()):
    nc = bass.Bass("TRN2", target_bir_lowering=False, debug=False)
    dumps = set(dumps)

    def dram(name, shape, dtype):
        kind = "ExternalOutput" if name in dumps else "Internal"
        return nc.dram_tensor(name, shape, dtype, kind=kind)

    EI = "ExternalInput"
    x_in = nc.dram_tensor("x", [T, D], F32, kind=EI).ap()
    wg_img = nc.dram_tensor("wg_img", [32, 128, 2048], F32, kind=EI).ap()
    wh_img = nc.dram_tensor("wh_img", [33, 128, 2048], F32, kind=EI).ap()
    wd_img = nc.dram_tensor("wd_img", [128, 1024], F32, kind=EI).ap()
    wo_img = nc.dram_tensor("wo_img", [16, 128, 4096], F32, kind=EI).ap()
    convw_in = nc.dram_tensor("convw_r", [128, 132], F32, kind=EI).ap()
    convb_in = nc.dram_tensor("convb_r", [128, 33], F32, kind=EI).ap()
    dtb_in = nc.dram_tensor("dtb", [64, 1], F32, kind=EI).ap()
    alog_in = nc.dram_tensor("alog", [64, 1], F32, kind=EI).ap()
    dcol_in = nc.dram_tensor("dcol", [64, 1], F32, kind=EI).ap()
    nw_in = nc.dram_tensor("nw_r", [128, 32], F32, kind=EI).ap()
    qup_in = nc.dram_tensor("q_up", [HF, D], I32, kind=EI).ap()
    scup_in = nc.dram_tensor("scup_r", [128, 64 * 32], F32, kind=EI).ap()
    bup_in = nc.dram_tensor("bup_r", [128, 64], F32, kind=EI).ap()
    qdn_in = nc.dram_tensor("q_dn", [D, HF], I32, kind=EI).ap()
    scdn_in = nc.dram_tensor("scdn_r", [128, 16 * 128], F32, kind=EI).ap()
    bdn_in = nc.dram_tensor("bdn_r", [128, 16], F32, kind=EI).ap()
    aup_in = nc.dram_tensor("aup_r", [128, 256], F32, kind=EI).ap()
    bupl_in = nc.dram_tensor("b_upl", [16, HF], F32, kind=EI).ap()
    adn_in = nc.dram_tensor("adn_r", [128, 1024], F32, kind=EI).ap()
    bdnl_in = nc.dram_tensor("b_dnl", [16, D], F32, kind=EI).ap()
    isodd_in = nc.dram_tensor("is_odd", [128, 1], F32, kind=EI).ap()

    identb_c = nc.inline_tensor(np.eye(128, dtype=BF), "identbc").ap()
    identf_c = nc.inline_tensor(np.eye(128, dtype=np.float32), "identfc").ap()
    e_all_np = np.zeros((64, 64 * 128), np.float32)
    for h in range(64):
        e_all_np[h, h * 128:(h + 1) * 128] = 1.0
    eall_c = nc.inline_tensor(e_all_np, "eallc").ap()
    mask_np = np.triu(np.ones((128, 128), np.float32))
    mask_c = nc.inline_tensor(mask_np, "maskc").ap()
    ones1f_c = nc.inline_tensor(np.ones((1, 128), np.float32), "ones1fc").ap()

    xfm_hbm = dram("xfm_hbm", [CONVD, T], BF16).ap()
    s_hbm = dram("s_hbm", [NCH, 64, NH * HD], F32).ap()
    ex1_in = nc.dram_tensor("ex1_in", [128, 99], F32, kind="Internal").ap()
    ex1_out = nc.dram_tensor("ex1_out", [256, 99], F32, kind="Internal").ap()
    ex2_in = nc.dram_tensor("ex2_in", [64, 4096], F32, kind="Internal").ap()
    ex2_out = nc.dram_tensor("ex2_out", [128, 4096], F32, kind="Internal").ap()
    yn_hbm = dram("yn_hbm", [T, INTER], BF16).ap()
    l1x_hbm = dram("l1x_hbm", [T, D], BF16).ap()
    yg_hbm = dram("yg_hbm", [INTER, T], BF16).ap()
    attn_hbm = dram("attn_hbm", [D, T], F32).ap()
    x4_hbm = dram("x4_hbm", [T, D], F32).ap()
    l3x_hbm = dram("l3x_hbm", [T, D], BF16).ap()
    wupn_hbm = dram("wupn_hbm", [HF, D], BF16).ap()
    wupi_hbm = dram("wupi_hbm", [64, 128, 2048], BF16).ap()
    wdnn_hbm = dram("wdnn_hbm", [D, HF], BF16).ap()
    wdni_hbm = dram("wdni_hbm", [16, 128, 64, 128], BF16).ap()
    y2_hbm = dram("y2_hbm", [D, T], F32).ap()
    out_hbm = nc.dram_tensor("out", [T, D], F32, kind="ExternalOutput").ap()

    fw = FW(nc)

    with ExitStack() as ctx:
        def sb(name, shape, dtype):
            return ctx.enter_context(nc.sbuf_tensor(name, shape, dtype))

        convw = sb("convw", [128, 132], F32)
        convb = sb("convb", [128, 33], F32)
        dtb = sb("dtbs", [64, 1], F32)
        acol = sb("acol", [64, 1], F32)
        dbc = sb("dbc", [128, 64], F32)
        nw = sb("nws", [128, 32], F32)
        isodd = sb("isodds", [128, 1], F32)
        identb = sb("identbs", [128, 128], BF16)
        identf = sb("identfs", [128, 128], F32)
        maskc = sb("maskcs", [128, 128], F32)
        ones1f = sb("ones1fs", [1, 128], F32)
        zeros64 = sb("zeros64", [64, 128], F32)
        bup = sb("bups", [128, 64], F32)
        bdn = sb("bdns", [128, 16], F32)
        aupb = sb("aupb", [128, 256], BF16)
        adnb = sb("adnb", [128, 1024], BF16)
        stat = sb("stat", [128, 128], F32)
        small = sb("small", [128, 16], F32)
        rowb = sb("rowb", [1, 64], F32)
        fixb = sb("fixb", [128, 12], F32)

        POOLW = 50176
        poolt = sb("poolt", [128, POOLW], F32)
        pool = Pool(poolt)

        pool.push()
        ldt = pool.f32(T, parts=64)
        cuma = pool.f32(T, parts=64)
        b_fm = pool.bf16(T, parts=64)
        c_fm = pool.bf16(T, parts=64)
        fixtok = pool.bf16(CONVD, parts=3)
        sendtail = pool.f32(99)
        firstc = pool.f32(99)
        fixfb = pool.bf16(99)

        PA = ctx.enter_context(nc.psum_tensor("PA", [128, 2048], F32))
        PB = ctx.enter_context(nc.psum_tensor("PB", [128, 2048], F32))

        def ps(b, lo=0, hi=512, parts=128):
            tns = PA if b < 4 else PB
            b = b % 4
            return tns[0:parts, b * 512 + lo:b * 512 + hi]

        def pa(b, lo=0, hi=512, parts=128):
            return ps(b, lo, hi, parts)

        def pb(b, lo=0, hi=512, parts=128):
            return ps(b + 4, lo, hi, parts)

        block = ctx.enter_context(nc.Block())

        cst = fw.dsem("cst", batch=True)
        cstg = fw.dsem("cstg", batch=True)
        for dst, src in [(convw, convw_in), (convb, convb_in), (dtb, dtb_in),
                         (nw, nw_in), (isodd, isodd_in), (identb, identb_c),
                         (identf, identf_c), (maskc, mask_c), (ones1f, ones1f_c),
                         (bup, bup_in), (bdn, bdn_in)]:
            fw.op("q", (lambda d, s_: lambda e: e.dma_start(out=d[:], in_=s_))(dst, src),
                  writes=[d_name(dst)], dma=cst)
        for dst, src in [(aupb, aup_in), (adnb, adn_in)]:
            fw.op("g", (lambda d, s_: lambda e: e.dma_start(out=d[:], in_=s_))(dst, src),
                  writes=[d_name(dst)], dma=cstg)
        fw.op("g", lambda e: e.dma_start(out=small[0:64, 0:1], in_=alog_in),
              writes=["alog"], dma=cstg)
        fw.op("g", lambda e: e.dma_start(out=small[0:64, 1:2], in_=dcol_in),
              writes=["dcolv"], dma=cstg)
        fw.op("v", lambda e: e.memset(zeros64[:], 0.0), writes=["zeros64"])
        fw.op("s", lambda e: e.activation(out=acol[:], in_=small[0:64, 0:1],
                                          func=AF.Exp),
              reads=["alog"], writes=["acol0"])
        fw.op("v", lambda e: e.tensor_scalar_mul(acol[:], acol[:], -1.0),
              reads=["acol0"], writes=["acol"])
        fw.op("t", lambda e: e.transpose(pa(7, 0, 64, parts=1), small[0:64, 1:2],
                                         identf[0:64, 0:64]),
              reads=["dcolv", d_name(identf)], writes=[("PS", 7)])
        fw.op("v", lambda e: e.tensor_copy(out=rowb[:], in_=pa(7, 0, 64, parts=1)),
              reads=[("PS", 7)], writes=["rowb"])
        fw.op("t", lambda e: e.matmul(pa(6, 0, 64), ones1f[:], rowb[:],
                                      start=True, stop=True),
              reads=["rowb", d_name(ones1f)], writes=[("PS", 6)])
        fw.op("v", lambda e: e.tensor_copy(out=dbc[:], in_=pa(6, 0, 64)),
              reads=[("PS", 6)], writes=["dbc"])

        pool.push()
        ln1fm = pool.bf16(16 * 2048)
        xtile = [pool.f32(2048) for _ in range(2)]
        lntok = [pool.bf16(2048) for _ in range(2)]
        hbc = pool.f32(T + 3)
        cacc = pool.f32(T)
        xbf = [pool.bf16(T) for _ in range(2)]
        wbuf = [pool.bf16(2048) for _ in range(3)]
        dtraw = pool.f32(T, parts=64)
        dtmp = cacc[0:64, :]
        qbuf = [pool.bf16(2048) for _ in range(2)]
        wnat = [pool.bf16(2048) for _ in range(2)]
        scup = pool.f32(64 * 32)

        fw.op("q", lambda e: e.dma_start(out=scup, in_=scup_in),
              writes=["scup"], dma=cst)
        fw.op("v", lambda e: e.memset(hbc[:, 0:3], 0.0), writes=["hbc03"])

        def ln1fm_k(k):
            return ln1fm[:, k * 2048:(k + 1) * 2048]

        def emit_lnstats(src_key, src_ap, eps, c0):
            skeys = src_key if isinstance(src_key, list) else [src_key]
            for gidx in range(4):
                fw.op("v", (lambda g_: lambda e: e.bn_stats(
                    stat[:, c0 + 4 + g_ * 6:c0 + 10 + g_ * 6],
                    src_ap[:, g_ * 512:(g_ + 1) * 512]))(gidx),
                    reads=skeys, writes=[("st", c0, gidx)])
            fw.op("v", lambda e: e.bn_aggr(stat[:, c0 + 2:c0 + 4],
                                           stat[:, c0 + 4:c0 + 28]),
                  reads=[("st", c0, g) for g in range(4)], writes=[("mv", c0)])
            fw.op("v", (lambda: lambda e: e.tensor_scalar(
                out=stat[:, c0 + 28:c0 + 29], in0=stat[:, c0 + 3:c0 + 4],
                scalar1=1.0, scalar2=eps, op0=ALU.mult, op1=ALU.add))(),
                reads=[("mv", c0)], writes=[("ve", c0)])
            fw.op("v", lambda e: e.reciprocal(stat[:, c0 + 29:c0 + 30],
                                              stat[:, c0 + 28:c0 + 29]),
                  reads=[("ve", c0)], writes=[("rc", c0)])
            fw.op("s", lambda e: e.activation(out=stat[:, c0:c0 + 1],
                                              in_=stat[:, c0 + 29:c0 + 30],
                                              func=AF.Sqrt),
                  reads=[("rc", c0)], writes=[("rs", c0)])
            fw.op("v", lambda e: e.tensor_scalar(
                out=stat[:, c0 + 1:c0 + 2], in0=stat[:, c0 + 2:c0 + 3],
                scalar1=stat[:, c0:c0 + 1], scalar2=-1.0,
                op0=ALU.mult, op1=ALU.mult),
                reads=[("rs", c0), ("mv", c0)], writes=[("nb", c0)])

        if phases >= 1:
            for tt in range(16):
                xt = xtile[tt % 2]
                xk = ("xt", tt % 2)
                fw.op("q", (lambda tt_, xt_: lambda e: e.dma_start(
                    out=xt_, in_=x_in[tt_ * 128:(tt_ + 1) * 128, :]))(tt, xt),
                    writes=[xk], dma=fw.dsem(f"xt{tt % 2}"))
                emit_lnstats(xk, xt, 1e-5, 0)
                lt = lntok[tt % 2]
                lk = ("lt", tt % 2)
                fw.op("s", (lambda xt_, lt_: lambda e: e.activation(
                    out=lt_, in_=xt_, func=AF.Identity,
                    bias=stat[:, 1:2], scale=stat[:, 0:1]))(xt, lt),
                    reads=[xk, ("rs", 0), ("nb", 0)], writes=[lk])
                fw.op("q", (lambda tt_, lt_: lambda e: e.dma_start(
                    out=l1x_hbm[tt_ * 128:(tt_ + 1) * 128, :], in_=lt_))(tt, lt),
                    reads=[lk], writes=[("l1x", tt)], dma=fw.dsem(f"l1xS{tt % 2}"))
                for k in range(16):
                    bnk = k % 2
                    slot = (k // 2) % 4
                    fw.op("t", (lambda lt_, k_, b_, s_: lambda e: e.transpose(
                        ps(b_).bitcast(BF16)[:, s_ * 128:(s_ + 1) * 128],
                        lt_[:, k_ * 128:(k_ + 1) * 128], identb[:]))(lt, k, bnk, slot),
                        reads=[lk, d_name(identb)], writes=[("PS", bnk)])
                    fw.op("v", (lambda tt_, k_, b_, s_: lambda e: e.tensor_copy(
                        out=ln1fm_k(k_)[:, tt_ * 128:(tt_ + 1) * 128],
                        in_=ps(b_).bitcast(BF16)[:, s_ * 128:(s_ + 1) * 128]))(tt, k, bnk, slot),
                        reads=[("PS", bnk)], writes=[("ln1fm", k, tt)])

        def emit_qup_chunk(o):
            qb = qbuf[o % 2]
            wn = wnat[o % 2]
            fw.op("g", (lambda o_, qb_: lambda e: e.dma_start(
                out=qb_, in_=qup_in[o_ * 128:(o_ + 1) * 128, :]))(o, qb),
                writes=[("qb", o % 2)], dma=fw.dsem(f"qup{o % 2}"))
            fw.op("v", (lambda o_, qb_, wn_: lambda e: e.scalar_tensor_tensor(
                out=wn_.rearrange("p (b k) -> p b k", b=32),
                in0=qb_.rearrange("p (b k) -> p b k", b=32),
                scalar=-7.5,
                in1=scup[:, o_ * 32:(o_ + 1) * 32].unsqueeze(2).broadcast_to([128, 32, 64]),
                op0=ALU.add, op1=ALU.mult))(o, qb, wn),
                reads=[("qb", o % 2), "scup"], writes=[("wn", o % 2)])
            fw.op("q", (lambda o_, wn_: lambda e: e.dma_start(
                out=wupn_hbm[o_ * 128:(o_ + 1) * 128, :], in_=wn_))(o, wn),
                reads=[("wn", o % 2)], writes=[("wupn", o)],
                dma=fw.dsem(f"wupnS{o % 2}"))

        def a1_mms(o, img_src, psum_sel, ofree=128):
            wb = wbuf[o % 3]
            wbk = ("wb", o % 3)
            fw.op("g", (lambda s_, wb_, n_: lambda e: e.dma_start(
                out=wb_[:, 0:16 * n_], in_=s_))(img_src, wb, ofree),
                writes=[wbk], dma=fw.dsem(f"wb{o % 3}"))
            Pt = psum_sel
            for tt in range(4):
                for k in range(16):
                    first, last = k == 0, k == 15
                    rd = []
                    if first:
                        rd = [wbk] + [("ln1fm", kk, t4) for kk in range(16)
                                      for t4 in range(tt * 4, tt * 4 + 4)]
                    elif last:
                        rd = [wbk]
                    fw.op("t", (lambda wb_, k_, tt_, f_, l_, p_, n_: lambda e: e.matmul(
                        (pa if p_ == 0 else pb)(tt_, parts=n_),
                        wb_[:, k_ * n_:(k_ + 1) * n_],
                        ln1fm_k(k_)[:, tt_ * 512:(tt_ + 1) * 512],
                        start=f_, stop=l_))(wb, k, tt, first, last, Pt, ofree),
                        reads=rd,
                        writes=([("PS", Pt * 4 + tt)] if (first or last) else []))

        if phases >= 2:
            qo = 0
            for o in range(34):
                Pt = o % 2
                Pn = "PA" if Pt == 0 else "PB"
                if o < 33:
                    a1_mms(o, wh_img[o], Pt)
                else:
                    a1_mms(o, wd_img, Pt, ofree=64)
                for _ in range(2):
                    if qo < 64:
                        emit_qup_chunk(qo)
                        qo += 1
                if o < 33:
                    fw.op("v", (lambda p_: lambda e: e.tensor_copy(
                        out=hbc[:, 3:3 + T], in_=(PA if p_ == 0 else PB)[:, 0:T]))(Pt),
                        reads=[("PS", Pt * 4 + tt) for tt in range(4)],
                        writes=["hbcw"])
                    fw.op("v", (lambda o_: lambda e: e.tensor_copy(
                        out=sendtail[:, o_ * 3:(o_ + 1) * 3], in_=hbc[:, 2048:2051]))(o),
                        reads=["hbcw"], writes=[("stl", o)])
                    fw.op("v", (lambda o_: lambda e: e.tensor_copy(
                        out=firstc[:, o_ * 3:(o_ + 1) * 3], in_=hbc[:, 3:6]))(o),
                        reads=["hbcw"], writes=[("fst", o)])
                    fw.op("s", (lambda o_: lambda e: e.activation(
                        out=cacc, in_=hbc[:, 0:T], func=AF.Identity,
                        scale=convw[:, o_ * 4:o_ * 4 + 1],
                        bias=convb[:, o_:o_ + 1]))(o),
                        reads=["hbcw", "hbc03", d_name(convw), d_name(convb)],
                        writes=["cacc"])
                    for kk in range(1, 4):
                        fw.op("v", (lambda o_, kk_: lambda e: e.scalar_tensor_tensor(
                            out=cacc, in0=hbc[:, kk_:T + kk_],
                            scalar=convw[:, o_ * 4 + kk_:o_ * 4 + kk_ + 1],
                            in1=cacc, op0=ALU.mult, op1=ALU.add))(o, kk),
                            reads=["hbcw"], rw=["cacc"])
                    xb = xbf[o % 2]
                    fw.op("s", (lambda xb_: lambda e: e.activation(
                        out=xb_, in_=cacc, func=AF.Silu))(xb),
                        reads=["cacc"], writes=[("xbf", o % 2)])
                    fw.op("q", (lambda o_, xb_: lambda e: e.dma_start(
                        out=xfm_hbm[o_ * 128:(o_ + 1) * 128, :], in_=xb_))(o, xb),
                        reads=[("xbf", o % 2)], writes=[("xfm", o)],
                        dma=fw.dsem(f"xfmS{o % 2}"))
                else:
                    fw.op("v", (lambda p_: lambda e: e.tensor_copy(
                        out=dtraw, in_=(PA if p_ == 0 else PB)[0:64, 0:T]))(Pt),
                        reads=[("PS", Pt * 4 + tt) for tt in range(4)], writes=["dtraw"])
            fw.op("q", lambda e: e.dma_start(out=b_fm[:], in_=xfm_hbm[4096:4160, :]),
                  reads=[("xfm", 32)], writes=["bfm"], dma=fw.dsem("bS"))
            fw.op("q", lambda e: e.dma_start(out=c_fm[:], in_=xfm_hbm[4160:4224, :]),
                  reads=[("xfm", 32)], writes=["cfm"], dma=fw.dsem("cS"))
            dtmp2 = hbc[0:64, 0:2048]
            fw.op("v", lambda e: e.tensor_scalar(
                out=dtraw, in0=dtraw, scalar1=dtb[:, 0:1], scalar2=None,
                op0=ALU.add),
                reads=[d_name(dtb)], rw=["dtraw"])
            fw.op("v", lambda e: e.tensor_scalar(
                out=dtmp, in0=dtraw, scalar1=30.0, scalar2=None, op0=ALU.min),
                reads=["dtraw"], writes=["dtc", "cacc"])
            fw.op("s", lambda e: e.activation(out=dtmp, in_=dtmp, func=AF.Exp),
                  rw=["dtc", "cacc"], writes=["dte"])
            fw.op("s", lambda e: e.activation(out=dtmp, in_=dtmp, func=AF.Ln,
                                              bias=1.0),
                  rw=["dte", "cacc"], writes=["dtp0"])
            fw.op("v", lambda e: e.tensor_scalar(
                out=dtmp2, in0=dtraw, scalar1=-30.0, scalar2=0.0,
                op0=ALU.add, op1=ALU.max),
                reads=["dtraw"], writes=["dtr30", "hbcw", "hbc03"])
            fw.op("v", lambda e: e.tensor_add(dtmp, dtmp, dtmp2),
                  reads=["dtr30", "dtp0"], rw=["dte", "cacc"], writes=["dtp"])
            fw.op("v", lambda e: e.tensor_scalar(
                out=dtraw, in0=dtmp, scalar1=acol[:, 0:1], scalar2=None,
                op0=ALU.mult),
                reads=["dtp", "acol"], writes=["a_fm"])
            fw.op("v", lambda e: e.tensor_scalar(
                out=dtmp, in0=dtmp, scalar1=1e-30, scalar2=None, op0=ALU.max),
                rw=["dtp", "cacc"], writes=["dtpc"])
            fw.op("s", lambda e: e.activation(out=ldt[:], in_=dtmp, func=AF.Ln),
                  reads=["dtpc"], writes=["ldt"])
            for q in range(NCH):
                fw.op("v", (lambda q_: lambda e: e.tensor_tensor_scan(
                    out=cuma[:, q_ * 128:(q_ + 1) * 128],
                    data0=dtraw[:, q_ * 128:(q_ + 1) * 128],
                    data1=zeros64[:], initial=0.0,
                    op0=ALU.add, op1=ALU.add))(q),
                    reads=["a_fm", "zeros64"], writes=[("cuma", q)])

        if phases >= 3:
            fw.op("q", lambda e: e.dma_start(out=ex1_in[0:128, 0:99], in_=sendtail[:]),
                  reads=[("stl", o) for o in range(33)], writes=["ex1i"],
                  dma=fw.dsem("ex1S"))
            fw.op("g", lambda e: e.collective_compute(
                "AllGather", ALU.bypass, replica_groups=RG,
                ins=[ex1_in], outs=[ex1_out]),
                reads=["ex1i"], writes=["ex1o"])
            fw.op("q", lambda e: e.dma_start(out=sendtail[:], in_=ex1_out[0:128, 0:99]),
                  reads=["ex1o"],
                  writes=["rtail"] + [("stl", o) for o in range(33)], dma="ex1S")
            fw.op("v", lambda e: e.tensor_scalar(
                out=sendtail[:], in0=sendtail[:], scalar1=isodd[:, 0:1],
                scalar2=None, op0=ALU.mult),
                reads=[d_name(isodd)], rw=["rtail"])
            pool.push()
            bufall = pool.f32(198)
            tmpb = pool.f32(99)
            facc = pool.f32(99)
            bav = bufall.rearrange("p (o t) -> p o t", o=33)
            cwv = convw.rearrange("p (o k) -> p o k", k=4)
            fw.op("v", lambda e: e.tensor_copy(
                out=bav[:, :, 0:3], in_=sendtail.rearrange(
                    "p (o t) -> p o t", o=33)),
                reads=["rtail"], writes=["bufA"])
            fw.op("v", lambda e: e.tensor_copy(
                out=bav[:, :, 3:6], in_=firstc.rearrange(
                    "p (o t) -> p o t", o=33)),
                reads=[("fst", o) for o in range(33)], writes=["bufB"])
            fv = facc.rearrange("p (o t) -> p o t", o=33)
            tv = tmpb.rearrange("p (o t) -> p o t", o=33)
            fw.op("v", lambda e: e.tensor_mul(
                fv, bav[:, :, 0:3],
                cwv[:, :, 0].unsqueeze(2).broadcast_to([128, 33, 3])),
                reads=["bufA", d_name(convw)], writes=["facc"])
            fw.op("v", lambda e: e.tensor_add(
                fv, fv, convb[:].unsqueeze(2).broadcast_to([128, 33, 3])),
                reads=[d_name(convb)], rw=["facc"])
            for kk in range(1, 4):
                fw.op("v", (lambda kk_: lambda e: e.tensor_mul(
                    tv, bufall.rearrange("p (o t) -> p o t", o=33,
                                         )[:, :, kk_:kk_ + 3],
                    cwv[:, :, kk_].unsqueeze(2).broadcast_to([128, 33, 3])))(kk),
                    reads=["bufA", "bufB", d_name(convw)], writes=["tmpb"])
                fw.op("v", lambda e: e.tensor_add(fv, fv, tv),
                      reads=["tmpb"], rw=["facc"])
            fw.op("s", lambda e: e.activation(
                out=fixfb[:, 0:99], in_=facc, func=AF.Silu),
                reads=["facc"], writes=["fixf"] + [("fixf", o) for o in range(33)])
            for o in range(33):
                r, bnk = o // 16, 4 + (o // 8) % 2
                fw.op("t", (lambda o_, b_: lambda e: e.transpose(
                    ps(b_).bitcast(BF16)[0:3, (o_ % 8) * 128:(o_ % 8) * 128 + 128],
                    fixfb[:, o_ * 3:(o_ + 1) * 3], identb[:]))(o, bnk),
                    reads=["fixf"]
                    + ([("ftkE", bnk, o // 16 - 1)] if o >= 16 else []),
                    writes=[("PSf", bnk, o // 16)])
                if o % 8 == 7 or o == 32:
                    g8 = o // 8
                    lo_ = (g8 % 2) * 8 * 128
                    n_ = (o % 8 + 1) * 128
                    fw.op("v", (lambda o_, b_, g8_, n2: lambda e: e.tensor_copy(
                        out=fixtok[0:3, (g8_ * 8) * 128:(g8_ * 8) * 128 + n2],
                        in_=ps(b_).bitcast(BF16)[0:3, 0:n2]))(o, bnk, g8, n_),
                        reads=[("PSf", bnk, o // 16)],
                        writes=[("ftkE", bnk, o // 16)]
                        + [("ftk", oo) for oo in range(g8 * 8, o + 1)])
            pool.pop()
            fw.op("v", lambda e: e.tensor_copy(out=b_fm[:, 0:3],
                                               in_=fixfb[0:64, 96:99]),
                  reads=[("fixf", 32)], rw=["bfm"])
            fw.op("t", lambda e: e.transpose(
                pa(5).bitcast(BF16)[0:64, 0:3],
                fixtok[0:3, 4160:4224], identb[0:3, 0:3]),
                reads=[("ftk", 32), d_name(identb)], writes=[("PS", 5)])
            fw.op("v", lambda e: e.tensor_copy(out=c_fm[:, 0:3],
                                               in_=pa(5).bitcast(BF16)[0:64, 0:3]),
                  reads=[("PS", 5)], rw=["cfm"])

        dsd = fw.dsem("dmpS", batch=True)

        def dump(nm, src_ap, rkeys):
            if nm not in dumps:
                return
            t_ = nc.dram_tensor("dmp_" + nm, list(src_ap.shape),
                                src_ap.dtype, kind="ExternalOutput").ap()
            fw.op("q", (lambda a_, t2: lambda e: e.dma_start(out=t2, in_=a_))(
                src_ap, t_), reads=rkeys, dma=dsd)

        if phases >= 2:
            dump("ln1fm", ln1fm[:], [("ln1fm", k, tt) for k in range(16)
                                     for tt in range(16)])
            dump("cuma", cuma[:], [("cuma", q) for q in range(NCH)])
            dump("ldt", ldt[:], ["ldt"])
            dump("a_fm", dtraw[:], ["a_fm"])
        if phases >= 3:
            dump("bfm", b_fm[:], ["bfm"])
            dump("cfm", c_fm[:], ["cfm"])
            dump("fixtok", fixtok[:], [("ftk", o) for o in range(33)])

        fw.barrier()
        pool.pop()

        if phases >= 4:
            pool.push()
            xtok = [pool.bf16(CONVD) for _ in range(2)]
            xwb = [pool.bf16(4096) for _ in range(2)]
            sstate = pool.f32(4096, parts=64)
            btokb = [pool.bf16(64) for _ in range(2)]
            wexpf = pool.f32(128, parts=64)
            wexpt = pool.f32(64)
            dacbb = [pool.f32(64, parts=64) for _ in range(2)]
            qd2 = [pool.bf16(2048) for _ in range(2)]
            wdn2 = [pool.bf16(2048) for _ in range(2)]
            scdn = pool.f32(2048)
            wtrb6 = [pool.bf16(8192) for _ in range(2)]

            def emit_wupT(k):
                wt = wtrb6[k % 2]
                tg = f"wtrG{k % 2}"
                fw.dma_begin(tg)
                for q2 in range(4):
                    fw.op("s", (lambda k_, q3, w_: lambda e: e.dma_start_transpose(
                        w_[:, q3 * 2048:(q3 + 1) * 2048],
                        wupn_hbm[q3 * 2048:(q3 + 1) * 2048,
                                 k_ * 128:(k_ + 1) * 128]))(k, q2, wt),
                        reads=([("wupn", oo) for oo in range(64)]
                               if (k == 0 and q2 == 0) else [])
                        + ([("wupiq", k - 2, q2)] if k >= 2 else []),
                        writes=[("wtrp", k % 2, q2)], dma=tg, group=True)
                fw.dma_end(tg, extra_keys=[("wtr", k % 2)])
                gs = f"wupiG{k % 2}"
                fw.dma_begin(gs)
                for qq in range(4):
                    dst = wupi_hbm.rearrange("o p j -> p o j")[
                        :, qq * 16:(qq + 1) * 16, k * 128:(k + 1) * 128]
                    fw.op("g", (lambda d_, k_, q2, w_: lambda e: e.dma_start(
                        out=d_, in_=w_[:, q2 * 2048:(q2 + 1) * 2048]
                        .rearrange("p (o j) -> p o j", o=16),
                        allow_slow_non_contiguous=True))(dst, k, qq, wt),
                        reads=[("wtr", k % 2)], writes=[("wupiq", k, qq)],
                        dma=gs, group=True)
                fw.dma_end(gs, extra_keys=[("wupik", k)])

            fw.op("q", lambda e: e.dma_start(out=scdn, in_=scdn_in),
                  writes=["scdn"], dma=fw.dsem("scdnS"))
            fw.op("v", lambda e: e.memset(sstate, 0.0),
                  writes=[("sst", w2) for w2 in range(4)])

            qdi = 0

            def emit_qdn_piece(idx):
                o, cq = idx // 4, idx % 4
                qb = qd2[idx % 2]
                wn = wdn2[idx % 2]
                fw.op("g", (lambda o_, cq_, qb_: lambda e: e.dma_start(
                    out=qb_, in_=qdn_in[o_ * 128:(o_ + 1) * 128,
                                        cq_ * 2048:(cq_ + 1) * 2048]))(o, cq, qb),
                    writes=[("qd", idx % 2)], dma=fw.dsem(f"qdn{idx % 2}"))
                fw.op("v", (lambda o_, cq_, qb_, wn_: lambda e: e.scalar_tensor_tensor(
                    out=wn_.rearrange("p (b k) -> p b k", b=32),
                    in0=qb_.rearrange("p (b k) -> p b k", b=32),
                    scalar=-7.5,
                    in1=scdn[:, o_ * 128 + cq_ * 32:o_ * 128 + (cq_ + 1) * 32]
                        .unsqueeze(2).broadcast_to([128, 32, 64]),
                    op0=ALU.add, op1=ALU.mult))(o, cq, qb, wn),
                    reads=[("qd", idx % 2), "scdn"], writes=[("wdn", idx % 2)])
                fw.op("q", (lambda o_, cq_, wn_: lambda e: e.dma_start(
                    out=wdnn_hbm[o_ * 128:(o_ + 1) * 128,
                                 cq_ * 2048:(cq_ + 1) * 2048], in_=wn_))(o, cq, wn),
                    reads=[("wdn", idx % 2)], writes=[("wdnn", idx)],
                    dma=fw.dsem(f"wdnnS{idx % 2}"))

            XKEYS = [("xfm", o) for o in range(33)]

            def emit_prep4(q):
                xi = q % 2
                fw.op("v", (lambda q_: lambda e: e.tensor_sub(
                    wexpf, ldt[:, q_ * 128:(q_ + 1) * 128],
                    cuma[:, q_ * 128:(q_ + 1) * 128]))(q),
                    reads=["ldt", ("cuma", q)], writes=["wef"])
                fw.op("v", (lambda q_: lambda e: e.tensor_scalar(
                    out=wexpf, in0=wexpf,
                    scalar1=cuma[:, q_ * 128 + 127:q_ * 128 + 128],
                    scalar2=None, op0=ALU.add))(q),
                    reads=[("cuma", q)], rw=["wef"])
                fw.op("s", lambda e: e.activation(out=wexpf, in_=wexpf, func=AF.Exp),
                      rw=["wef"])
                tg = f"xtkG{xi}"
                fw.dma_begin(tg)
                for r2 in range(2):
                    fw.op("s", (lambda q_, xi_, r3: lambda e: e.dma_start_transpose(
                        xtok[xi_][:, r3 * 2112:(r3 + 1) * 2112],
                        xfm_hbm[r3 * 2112:(r3 + 1) * 2112,
                                q_ * 128:(q_ + 1) * 128]))(q, xi, r2),
                        reads=(XKEYS if (q == 0 and r2 == 0) else [])
                        + ([("xwm", xi)] if q >= 2 else []),
                        writes=[("xtkp", xi, r2)], dma=tg, group=True)
                fw.dma_end(tg, extra_keys=[("xtok", xi)])
                if q == 0:
                    fw.op("v", (lambda xi_: lambda e: e.tensor_copy(
                        out=xtok[xi_][0:3, :], in_=fixtok[0:3, :]))(xi),
                        rw=[("xtok", xi)])
                fw.op("t", lambda e: e.transpose(ps(6, 0, 64), wexpf,
                                                 identf[0:64, 0:64]),
                      reads=["wef", d_name(identf)], writes=[("PS", 6)])
                fw.op("v", lambda e: e.tensor_copy(out=wexpt, in_=ps(6, 0, 64)),
                      reads=[("PS", 6)], writes=["wet"])
                fw.op("v", (lambda xi_: lambda e: e.tensor_mul(
                    xwb[xi_].rearrange("p (b k) -> p b k", b=64),
                    xtok[xi_][:, 0:4096].rearrange("p (b k) -> p b k", b=64),
                    wexpt.unsqueeze(2).broadcast_to([128, 64, 64])))(xi),
                    reads=[("xtok", xi), "wet"], writes=[("xw", xi), ("xwm", xi)])
                fw.op("t", (lambda q_: lambda e: e.transpose(
                    ps(7, 0, 32).bitcast(BF16)[:, 0:64],
                    b_fm[:, q_ * 128:(q_ + 1) * 128], identb[0:64, 0:64]))(q),
                    reads=["bfm", d_name(identb)], writes=[("PS", 7)])
                fw.op("v", (lambda xi_: lambda e: e.tensor_copy(
                    out=btokb[xi_], in_=ps(7, 0, 32).bitcast(BF16)[:, 0:64]))(xi),
                    reads=[("PS", 7)], writes=[("btk", xi)])
                fw.op("t", (lambda q_: lambda e: e.transpose(
                    ps(6, 64, 128, parts=1),
                    cuma[:, q_ * 128 + 127:q_ * 128 + 128],
                    identf[0:64, 0:64]))(q),
                    reads=[("cuma", q), d_name(identf)], writes=[("PS", 6)])
                fw.op("v", lambda e: e.tensor_copy(out=rowb[0:1, 0:64],
                                                   in_=ps(6, 64, 128, parts=1)),
                      reads=[("PS", 6)], writes=["rowb"])
                fw.op("t", lambda e: e.matmul(ps(6, 128, 192, parts=64),
                                              ones1f[0:1, 0:64], rowb[0:1, 0:64],
                                              start=True, stop=True),
                      reads=["rowb", d_name(ones1f)], writes=[("PS", 6)])
                fw.op("s", (lambda xi_: lambda e: e.activation(
                    out=dacbb[xi_], in_=ps(6, 128, 192, parts=64),
                    func=AF.Exp))(xi),
                    reads=[("PS", 6)], writes=[("dacb", xi)])

            def emit_waves4(q, wlist):
                xi = q % 2
                for w in wlist:
                    bbase = (w % 2) * 2
                    for hh in range(16):
                        h = w * 16 + hh
                        b_ = bbase + hh // 8
                        lo = (hh % 8) * 64
                        fw.op("t", (lambda h_, b2, lo_, xi_: lambda e: e.matmul(
                            ps(b2, lo_, lo_ + 64, parts=64),
                            btokb[xi_], xwb[xi_][:, h_ * 64:(h_ + 1) * 64],
                            start=True, stop=True))(h, b_, lo, xi),
                            reads=([("btk", xi), ("xw", xi)]
                                   if hh in (0, 15) else []),
                            writes=([("PS", b2) for b2 in (bbase, bbase + 1)]
                                    if hh == 0 else
                                    ([("PS", b_)] if hh % 8 == 7 else [])))
                    fw.op("v", (lambda w_, xi_: lambda e: e.tensor_mul(
                        sstate[:, w_ * 1024:(w_ + 1) * 1024].rearrange(
                            "p (b k) -> p b k", b=16),
                        sstate[:, w_ * 1024:(w_ + 1) * 1024].rearrange(
                            "p (b k) -> p b k", b=16),
                        dacbb[xi_][:, w_ * 16:(w_ + 1) * 16].unsqueeze(2)
                        .broadcast_to([64, 16, 64])))(w, xi),
                        reads=[("dacb", xi)], rw=[("sst", w)])
                    fw.op("v", (lambda w_: lambda e: e.tensor_add(
                        sstate[:, w_ * 1024:(w_ + 1) * 1024],
                        sstate[:, w_ * 1024:(w_ + 1) * 1024],
                        PA[0:64, (w_ % 2) * 1024:(w_ % 2) * 1024 + 1024]))(w),
                        reads=[("PS", bbase), ("PS", bbase + 1)], rw=[("sst", w)])
                    if w % 2 == 1:
                        half = w // 2
                        fw.op("q", (lambda q2, h2: lambda e: e.dma_start(
                            out=s_hbm[q2][:, h2 * 2048:(h2 + 1) * 2048],
                            in_=sstate[:, h2 * 2048:(h2 + 1) * 2048]))(q, half),
                            reads=[("sst", half * 2), ("sst", half * 2 + 1)],
                            writes=[("shbm", q, half)],
                            dma=fw.dsem(f"shS{half}"))

            for q in range(NCH):
                if q == 0:
                    emit_prep4(0)
                emit_waves4(q, [0, 1])
                if q + 1 < NCH:
                    emit_prep4(q + 1)
                emit_waves4(q, [2, 3])
                for _ in range(4):
                    if qdi < 64:
                        emit_qdn_piece(qdi)
                        qdi += 1
                if q < 16:
                    emit_wupT(q)
            dump("sfin", sstate, [("sst", w2) for w2 in range(4)])
            fw.op("q", lambda e: e.dma_start(out=ex2_in, in_=sstate),
                  reads=[("sst", w2) for w2 in range(4)], writes=["ex2i"],
                  dma=fw.dsem("ex2S"))
            fw.op("g", lambda e: e.collective_compute(
                "AllGather", ALU.bypass, replica_groups=RG,
                ins=[ex2_in], outs=[ex2_out]),
                reads=["ex2i"], writes=["ex2o"])
            fw.barrier()
            pool.pop()

        if phases >= 5:
            pool.push()
            xtok = [pool.bf16(CONVD) for _ in range(2)]
            ysb = pool.f32(4096)
            yntok = pool.bf16(4096)
            sinb = pool.f32(4096, parts=64)
            seffb = [pool.bf16(4096, parts=64) for _ in range(2)]
            sloc = [pool.bf16(4096, parts=64) for _ in range(2)]
            cbmb = [pool.bf16(128) for _ in range(2)]
            diall = pool.bf16(64 * 128)
            dclg = [pool.f32(512) for _ in range(4)]
            expg = [pool.f32(512) for _ in range(4)]
            msTg = [pool.bf16(512) for _ in range(16)]
            cumatb = [pool.f32(64) for _ in range(2)]
            etokb = [pool.f32(64) for _ in range(2)]
            gqb = [pool.f32(64, parts=64) for _ in range(2)]
            ccol = pool.f32(1, parts=64)
            tmpcb = [pool.f32(128, parts=64) for _ in range(2)]
            wt2 = [pool.bf16(2048) for _ in range(2)]

            def emit_wdnT(o):
                w_ = wt2[o % 2]
                fw.op("s", (lambda o_, w2: lambda e: e.dma_start_transpose(
                    w2, wdnn_hbm[:, o_ * 128:(o_ + 1) * 128]))(o, w_),
                    reads=([("wdnn", ii) for ii in range(64)] if o == 0 else []),
                    writes=[("wt2", o % 2)], dma=fw.dsem(f"wt2{o % 2}"))
                dst = wdni_hbm.rearrange("g p o j -> p g o j")[:, :, o, :]
                fw.op("q", (lambda d_, o_, w2: lambda e: e.dma_start(
                    out=d_, in_=w2.rearrange("p (g j) -> p g j", g=16),
                    allow_slow_non_contiguous=True))(dst, o, w_),
                    reads=[("wt2", o % 2)], writes=[("wdni", o)],
                    dma=fw.dsem(f"wdi{o % 2}"))

            for h in range(NH):
                fw.op("v", (lambda h_: lambda e: e.tensor_scalar(
                    out=diall[:, h_ * 128:(h_ + 1) * 128], in0=identb[:],
                    scalar1=dbc[:, h_:h_ + 1], scalar2=None, op0=ALU.mult))(h),
                    reads=[d_name(identb), "dbc"], writes=[("dia", h)])
            fw.op("q", lambda e: e.dma_start(out=sinb, in_=ex2_out[0:64, :]),
                  writes=["sin0"], dma=fw.dsem("sinS"))
            fw.op("v", lambda e: e.tensor_scalar(
                out=sinb, in0=sinb, scalar1=isodd[0:64, 0:1], scalar2=None,
                op0=ALU.mult),
                reads=["sin0", d_name(isodd)], writes=["sin"])
            fw.op("v", lambda e: e.memset(sloc[0], 0.0), writes=[("sloc", 0)])
            fw.op("v", lambda e: e.memset(ccol, 0.0), writes=["ccol"])
            wdni_cnt = [0]

            def emit_xtok5(q):
                xi = q % 2
                tg = f"xtkG{xi}"
                fw.dma_begin(tg)
                for r2 in range(2):
                    fw.op("s", (lambda q_, xi_, r3: lambda e: e.dma_start_transpose(
                        xtok[xi_][:, r3 * 2112:(r3 + 1) * 2112],
                        xfm_hbm[r3 * 2112:(r3 + 1) * 2112,
                                q_ * 128:(q_ + 1) * 128]))(q, xi, r2),
                        reads=([("xtk5", xi)] if q >= 2 else []),
                        writes=[("xtkp", xi, r2)], dma=tg, group=True)
                fw.dma_end(tg, extra_keys=[("xtok", xi)])
                if q == 0:
                    fw.op("v", (lambda xi_: lambda e: e.tensor_copy(
                        out=xtok[xi_][0:3, :], in_=fixtok[0:3, :]))(xi),
                        rw=[("xtok", xi)])

            def emit_prep5(q):
                pr = q % 2
                if q >= 1:
                    fw.op("g", (lambda q_, pr_: lambda e: e.dma_start(
                        out=sloc[pr_], in_=s_hbm[q_ - 1]))(q, pr),
                        writes=[("sloc", pr)], dma=fw.dsem(f"slc{pr}"))
                    fw.op("v", (lambda q_: lambda e: e.tensor_scalar(
                        out=ccol, in0=ccol,
                        scalar1=cuma[:, (q_ - 1) * 128 + 127:(q_ - 1) * 128 + 128],
                        scalar2=None, op0=ALU.add))(q),
                        rw=["ccol"])
                fw.op("t", lambda e: e.transpose(ps(0, 0, 64, parts=1), ccol,
                                                 identf[0:64, 0:64]),
                      reads=["ccol", d_name(identf)], writes=[("PS", 0)])
                fw.op("v", lambda e: e.tensor_copy(out=rowb[0:1, 0:64],
                                                   in_=ps(0, 0, 64, parts=1)),
                      reads=[("PS", 0)], writes=["rowb"])
                fw.op("t", lambda e: e.matmul(ps(0, 64, 128, parts=64),
                                              ones1f[0:1, 0:64], rowb[0:1, 0:64],
                                              start=True, stop=True),
                      reads=["rowb", d_name(ones1f)], writes=[("PS", 0)])
                fw.op("s", (lambda pr_: lambda e: e.activation(
                    out=gqb[pr_], in_=ps(0, 64, 128, parts=64),
                    func=AF.Exp))(pr),
                    reads=[("PS", 0)], writes=[("gq", pr)])
                fw.op("v", (lambda pr_: lambda e: e.tensor_mul(
                    seffb[pr_].rearrange("p (b k) -> p b k", b=64),
                    sinb.rearrange("p (b k) -> p b k", b=64),
                    gqb[pr_].unsqueeze(2).broadcast_to([64, 64, 64])))(pr),
                    reads=["sin", ("gq", pr)], writes=[("seff", pr)])
                fw.op("v", (lambda pr_: lambda e: e.tensor_add(
                    seffb[pr_], seffb[pr_], sloc[pr_]))(pr),
                    reads=[("sloc", pr)], rw=[("seff", pr)])
                fw.op("v", (lambda q_, pr_: lambda e: e.tensor_sub(
                    tmpcb[pr_], ldt[:, q_ * 128:(q_ + 1) * 128],
                    cuma[:, q_ * 128:(q_ + 1) * 128]))(q, pr),
                    reads=["ldt", ("cuma", q)], writes=[("tmpc", pr)])
                fw.op("t", (lambda q_: lambda e: e.transpose(
                    ps(0, 192, 256), cuma[:, q_ * 128:(q_ + 1) * 128],
                    identf[0:64, 0:64]))(q),
                    reads=[("cuma", q), d_name(identf)], writes=[("PS", 0)])
                fw.op("v", (lambda pr_: lambda e: e.tensor_copy(
                    out=cumatb[pr_], in_=ps(0, 192, 256)))(pr),
                    reads=[("PS", 0)], writes=[("cumat", pr)])
                fw.op("s", (lambda pr_: lambda e: e.activation(
                    out=etokb[pr_], in_=cumatb[pr_], func=AF.Exp))(pr),
                    reads=[("cumat", pr)], writes=[("etok", pr)])
                fw.op("t", (lambda q_: lambda e: e.matmul(
                    ps(1, 0, 128), b_fm[:, q_ * 128:(q_ + 1) * 128],
                    c_fm[:, q_ * 128:(q_ + 1) * 128],
                    start=True, stop=True))(q),
                    reads=["bfm", "cfm"], writes=[("PS", 1)])
                fw.op("v", (lambda pr_: lambda e: e.tensor_mul(
                    cbmb[pr_], ps(1, 0, 128), maskc[:]))(pr),
                    reads=[("PS", 1), d_name(maskc)], writes=[("cbm", pr)])

            def ecol(h_):
                return identf[0:64, h_:h_ + 1].broadcast_to([64, 128])

            def emit_M5(q):
                pr = q % 2
                for g4 in range(16):
                    dp = 6 + (g4 % 2)
                    gi = g4 % 4
                    gm = g4
                    for hh in range(4):
                        h = g4 * 4 + hh
                        fw.op("t", (lambda q_, h_, dp_, s_: lambda e: e.matmul(
                            ps(dp_, s_ * 128, (s_ + 1) * 128), ecol(h_),
                            cuma[:, q_ * 128:(q_ + 1) * 128],
                            start=True, stop=False))(q, h, dp, hh),
                            reads=([("cuma", q), d_name(identf)] if hh == 0 else []),
                            writes=([("PS", dp)] if hh == 0 else []))
                        fw.op("t", (lambda h_, dp_, s_, pr_: lambda e: e.matmul(
                            ps(dp_, s_ * 128, (s_ + 1) * 128), tmpcb[pr_], ecol(h_),
                            start=False, stop=True))(h, dp, hh, pr),
                            reads=([("tmpc", pr)] if hh == 0 else []),
                            writes=([("PS", dp)] if hh == 3 else []))
                    fw.op("v", (lambda dp_, g_: lambda e: e.tensor_scalar(
                        out=dclg[g_], in0=ps(dp_, 0, 512),
                        scalar1=80.0, scalar2=None, op0=ALU.min))(dp, gi),
                        reads=[("PS", dp)], writes=[("dcl", gi)])
                    fw.op("s", (lambda g_: lambda e: e.activation(
                        out=expg[g_], in_=dclg[g_], func=AF.Exp))(gi),
                        reads=[("dcl", gi)], writes=[("expg", gi)])
                    fw.op("g", (lambda g_, gm_, pr_: lambda e: e.tensor_mul(
                        msTg[gm_].rearrange("p (a b) -> p a b", a=4),
                        expg[g_].rearrange("p (a b) -> p a b", a=4),
                        cbmb[pr_].unsqueeze(1).broadcast_to([128, 4, 128])))(
                        gi, gm, pr),
                        reads=[("expg", gi), ("cbm", pr)], writes=[("msT", gm)])
                    fw.op("g", (lambda gm_, g4_: lambda e: e.tensor_add(
                        msTg[gm_], msTg[gm_],
                        diall[:, g4_ * 512:(g4_ + 1) * 512]))(gm, g4),
                        reads=[("dia", g4 * 4)], rw=[("msT", gm)])

            def emit_Y5(q):
                pr = q % 2
                xi = q % 2
                for g4 in range(16):
                    gm = g4
                    bi = 2 + ((g4 // 2) % 2) * 2
                    for hh in range(4):
                        h = g4 * 4 + hh
                        lo = (h % 8) * 64
                        fw.op("t", (lambda h_, b_, lo_, xi_, g_, s_: lambda e: e.matmul(
                            ps(b_, lo_, lo_ + 64), msTg[g_][:, s_ * 128:(s_ + 1) * 128],
                            xtok[xi_][:, h_ * 64:(h_ + 1) * 64],
                            start=True, stop=True))(h, bi, lo, xi, gm, hh),
                            reads=[("msT", gm), ("xtok", xi)],
                            rw=([("xtk5", xi)] if (g4 == 15 and hh == 3) else []),
                            writes=([("PS", bi)]
                                    if ((g4 % 2 == 0 and hh == 0) or
                                        (g4 % 2 == 1 and hh == 3)) else []))
                        fw.op("t", (lambda q_, h_, b_, lo_, pr_: lambda e: e.matmul(
                            ps(b_, lo_, lo_ + 64), c_fm[:, q_ * 128:(q_ + 1) * 128],
                            seffb[pr_][:, h_ * 64:(h_ + 1) * 64],
                            start=True, stop=True))(q, h, bi + 1, lo, pr),
                            reads=([("seff", pr), "cfm"]
                                   if (h < 4 or h == 63) else []),
                            writes=([("PS", bi + 1)]
                                    if ((g4 % 2 == 0 and hh == 0) or
                                        (g4 % 2 == 1 and hh == 3)) else []))
                    if g4 % 2 == 1:
                        g8 = g4 // 2
                        fw.op("v", (lambda g8_, b_, pr_: lambda e: e.tensor_mul(
                            ysb[:, g8_ * 512:(g8_ + 1) * 512]
                            .rearrange("p (a b) -> p a b", a=8),
                            ps(b_ + 1, 0, 512).rearrange("p (a b) -> p a b", a=8),
                            etokb[pr_][:, g8_ * 8:(g8_ + 1) * 8].unsqueeze(2)
                            .broadcast_to([128, 8, 64])))(g8, bi, pr),
                            reads=[("PS", bi + 1), ("etok", pr)],
                            writes=[("ysb", g8)])
                        fw.op("v", (lambda g8_, b_: lambda e: e.tensor_add(
                            ysb[:, g8_ * 512:(g8_ + 1) * 512],
                            ysb[:, g8_ * 512:(g8_ + 1) * 512],
                            ps(b_, 0, 512)))(g8, bi),
                            reads=[("PS", bi)], rw=[("ysb", g8)])

            def emit_rms5(q):
                fw.op("s", lambda e: e.activation(out=yntok, in_=ysb, func=AF.Square,
                                                  accum_out=stat[:, 120:121]),
                      reads=[("ysb", g2) for g2 in range(8)], writes=["ss", "yn"])
                fw.op("v", lambda e: e.tensor_scalar(
                    out=stat[:, 121:122], in0=stat[:, 120:121], scalar1=1.0 / 4096,
                    scalar2=1e-6, op0=ALU.mult, op1=ALU.add),
                    reads=["ss"], writes=["ssv"])
                fw.op("v", lambda e: e.reciprocal(stat[:, 122:123], stat[:, 121:122]),
                      reads=["ssv"], writes=["ssr"])
                fw.op("s", lambda e: e.activation(out=stat[:, 123:124],
                                                  in_=stat[:, 122:123], func=AF.Sqrt),
                      reads=["ssr"], writes=["rms"])
                fw.op("s", lambda e: e.activation(out=yntok, in_=ysb,
                                                  func=AF.Identity,
                                                  scale=stat[:, 123:124]),
                      reads=[("ysb", g2) for g2 in range(8)] + ["rms"],
                      rw=["yn"])
                fw.op("q", (lambda q_: lambda e: e.dma_start(
                    out=yn_hbm[q_ * 128:(q_ + 1) * 128, :], in_=yntok))(q),
                    reads=["yn"], writes=[("ynh", q)], dma=fw.dsem(f"ynS{q % 2}"))

            for q in range(NCH):
                if q == 0:
                    emit_xtok5(0)
                    emit_prep5(0)
                emit_M5(q)
                if q + 1 < NCH:
                    emit_xtok5(q + 1)
                    emit_prep5(q + 1)
                emit_Y5(q)
                emit_rms5(q)
                for _ in range(4):
                    if wdni_cnt[0] < 64:
                        emit_wdnT(wdni_cnt[0])
                        wdni_cnt[0] += 1
            fw.barrier()
            pool.pop()
            pool.pop()

        if phases >= 6:
            pool.push()
            stgb = [pool.f32(2048) for _ in range(2)]
            xtl = pool.f32(2048)
            l3tv = xtl.bitcast(BF16)[:, 0:2048]

            def emit_p7_tile(tt):
                ti = tt % 2
                stg = stgb[ti]
                pb = ti * 4
                gs = f"asS{ti}"
                fw.dma_begin(gs)
                for hh2 in range(2):
                    srcap = attn_hbm.rearrange("(k p) t -> p k t", p=128)[
                        :, hh2 * 8:(hh2 + 1) * 8, tt * 128:(tt + 1) * 128]
                    fw.op("g", (lambda s_, h2, st_: lambda e: e.dma_start(
                        out=st_[:, h2 * 1024:(h2 + 1) * 1024]
                        .rearrange("p (k j) -> p k j", k=8),
                        in_=s_, allow_slow_non_contiguous=True))(srcap, hh2, stg),
                        reads=[("attn", k2, tt // 8) for k2 in range(hh2 * 8, hh2 * 8 + 8)],
                        writes=[("ast", ti, k2) for k2 in range(hh2 * 8, hh2 * 8 + 8)]
                        + ([("ln2o", ti), ("x4t", ti)] if hh2 == 0 else []),
                        dma=gs, group=True)
                fw.dma_end(gs)
                for k in range(16):
                    bnk = pb + k // 4
                    fw.op("t", (lambda k_, b_, st_: lambda e: e.transpose(
                        ps(b_, (k_ % 4) * 128, (k_ % 4) * 128 + 128),
                        st_[:, k_ * 128:(k_ + 1) * 128], identf[:]))(k, bnk, stg),
                        reads=[("ast", ti, k), d_name(identf)], writes=[("PS", bnk)])
                for b_ in range(4):
                    fw.op("v", (lambda b2, st_: lambda e: e.tensor_copy(
                        out=st_[:, b2 * 512:(b2 + 1) * 512],
                        in_=ps(pb + b2, 0, 512)))(b_, stg),
                        reads=[("PS", pb + b_)],
                        writes=[("atok", ti, b_)]
                        + [("ast", ti, 4 * b_ + i2) for i2 in range(4)]
                        + ([("x4t", ti), ("ln2o", ti)] if b_ == 0 else []))
                emit_lnstats([("atok", ti, b2) for b2 in range(4)], stg, 1e-5, 32)
                fw.op("g", (lambda tt_, x_: lambda e: e.dma_start(
                    out=x_, in_=x_in[tt_ * 128:(tt_ + 1) * 128, :]))(tt, xtl),
                    writes=["xtl"], dma=fw.dsem("x7S"))
                fw.op("s", (lambda st_: lambda e: e.activation(
                    out=st_, in_=st_, func=AF.Identity,
                    bias=stat[:, 33:34], scale=stat[:, 32:33]))(stg),
                    reads=[("rs", 32), ("nb", 32)],
                    rw=[("atok", ti, b2) for b2 in range(4)],
                    writes=[("ln2o", ti)] + [("ast", ti, k2) for k2 in range(16)])
                fw.op("v", (lambda st_, x_: lambda e: e.tensor_add(
                    st_, x_, st_))(stg, xtl),
                    reads=["xtl"], rw=[("ln2o", ti)], writes=[("x4t", ti)])
                emit_lnstats([("x4t", ti)], stg, 1e-5, 64)
                fw.op("s", (lambda st_, l_: lambda e: e.activation(
                    out=l_, in_=st_, func=AF.Identity,
                    bias=stat[:, 65:66], scale=stat[:, 64:65]))(stg, l3tv),
                    reads=[("x4t", ti), ("rs", 64), ("nb", 64)],
                    writes=[("l3t", ti)], rw=["xtl"])
                fw.op("q", (lambda tt_, st_: lambda e: e.dma_start(
                    out=x4_hbm[tt_ * 128:(tt_ + 1) * 128, :], in_=st_))(tt, stg),
                    reads=[("x4t", ti)], writes=[("x4", tt)], dma=fw.dsem(f"x4S{ti}"))
                fw.op("q", (lambda tt_, l_: lambda e: e.dma_start(
                    out=l3x_hbm[tt_ * 128:(tt_ + 1) * 128, :], in_=l_))(tt, l3tv),
                    reads=[("l3t", ti), "xtl"], writes=[("l3", tt)],
                    dma=fw.dsem(f"l3S{ti}"))

            def emit_p9_tile(tt):
                ti = tt % 2
                stg = stgb[ti]
                pb = ti * 4
                gs = f"ysS{ti}"
                fw.dma_begin(gs)
                for hh2 in range(2):
                    srcap = y2_hbm.rearrange("(k p) t -> p k t", p=128)[
                        :, hh2 * 8:(hh2 + 1) * 8, tt * 128:(tt + 1) * 128]
                    fw.op("g", (lambda s_, h2, st_: lambda e: e.dma_start(
                        out=st_[:, h2 * 1024:(h2 + 1) * 1024]
                        .rearrange("p (k j) -> p k j", k=8),
                        in_=s_, allow_slow_non_contiguous=True))(srcap, hh2, stg),
                        reads=[("y2", k2, tt // 8) for k2 in range(hh2 * 8, hh2 * 8 + 8)],
                        writes=[("yst", ti, k2) for k2 in range(hh2 * 8, hh2 * 8 + 8)]
                        + ([("ln4o", ti), ("outt", ti)] if hh2 == 0 else []),
                        dma=gs, group=True)
                fw.dma_end(gs)
                for k in range(16):
                    bnk = pb + k // 4
                    fw.op("t", (lambda k_, b_, st_: lambda e: e.transpose(
                        ps(b_, (k_ % 4) * 128, (k_ % 4) * 128 + 128),
                        st_[:, k_ * 128:(k_ + 1) * 128], identf[:]))(k, bnk, stg),
                        reads=[("yst", ti, k), d_name(identf)], writes=[("PS", bnk)])
                for b_ in range(4):
                    fw.op("v", (lambda b2, st_: lambda e: e.tensor_copy(
                        out=st_[:, b2 * 512:(b2 + 1) * 512],
                        in_=ps(pb + b2, 0, 512)))(b_, stg),
                        reads=[("PS", pb + b_)],
                        writes=[("y2t", ti, b_)]
                        + [("yst", ti, 4 * b_ + i2) for i2 in range(4)]
                        + ([("ln4o", ti), ("outt", ti)] if b_ == 0 else []))
                emit_lnstats([("y2t", ti, b2) for b2 in range(4)], stg, 1e-5, 96)
                fw.op("g", (lambda tt_, x_: lambda e: e.dma_start(
                    out=x_, in_=x4_hbm[tt_ * 128:(tt_ + 1) * 128, :]))(tt, xtl),
                    reads=[("x4", tt)], writes=["xtl"], dma=fw.dsem("x49S"))
                fw.op("s", (lambda st_: lambda e: e.activation(
                    out=st_, in_=st_, func=AF.Identity,
                    bias=stat[:, 97:98], scale=stat[:, 96:97]))(stg),
                    reads=[("rs", 96), ("nb", 96)],
                    rw=[("y2t", ti, b2) for b2 in range(4)],
                    writes=[("ln4o", ti)] + [("yst", ti, k2) for k2 in range(16)])
                fw.op("v", (lambda st_, x_: lambda e: e.tensor_add(
                    st_, x_, st_))(stg, xtl),
                    reads=["xtl"], rw=[("ln4o", ti)], writes=[("outt", ti)])
                fw.op("q", (lambda tt_, st_: lambda e: e.dma_start(
                    out=out_hbm[tt_ * 128:(tt_ + 1) * 128, :], in_=st_))(tt, stg),
                    reads=[("outt", ti)], writes=[("out", tt)],
                    dma=fw.dsem(f"outS{ti}"))

            pool.push()
            l1fm = pool.bf16(16 * 2048)
            wbuf6 = [pool.bf16(2048) for _ in range(3)]
            gtile = pool.bf16(2048)
            yfm = [pool.bf16(2048) for _ in range(2)]
            ygout = pool.bf16(2048)

            def l1fm_k(k):
                return l1fm[:, k * 2048:(k + 1) * 2048]

            fw.dma_begin("l1fS")
            for k in range(16):
                fw.op("s", (lambda k_: lambda e: e.dma_start_transpose(
                    l1fm_k(k_), l1x_hbm[:, k_ * 128:(k_ + 1) * 128]))(k),
                    reads=[("l1x", tt2) for tt2 in range(16)] if k == 0 else [],
                    writes=[("l1f", k)], dma="l1fS", group=True)
            fw.dma_end("l1fS")
            for o in range(32):
                wb = wbuf6[o % 3]
                wbk = ("wb6", o % 3)
                fw.op("g", (lambda o_, wb_: lambda e: e.dma_start(
                    out=wb_, in_=wg_img[o_]))(o, wb),
                    writes=[wbk], dma=fw.dsem(f"wg{o % 3}"))
                Pt = o % 2
                for tt in range(4):
                    for k in range(16):
                        first, last = k == 0, k == 15
                        rd = []
                        if first:
                            rd = [wbk] + [("l1f", kk) for kk in range(16)]
                        elif last:
                            rd = [wbk]
                        fw.op("t", (lambda wb_, k_, tt_, f_, l_, p_: lambda e: e.matmul(
                            ps(p_ * 4 + tt_), wb_[:, k_ * 128:(k_ + 1) * 128],
                            l1fm_k(k_)[:, tt_ * 512:(tt_ + 1) * 512],
                            start=f_, stop=l_))(wb, k, tt, first, last, Pt),
                            reads=rd,
                            writes=([("PS", Pt * 4 + tt)] if (first or last) else []))
                fw.op("s", (lambda p_: lambda e: e.activation(
                    out=gtile, in_=(PA if p_ == 0 else PB)[:, 0:T],
                    func=AF.Silu))(Pt),
                    reads=[("PS", Pt * 4 + tt) for tt in range(4)],
                    writes=["gtile"])
                yf = yfm[o % 2]
                fw.op("s", (lambda o_, yf_: lambda e: e.dma_start_transpose(
                    yf_, yn_hbm[:, o_ * 128:(o_ + 1) * 128]))(o, yf),
                    reads=[("ynh", q2) for q2 in range(NCH)],
                    writes=[("yfm", o % 2)], dma=fw.dsem(f"yfm{o % 2}"))
                fw.op("v", (lambda o_, yf_: lambda e: e.tensor_scalar(
                    out=ygout, in0=yf_, scalar1=nw[:, o_:o_ + 1], scalar2=None,
                    op0=ALU.mult))(o, yf),
                    reads=[("yfm", o % 2), d_name(nw)], writes=["ygo"])
                fw.op("v", lambda e: e.tensor_mul(ygout, ygout, gtile),
                      reads=["gtile"], rw=["ygo"])
                fw.op("q", (lambda o_: lambda e: e.dma_start(
                    out=yg_hbm[o_ * 128:(o_ + 1) * 128, :], in_=ygout))(o),
                    reads=["ygo"], writes=[("ygh", o)], dma=fw.dsem(f"ygS{o % 2}"))
            fw.barrier()
            pool.pop()

            pool.push()
            ygh = pool.bf16(32 * 1024)
            wobuf = [pool.bf16(4096) for _ in range(2)]
            attnev = pool.f32(1024)
            for H in range(2):
                fw.dma_begin("ygtS")
                for k in range(32):
                    fw.op("q", (lambda k_, H_: lambda e: e.dma_start(
                        out=ygh[:, k_ * 1024:(k_ + 1) * 1024],
                        in_=yg_hbm[k_ * 128:(k_ + 1) * 128,
                                   H_ * 1024:(H_ + 1) * 1024]))(k, H),
                        reads=[("ygh", oo) for oo in range(32)] if k == 0 else [],
                        writes=[("ygt", k)], dma="ygtS", group=True)
                fw.dma_end("ygtS")
                for o in range(16):
                    wb = wobuf[o % 2]
                    wbk = ("wo", o % 2)
                    fw.op("g", (lambda o_, wb_: lambda e: e.dma_start(
                        out=wb_, in_=wo_img[o_]))(o, wb),
                        writes=[wbk], dma=fw.dsem(f"wo{o % 2}"))
                    slot = o % 4
                    bk = [slot * 2, slot * 2 + 1]
                    for ts_ in range(2):
                        for k in range(32):
                            first, last = k == 0, k == 31
                            rd = []
                            if first or last:
                                rd = [wbk] + [("ygt", kk) for kk in range(32)]
                            fw.op("t", (lambda wb_, k_, t2, f_, l_, b2: lambda e: e.matmul(
                                ps(b2), wb_[:, k_ * 128:(k_ + 1) * 128],
                                ygh[:, k_ * 1024 + t2 * 512:k_ * 1024 + (t2 + 1) * 512],
                                start=f_, stop=l_))(wb, k, ts_, first, last, bk[ts_]),
                                reads=rd,
                                writes=([("PS", bk[ts_])] if (first or last) else []))
                    fw.op("v", (lambda b0: lambda e: e.tensor_copy(
                        out=attnev, in_=ps(b0, 0, 1024)))(bk[0]),
                        reads=[("PS", bk[0]), ("PS", bk[1])], writes=["aev"])
                    fw.op("q", (lambda o_, H_: lambda e: e.dma_start(
                        out=attn_hbm[o_ * 128:(o_ + 1) * 128,
                                     H_ * 1024:(H_ + 1) * 1024], in_=attnev))(o, H),
                        reads=["aev"], writes=[("attn", o, H)],
                        dma=fw.dsem(f"atS{o % 2}"))
                for tt in range(H * 8, H * 8 + 8):
                    emit_p7_tile(tt)
            fw.barrier()
            pool.pop()

        if phases >= 8:
            pool.push()
            x2sb = pool.bf16(64 * 1024)

            def x2_o(o):
                return x2sb[:, o * 1024:(o + 1) * 1024]

            for H in range(2):
                pool.push()
                l3fm = pool.bf16(16 * 1024)
                upw = [pool.bf16(2048) for _ in range(2)]
                bupl = pool.bf16(HF // 8, parts=16)
                t1b = pool.bf16(1024, parts=16)
                fw.op("g", lambda e: e.dma_start(out=bupl, in_=bupl_in[:, 0:HF // 8]),
                      writes=["bupl"], dma=fw.dsem("buplS"))
                fw.dma_begin("l3fS")
                for k in range(16):
                    fw.op("s", (lambda k_, H_: lambda e: e.dma_start_transpose(
                        l3fm[:, k_ * 1024:(k_ + 1) * 1024],
                        l3x_hbm[H_ * 1024:(H_ + 1) * 1024,
                                k_ * 128:(k_ + 1) * 128]))(k, H),
                        reads=[("l3", tt2) for tt2 in range(H * 8, H * 8 + 8)],
                        writes=[("l3f", k)], dma="l3fS", group=True)
                fw.dma_end("l3fS")
                for j in range(2):
                    for k in range(16):
                        first, last = k == 0, k == 15
                        fw.op("t", (lambda k_, j_, f_, l_: lambda e: e.matmul(
                            ps(j_, 0, 512, parts=16),
                            aupb[:, k_ * 16:k_ * 16 + 16],
                            l3fm[:, k_ * 1024 + j_ * 512:k_ * 1024 + (j_ + 1) * 512],
                            start=f_, stop=l_))(k, j, first, last),
                            reads=([d_name(aupb)] + [("l3f", kk) for kk in range(16)]
                                   if first else []),
                            writes=([("PS", j)] if (first or last) else []))
                fw.op("v", lambda e: e.tensor_copy(
                    out=t1b, in_=PA[0:16, 0:1024]),
                    reads=[("PS", 0), ("PS", 1)], writes=["t1b"])
                def uw_load(o):
                    uw_ = upw[o % 2]
                    fw.op("q", (lambda o_, u_: lambda e: e.dma_start(
                        out=u_, in_=wupi_hbm[o_]))(o, uw_),
                        reads=([("wupik", kk) for kk in range(16)] if o == 0 else []),
                        writes=[("uw", o % 2)], dma=fw.dsem(f"uw{o % 2}"))

                for o in range(64):
                    if o % 8 == 0 and o > 0:
                        fw.op("g", (lambda o_: lambda e: e.dma_start(
                            out=bupl, in_=bupl_in[:, o_ * 128:(o_ + 8) * 128]))(o),
                            rw=["bupl"], dma="buplS")
                    if o == 0:
                        uw_load(0)
                    if o + 1 < 64:
                        uw_load(o + 1)
                    uw = upw[o % 2]
                    uwk = ("uw", o % 2)
                    slot = 2 + (o % 2) * 4
                    for j in range(2):
                        b2 = slot + j
                        for k in range(16):
                            first, last = k == 0, k == 15
                            rd = [uwk] + ([("l3f", kk) for kk in range(16)]
                                          if (first or last) else [])
                            if not (first or last):
                                rd = []
                            fw.op("t", (lambda u_, k_, j_, f_, b3: lambda e: e.matmul(
                                ps(b3), u_[:, k_ * 128:(k_ + 1) * 128],
                                l3fm[:, k_ * 1024 + j_ * 512:k_ * 1024 + (j_ + 1) * 512],
                                start=f_, stop=False))(uw, k, j, first, b2),
                                reads=rd,
                                writes=([("PS", b2)] if first else []))
                        fw.op("t", (lambda o_, j_, b3: lambda e: e.matmul(
                            ps(b3), bupl[:, (o_ % 8) * 128:(o_ % 8 + 1) * 128],
                            t1b[:, j_ * 512:(j_ + 1) * 512],
                            start=False, stop=True))(o, j, b2),
                            reads=["t1b", "bupl"], writes=[("PS", b2)])
                    fw.op("s", (lambda o_, s_: lambda e: e.activation(
                        out=x2_o(o_), in_=(PA if s_ == 2 else PB)[:, 1024:2048],
                        func=AF.Relu, bias=bup[:, o_:o_ + 1]))(o, slot),
                        reads=[("PS", slot), ("PS", slot + 1), d_name(bup)],
                        writes=[("x2", o)])
                fw.barrier()
                pool.pop()
                pool.push()
                wdg = [pool.bf16(64 * 128) for _ in range(2)]
                bdnl = pool.bf16(D, parts=16)
                t2b = pool.bf16(1024, parts=16)
                y2ev = pool.f32(1024)
                fw.op("g", lambda e: e.dma_start(out=bdnl, in_=bdnl_in),
                      writes=["bdnl"], dma=fw.dsem("bdnlS"))
                for j in range(2):
                    for o in range(64):
                        first, last = o == 0, o == 63
                        fw.op("t", (lambda o_, j_, f_, l_: lambda e: e.matmul(
                            ps(j_, 0, 512, parts=16),
                            adnb[:, o_ * 16:(o_ + 1) * 16],
                            x2_o(o_)[:, j_ * 512:(j_ + 1) * 512],
                            start=f_, stop=l_))(o, j, first, last),
                            reads=([d_name(adnb)] + [("x2", oo) for oo in range(64)]
                                   if first else ([("x2", 63)] if last else [])),
                            writes=([("PS", j)] if (first or last) else []))
                fw.op("v", lambda e: e.tensor_copy(
                    out=t2b, in_=PA[0:16, 0:1024]),
                    reads=[("PS", 0), ("PS", 1)], writes=["t2b"])
                def wdg_load(d2):
                    w2_ = wdg[d2 % 2]
                    fw.op("s", (lambda d3, w3: lambda e: e.dma_start(
                        out=w3, in_=wdni_hbm[d3].rearrange("p o j -> p (o j)")))(
                        d2, w2_),
                        reads=([("wdni", ii) for ii in range(64)]
                               if (H == 0 and d2 == 0) else []),
                        writes=[("wdg", d2 % 2)], dma=fw.dsem(f"wdgS{d2 % 2}"))

                for d_ in range(16):
                    if d_ == 0:
                        wdg_load(0)
                    if d_ + 1 < 16:
                        wdg_load(d_ + 1)
                    wg_ = wdg[d_ % 2]
                    slot = 2 + (d_ % 2) * 4
                    for j in range(2):
                        b2 = slot + j
                        for o in range(64):
                            first = o == 0
                            fw.op("t", (lambda o_, j_, f_, b3, w2: lambda e: e.matmul(
                                ps(b3), w2[:, o_ * 128:(o_ + 1) * 128],
                                x2_o(o_)[:, j_ * 512:(j_ + 1) * 512],
                                start=f_, stop=False))(o, j, first, b2, wg_),
                                reads=([("wdg", d_ % 2)]
                                       if (first or o == 63) else []),
                                writes=([("PS", b2)] if first else []))
                        fw.op("t", (lambda d2, j_, b3: lambda e: e.matmul(
                            ps(b3), bdnl[:, d2 * 128:(d2 + 1) * 128],
                            t2b[:, j_ * 512:(j_ + 1) * 512],
                            start=False, stop=True))(d_, j, b2),
                            reads=["t2b", "bdnl"], writes=[("PS", b2)])
                    fw.op("s", (lambda d2, s_: lambda e: e.activation(
                        out=y2ev, in_=(PA if s_ == 2 else PB)[:, 1024:2048],
                        func=AF.Identity, bias=bdn[:, d2:d2 + 1]))(d_, slot),
                        reads=[("PS", slot), ("PS", slot + 1), d_name(bdn)],
                        writes=["y2e"])
                    fw.op("q", (lambda d2, H_: lambda e: e.dma_start(
                        out=y2_hbm[d2 * 128:(d2 + 1) * 128,
                                   H_ * 1024:(H_ + 1) * 1024], in_=y2ev))(d_, H),
                        reads=["y2e"], writes=[("y2", d_, H)],
                        dma=fw.dsem(f"y2S{d_ % 2}"))
                fw.barrier()
                pool.pop()
                for tt in range(H * 8, H * 8 + 8):
                    emit_p9_tile(tt)
            pool.pop()
            pool.pop()

        fw.finalize()
        fw.emit(block, ctx)
    return nc


def d_name(t):
    return t.tensor.name if hasattr(t, "tensor") else t.name



def _host_prep(inputs):
    f = np.float32
    W = np.asarray(inputs["L0_attn_in_proj_w"], f)
    wg = np.ascontiguousarray(
        W[:4096].reshape(32, 128, 16, 128).transpose(0, 3, 2, 1)).reshape(32, 128, 2048)
    wh = np.ascontiguousarray(
        W[4096:8320].reshape(33, 128, 16, 128).transpose(0, 3, 2, 1)).reshape(33, 128, 2048)
    wd = np.ascontiguousarray(
        W[8320:8384].reshape(64, 16, 128).transpose(2, 1, 0)).reshape(128, 1024)
    Wo = np.asarray(inputs["L0_attn_out_proj_w"], f)
    wo = np.ascontiguousarray(
        Wo.reshape(16, 128, 32, 128).transpose(0, 3, 2, 1)).reshape(16, 128, 4096)
    d = {
        "wg_img": wg, "wh_img": wh, "wd_img": wd, "wo_img": wo,
        "convw_r": np.ascontiguousarray(
            np.asarray(inputs["L0_attn_conv_w"], f).reshape(33, 128, 4)
            .transpose(1, 0, 2)).reshape(128, 132),
        "convb_r": np.ascontiguousarray(
            np.asarray(inputs["L0_attn_conv_b"], f).reshape(33, 128).T),
        "dtb": np.asarray(inputs["L0_attn_dt_bias"], f).reshape(64, 1),
        "alog": np.asarray(inputs["L0_attn_A_log"], f).reshape(64, 1),
        "dcol": np.asarray(inputs["L0_attn_D"], f).reshape(64, 1),
        "nw_r": np.ascontiguousarray(
            np.asarray(inputs["L0_attn_norm_weight"], f).reshape(32, 128).T),
        "q_up": np.asarray(inputs["L0_ffn_w_up_q"], np.int32),
        "scup_r": np.ascontiguousarray(
            np.asarray(inputs["L0_ffn_w_up_scale"], f).reshape(64, 128, 32)
            .transpose(1, 0, 2)).reshape(128, 64 * 32),
        "bup_r": np.ascontiguousarray(
            np.asarray(inputs["L0_ffn_b_up"], f).reshape(64, 128).T),
        "q_dn": np.asarray(inputs["L0_ffn_w_down_q"], np.int32),
        "scdn_r": np.ascontiguousarray(
            np.asarray(inputs["L0_ffn_w_down_scale"], f).reshape(16, 128, 128)
            .transpose(1, 0, 2)).reshape(128, 16 * 128),
        "bdn_r": np.ascontiguousarray(
            np.asarray(inputs["L0_ffn_b_down"], f).reshape(16, 128).T),
        "aup_r": np.ascontiguousarray(
            np.asarray(inputs["L0_ffn_w_up_lora_a"], f).reshape(16, 128, 16)
            .transpose(1, 0, 2)).reshape(128, 256),
        "b_upl": np.asarray(inputs["L0_ffn_w_up_lora_b"], f),
        "adn_r": np.ascontiguousarray(
            np.asarray(inputs["L0_ffn_w_down_lora_a"], f).reshape(64, 128, 16)
            .transpose(1, 0, 2)).reshape(128, 1024),
        "b_dnl": np.asarray(inputs["L0_ffn_w_down_lora_b"], f),
    }
    return {k: np.ascontiguousarray(v) for k, v in d.items()}


def make_in_maps(inputs, shared=None):
    if shared is None:
        shared = _host_prep(inputs)
    x = np.ascontiguousarray(
        np.asarray(inputs["L0_attn_hidden_states"], np.float32).reshape(16384, 2048))
    in_maps = []
    for c in range(NCORES):
        m = dict(shared)
        m["x"] = np.ascontiguousarray(x[c * T:(c + 1) * T])
        m["is_odd"] = np.full((128, 1), 1.0 if c % 2 else 0.0, np.float32)
        in_maps.append(m)
    return in_maps


def kernel(**inputs):
    nc = build_program()
    in_maps = make_in_maps(inputs)
    res = run_bass_kernel_spmd(nc, in_maps, core_ids=list(range(NCORES)))
    out = np.concatenate([np.asarray(r["out"]) for r in res.results], axis=0)
    return out.reshape(4, 4096, 2048)
